# version 03
# baseline (speedup 1.0000x reference)
import math
import numpy as np
import ml_dtypes
from contextlib import ExitStack

import os
os.environ.setdefault("TRN_ALLOW_ICI_ALL_TO_ALL", "1")
import concourse.bass as bass
import concourse.mybir as mybir
from concourse.bass_utils import run_bass_kernel_spmd

F32 = mybir.dt.float32
BF16 = mybir.dt.bfloat16
F16 = mybir.dt.float16
I32 = mybir.dt.int32
AF = mybir.ActivationFunctionType
OP = mybir.AluOpType

NCORES = 8
B, S, D = 4, 2048, 2048
TOK = B * S
TLOC = TOK // NCORES
LN2 = math.log(2.0)

NK, NV, DK, DV = 16, 32, 128, 128
KEY_DIM, VALUE_DIM = NK * DK, NV * DV
CONV_DIM = 2 * KEY_DIM + VALUE_DIM
FF_I = 5632
NH, HD = 16, 128
MOE_E, MOE_I = 16, 1408
SH_I = 2816


class Em:
    ENGS = ("sync", "scalar", "vector", "tensor", "gpsimd")

    def __init__(self, nc):
        self.nc = nc
        self.ctx = ExitStack()
        self.script = {e: [] for e in self.ENGS}
        self.sems = {}
        self.cnt = {}
        self.sb = {}
        self.dram = {}

    def sem(self, name):
        if name not in self.sems:
            self.sems[name] = self.ctx.enter_context(self.nc.semaphore(name))
            self.cnt[name] = 0
        return self.sems[name]

    def sbuf(self, name, shape, dtype, stack=None, **kw):
        t = (stack or self.ctx).enter_context(self.nc.sbuf_tensor(name, shape, dtype, **kw))
        self.sb[name] = t
        return t

    def psum(self, name, shape, dtype, stack=None):
        t = (stack or self.ctx).enter_context(self.nc.psum_tensor(name, shape, dtype))
        return t

    def dram_t(self, name, shape, dtype, **kw):
        t = self.nc.dram_tensor(name, shape, dtype, **kw)
        self.dram[name] = t
        return t

    def cv(self, val, rows=128, base=0):
        val = float(val)
        if not hasattr(self, "const_vals"):
            self.const_vals = []
        if val not in self.const_vals:
            self.const_vals.append(val)
        k = self.const_vals.index(val)
        return self.sb["constv"][base:base + rows, k:k + 1]

    def wait(self, eng, sem_name, val=None):
        h = self.sem(sem_name)
        v = self.cnt[sem_name] if val is None else val
        self.script[eng].append(lambda e, h=h, v=v: e.wait_ge(h, v))
        return v

    AUTO = {"vector": "E_dve", "scalar": "E_act", "gpsimd": "E_pool"}

    def op(self, eng, fn, inc=None, inc_by=1):
        if eng in self.AUTO:
            own = self.AUTO[eng]
            h = self.sem(own)
            prev = self.cnt[own]
            self.cnt[own] = prev + 1
            if inc is not None and inc != own:
                h2 = self.sem(inc)
                self.cnt[inc] += inc_by

                def step(e, fn=fn, h=h, prev=prev, h2=h2, k=inc_by):
                    e.wait_ge(h, prev)
                    fn(e).then_inc(h, 1).then_inc(h2, k)
            else:
                def step(e, fn=fn, h=h, prev=prev):
                    e.wait_ge(h, prev)
                    fn(e).then_inc(h, 1)
            self.script[eng].append(step)
            return self.cnt[own]
        if inc is not None:
            h = self.sem(inc)
            self.cnt[inc] += inc_by
            self.script[eng].append(
                lambda e, fn=fn, h=h, k=inc_by: fn(e).then_inc(h, k))
            return self.cnt[inc]
        self.script[eng].append(fn)
        return None

    def dma(self, eng, out, in_, inc, **kw):
        if eng in self.AUTO:
            own = self.AUTO[eng]
            self.wait(eng, own)
        h = self.sem(inc)
        self.cnt[inc] += 16
        self.script[eng].append(
            lambda e, o=out, i=in_, kw=kw, h=h: e.dma_start(out=o, in_=i, **kw).then_inc(h, 16))
        return self.cnt[inc]

    def collective(self, kind, op, groups, ins, outs, inc):
        self.wait("gpsimd", self.AUTO["gpsimd"])
        h = self.sem(inc)
        self.cnt[inc] += 1
        self.script["gpsimd"].append(
            lambda e, h=h: e.collective_compute(
                kind, op, replica_groups=groups, ins=ins, outs=outs).then_inc(h, 1))
        return self.cnt[inc]

    def emit(self):
        block = self.ctx.enter_context(self.nc.Block())
        for eng in self.ENGS:
            steps = self.script[eng]
            if not steps:
                continue

            def body(e, steps=steps):
                for f in steps:
                    f(e)
            getattr(block, eng)(body)



def _tiled(dram_ap, p=128):
    return dram_ap.rearrange("(t p) w -> p t w", p=p)


def emit_fp4_tile(em, eng, pk_ap, dq, W, esem, act_sem):
    n = 8 * W
    nb, nf, ge8, p, m_i, m, pm = (dq[k] for k in ("nb", "nf", "ge8", "p", "m_i", "m", "pm"))
    pw, base, z, msk, vp, sgn, val = (dq[k] for k in ("pw", "base", "z", "msk", "vp",
                                                      "sgn", "val"))
    for j in range(8):
        em.op(eng, lambda e, j=j: e.tensor_scalar(
            nb[:, j:n:8], pk_ap, 4 * j, 15, OP.logical_shift_right, OP.bitwise_and))
    em.op(eng, lambda e: e.tensor_copy(nf[:, :n], nb[:, :n]))
    em.op(eng, lambda e: e.tensor_scalar(m_i[:, :n], nb[:, :n], 1, None, OP.bitwise_and))
    em.op(eng, lambda e: e.tensor_copy(m[:, :n], m_i[:, :n]))
    em.op(eng, lambda e: e.tensor_scalar(ge8[:, :n], nf[:, :n], 8.0, None, OP.is_ge))
    em.op(eng, lambda e: e.scalar_tensor_tensor(
        p[:, :n], ge8[:, :n], -8.0, nf[:, :n], OP.mult, OP.add))
    v = em.op(eng, lambda e: e.tensor_tensor(pm[:, :n], p[:, :n], m[:, :n],
                                             OP.subtract), inc=esem)
    em.wait("scalar", esem, v)
    va = em.op("scalar", lambda e: e.activation(
        pw[:, :n], pm[:, :n], AF.Exp, scale=LN2 / 2, bias=em.cv(-2 * LN2)), inc=act_sem)
    em.wait(eng, act_sem, va)
    em.op(eng, lambda e: e.scalar_tensor_tensor(
        base[:, :n], m[:, :n], 2.0, pw[:, :n], OP.add, OP.mult))
    em.op(eng, lambda e: e.tensor_scalar(z[:, :n], m[:, :n], 0.5, None, OP.mult))
    em.op(eng, lambda e: e.tensor_scalar(msk[:, :n], p[:, :n], 2.0, None, OP.is_lt))
    em.op(eng, lambda e: e.tensor_tensor(z[:, :n], z[:, :n], base[:, :n], OP.subtract))
    em.op(eng, lambda e: e.tensor_tensor(z[:, :n], msk[:, :n], z[:, :n], OP.mult))
    em.op(eng, lambda e: e.tensor_tensor(vp[:, :n], base[:, :n], z[:, :n], OP.add))
    em.op(eng, lambda e: e.tensor_scalar(sgn[:, :n], ge8[:, :n], -2.0, 1.0,
                                         OP.mult, OP.add))
    em.op(eng, lambda e: e.tensor_tensor(val[:, :n], vp[:, :n], sgn[:, :n], OP.mult))


def _mk_dq(em, pref, width, stack=None):
    dq = {"nb": em.sbuf(f"dq_{pref}_nb", [128, width], I32, stack),
          "m_i": em.sbuf(f"dq_{pref}_mi", [128, width], I32, stack)}
    for nm in ("nf", "ge8", "p", "m", "pm", "pw", "base", "z", "msk", "vp", "sgn", "val"):
        dq[nm] = em.sbuf(f"dq_{pref}_{nm}", [128, width], F32, stack)
    dq["bf"] = em.sbuf(f"dq_{pref}_bf", [128, width], BF16, stack)
    return dq


def _dn_ranges(core_id):
    base = 704 * core_id
    g0 = base // 128
    ranges, j = [], 0
    while j < 704:
        g = (base + j) // 128
        hi = min(704, (g + 1) * 128 - base)
        ranges.append((j, hi, g - g0))
        j = hi
    return ranges


def emit_ln(em, xacc, junk_bf, stat, ACTS="E_act", DVE="E_dve"):
    em.op("scalar", lambda e: e.activation(
        junk_bf[:, :], xacc[:, :], AF.Copy, accum_out=stat[:, 3:4]))
    va = em.op("scalar", lambda e: e.activation(
        junk_bf[:, :], xacc[:, :], AF.Square, accum_out=stat[:, 4:5]), inc=ACTS)
    em.wait("vector", ACTS, va)
    em.op("vector", lambda e: e.tensor_scalar(
        stat[:, 5:6], stat[:, 3:4], 1.0 / 2048, None, OP.mult))
    em.op("vector", lambda e: e.tensor_tensor(
        stat[:, 6:7], stat[:, 5:6], stat[:, 5:6], OP.mult))
    vv = em.op("vector", lambda e: e.scalar_tensor_tensor(
        stat[:, 7:8], stat[:, 4:5], 1.0 / 2048, stat[:, 6:7],
        OP.mult, OP.subtract), inc=DVE)
    em.wait("scalar", DVE, vv)
    em.op("scalar", lambda e: e.activation(
        stat[:, 0:1], stat[:, 7:8], AF.Ln, bias=em.cv(1e-5)))
    va = em.op("scalar", lambda e: e.activation(
        stat[:, 1:2], stat[:, 0:1], AF.Exp, scale=-0.5), inc=ACTS)
    em.wait("vector", ACTS, va)
    return em.op("vector", lambda e: e.tensor_scalar(
        xacc[:, :], xacc[:, :], stat[:, 5:6], stat[:, 1:2],
        OP.subtract, OP.mult), inc=DVE)


def emit_stageB(em, core_id, x_in, x_out):
    nc = em.nc
    wguT_loc = em.dram_t("wguT_loc", [256, 2 * FF_I], BF16)
    wguT = em.dram_t("wguT", [2048, 2 * FF_I], BF16)
    wdnT_loc = em.dram_t("wdnT_loc", [704, 2048], BF16)
    wdnT = em.dram_t("wdnT", [FF_I, 2048], BF16)
    xbf_d = em.dram_t("bx_bf", [TLOC, 2048], BF16)

    gup = em.dram["B_gup"].ap()
    gus = em.dram["B_gus"].ap()
    dnp = em.dram["B_dnp"].ap()
    dns = em.dram["B_dns"].ap()
    ident = em.sb["ident_bf"]

    LD, SW, SWX, TRD, CC = "b_ld", "b_sw", "b_swx", "b_trd", "b_cc"
    PE, DVE, ACTS, POOL = "b_pe", "E_dve", "E_act", "E_pool"
    WST0, WST1, WDN, XRS, HST = "b_wst0", "b_wst1", "b_wdn", "b_xrs", "b_hst"

    dstack = ExitStack()
    pk_gu = em.sbuf("pk_gu", [128, 88 * 32], I32, dstack)
    pk_dn = em.sbuf("pk_dn", [128, 16 * 88], I32, dstack)
    scl_gu16 = em.sbuf("scl_gu16", [128, 88 * 2], F16, dstack)
    scl_gu = em.sbuf("scl_gu", [128, 88 * 2], F32, dstack)
    scl_dn = em.sbuf("scl_dn", [128, 16 * 11], F32, dstack)
    dqv = _mk_dq(em, "v", 704, dstack)
    dqg = _mk_dq(em, "g", 704, dstack)
    wguT_sb = em.sbuf("wguT_sb", [128, 2 * 2 * FF_I], BF16, dstack)
    wdnT_sb = em.sbuf("wdnT_sb", [128, 6 * 2048], BF16, dstack)
    tr_ps = [em.psum("tr_ps0", [128, 1024], BF16, dstack),
             em.psum("tr_ps1", [128, 1024], BF16, dstack)]

    em.dma("sync", pk_gu[:, :].rearrange("p (t w) -> p t w", w=32), _tiled(gup), LD)
    em.dma("sync", pk_dn[:, :].rearrange("p (t w) -> p t w", w=88), _tiled(dnp), LD)
    em.dma("sync", scl_gu16[:, :].rearrange("p (t w) -> p t w", w=2), _tiled(gus), LD)
    em.dma("sync", scl_dn[:, :].rearrange("p (t w) -> p t w", w=11), _tiled(dns), LD)
    ld_done = em.cnt[LD]
    em.dma("gpsimd", xbf_d[:, :], x_in[:, :], SWX)
    sw_x = em.cnt[SWX]

    em.wait("vector", LD, ld_done)
    em.op("vector", lambda e: e.tensor_copy(scl_gu[:, :], scl_gu16[:, :]), inc=DVE)

    last_tr = {"v": 0, "g": 0}
    last_ev = {0: 0, 1: 0}
    n_t = 0

    def dq_tile(tile_kind, t, n_t):
        eng = "vector" if True else "gpsimd"
        pref = "v" if eng == "vector" else "g"
        dq = dqv if pref == "v" else dqg
        esem = DVE if eng == "vector" else POOL
        par = n_t % 2
        if n_t < 2:
            em.wait(eng, LD, ld_done)
            if eng == "gpsimd":
                em.wait(eng, DVE, 1)
        else:
            em.wait(eng, PE, last_tr[pref])
        if tile_kind == "gu":
            pk_ap = pk_gu[:, 32 * t:32 * (t + 1)]
            W, nrow = 32, 2
        else:
            pk_ap = pk_dn[:, 88 * t:88 * (t + 1)]
            W, nrow = 88, 6
        emit_fp4_tile(em, eng, pk_ap, dq, W, esem, ACTS)
        if tile_kind == "gu":
            em.op(eng, lambda e: e.tensor_scalar(
                dq["bf"][:, 0:128], dq["val"][:, 0:128],
                scl_gu[:, 2 * t:2 * t + 1], None, OP.mult))
            vd = em.op(eng, lambda e: e.tensor_scalar(
                dq["bf"][:, 128:256], dq["val"][:, 128:256],
                scl_gu[:, 2 * t + 1:2 * t + 2], None, OP.mult), inc=esem)
        else:
            for b in range(11):
                vd = em.op(eng, lambda e, b=b: e.tensor_scalar(
                    dq["bf"][:, 64 * b:64 * (b + 1)], dq["val"][:, 64 * b:64 * (b + 1)],
                    scl_dn[:, 11 * t + b:11 * t + b + 1], None, OP.mult),
                    inc=esem if b == 10 else None)
        em.wait("tensor", esem, vd)
        em.wait("tensor", ACTS, last_ev[par])
        for r in range(nrow):
            w = min(128, 8 * W - r * 128)
            pe_c = em.op("tensor", lambda e, r=r, w=w: e.transpose(
                tr_ps[par][0:w, r * 128:r * 128 + 128],
                dq["bf"][:, r * 128:r * 128 + w], ident[:, :]), inc=PE)
        last_tr[pref] = pe_c
        em.wait("scalar", PE, pe_c)
        for r in range(nrow):
            w = min(128, 8 * W - r * 128)
            if tile_kind == "gu":
                dst = wguT_sb[:, r * 2 * FF_I + 128 * t: r * 2 * FF_I + 128 * (t + 1)]
            else:
                dst = wdnT_sb[0:w, r * 2048 + 128 * t: r * 2048 + 128 * (t + 1)]
            ac = em.op("scalar", lambda e, dst=dst, r=r, w=w: e.copy(
                dst, tr_ps[par][0:w, r * 128:r * 128 + 128]), inc=ACTS)
        last_ev[par] = ac
        return n_t + 1

    for t in range(88):
        n_t = dq_tile("gu", t, n_t)
    for t in range(16):
        n_t = dq_tile("dn", t, n_t)

    ev_all = em.cnt[ACTS]
    em.wait("gpsimd", ACTS, ev_all)
    em.dma("gpsimd", _tiled(wguT_loc.ap(), p=128),
           wguT_sb[:, :].rearrange("p (r o) -> p r o", o=2 * FF_I), SW)
    em.dma("gpsimd", _tiled(wdnT_loc.ap()[0:640, :], p=128),
           wdnT_sb[:, 0:5 * 2048].rearrange("p (r o) -> p r o", o=2048), SW)
    em.dma("gpsimd", wdnT_loc.ap()[640:704, :], wdnT_sb[0:64, 5 * 2048:6 * 2048], SW)
    em.wait("gpsimd", SW)
    em.collective("AllGather", OP.bypass, [list(range(8))],
                  [wguT_loc.ap().opt()], [wguT.ap().opt()], CC)
    em.collective("AllGather", OP.bypass, [list(range(8))],
                  [wdnT_loc.ap().opt()], [wdnT.ap().opt()], CC)
    cc_done = em.cnt[CC]
    dstack.close()

    fstack = ExitStack()
    xT_sb = em.sbuf("bxT_sb", [128, 16 * 512], BF16, fstack, is_transpose_dest=True)
    hT_sb = em.sbuf("bhT_sb", [128, 44 * 512], BF16, fstack)
    w1buf = em.sbuf("bw1buf", [128, 2 * 16 * 512], BF16, fstack)
    wdn_ob = em.sbuf("bwdn_ob", [128, 44 * 256], BF16, fstack)
    silu_sb = em.sbuf("bsilu", [128, 4 * 512], BF16, fstack)
    y_sb = em.sbuf("by_sb", [128, 4 * 2048], F32, fstack)
    junk_bf = em.sbuf("bjunk", [128, 2048], BF16, fstack)
    stat = em.sbuf("bstat", [128, 8], F32, fstack)
    xres = em.sbuf("bxres", [128, 2048], F32, fstack)
    xo_sb = em.sbuf("bxo", [128, 2048], F32, fstack)
    ps_g = em.psum("bps_g", [128, 512], F32, fstack)
    ps_u = em.psum("bps_u", [128, 512], F32, fstack)
    ps_y = [em.psum("bps_y0", [128, 512], F32, fstack),
            em.psum("bps_y1", [128, 512], F32, fstack)]

    em.wait("sync", SWX, sw_x)
    em.wait("sync", CC, cc_done)
    em.wait("tensor", CC, cc_done)

    last_hmul = 0
    last_silu = 0
    yev_done = {0: 0, 1: 0}
    last_stt = 0
    last_hst = 0
    last_vln = 0
    xT_used = 0
    for tb in range(2):
        em.wait("sync", PE, xT_used)
        for i in range(16):
            em.dma("sync", xT_sb[:, i * 512:(i + 1) * 512],
                   xbf_d[tb * 512:(tb + 1) * 512, i * 128:(i + 1) * 128], TRD,
                   transpose=True)
        trd_done = em.cnt[TRD]
        em.wait("tensor", TRD, trd_done)
        w1_ld = {0: 0, 1: 0}
        pe_w1 = {0: 0, 1: 0}
        par = 0
        for og in range(11):
            for half in (0, 1):
                buf = par % 2
                par += 1
                ocol = half * FF_I + og * 512
                em.wait("sync", PE, pe_w1[buf])
                wsem = WST0 if buf == 0 else WST1
                w1_ld[buf] = em.dma(
                    "sync",
                    w1buf[:, buf * 8192:(buf + 1) * 8192].rearrange(
                        "p (t o) -> p t o", o=512),
                    _tiled(wguT.ap()[:, ocol:ocol + 512]), wsem)
                ps = ps_g if half == 0 else ps_u
                em.wait("tensor", wsem, w1_ld[buf])
                for j in range(4):
                    it = og * 4 + j
                    if half == 0:
                        em.wait("tensor", ACTS, last_silu)
                    else:
                        em.wait("tensor", DVE, last_hmul)
                    for d in range(16):
                        pe_c = em.op("tensor", lambda e, d=d, j=j, buf=buf, ps=ps: e.matmul(
                            ps[:, :],
                            w1buf[:, buf * 8192 + d * 512 + j * 128:
                                  buf * 8192 + d * 512 + (j + 1) * 128],
                            xT_sb[:, d * 512:(d + 1) * 512],
                            start=(d == 0), stop=(d == 15)),
                            inc=PE if d == 15 else None)
                    if half == 0:
                        em.wait("scalar", PE, pe_c)
                        em.wait("scalar", DVE, last_hmul)
                        last_silu = em.op("scalar", lambda e, j=j: e.activation(
                            silu_sb[:, j * 512:(j + 1) * 512], ps_g[:, :], AF.Silu),
                            inc=ACTS)
                    else:
                        em.wait("vector", ACTS, last_silu)
                        em.wait("vector", PE, pe_c)
                        last_hmul = em.op("vector", lambda e, it=it, j=j: e.tensor_tensor(
                            hT_sb[:, it * 512:(it + 1) * 512],
                            silu_sb[:, j * 512:(j + 1) * 512],
                            ps_u[:, :], OP.mult), inc=DVE)
                pe_w1[buf] = pe_c
        xT_used = em.cnt[PE]
        hT_ready = em.cnt[DVE]
        wdn_ld = 0
        pe_wdn = 0
        for ob in range(8):
            em.wait("sync", PE, pe_wdn)
            wdn_ld = em.dma(
                "sync", wdn_ob[:, :].rearrange("p (t o) -> p t o", o=256),
                _tiled(wdnT.ap()[:, ob * 256:(ob + 1) * 256]), WDN)
            em.wait("tensor", WDN, wdn_ld)
            for tsub in range(4):
                pidx = (ob * 4 + tsub) % 2
                psy = ps_y[pidx]
                em.wait("tensor", DVE, max(hT_ready, yev_done[pidx]))
                for it in range(44):
                    pe_c = em.op("tensor", lambda e, it=it, tsub=tsub, psy=psy: e.matmul(
                        psy[:, 0:256],
                        hT_sb[:, it * 512 + tsub * 128: it * 512 + (tsub + 1) * 128],
                        wdn_ob[:, it * 256:(it + 1) * 256],
                        start=(it == 0), stop=(it == 43)),
                        inc=PE if it == 43 else None)
                em.wait("vector", PE, pe_c)
                yev_done[pidx] = em.op(
                    "vector", lambda e, tsub=tsub, ob=ob, psy=psy: e.tensor_copy(
                        y_sb[:, tsub * 2048 + ob * 256: tsub * 2048 + (ob + 1) * 256],
                        psy[:, 0:256]), inc=DVE)
            pe_wdn = em.cnt[PE]
        y_ready = em.cnt[DVE]
        for tsub in range(4):
            tt = tb * 4 + tsub
            em.wait("scalar", DVE, y_ready)
            em.wait("scalar", DVE, last_vln)
            em.op("scalar", lambda e, tsub=tsub: e.activation(
                junk_bf[:, :], y_sb[:, tsub * 2048:(tsub + 1) * 2048], AF.Square,
                accum_out=stat[:, 0:1]))
            em.op("scalar", lambda e: e.activation(
                stat[:, 1:2], stat[:, 0:1], AF.Ln, scale=1.0 / 2048, bias=em.cv(1e-6)))
            va = em.op("scalar", lambda e: e.activation(
                stat[:, 2:3], stat[:, 1:2], AF.Exp, scale=-0.5), inc=ACTS)
            em.wait("sync", DVE, last_stt)
            xl = em.dma("sync", xres[:, :], x_in[tt * 128:(tt + 1) * 128, :], XRS)
            em.wait("vector", ACTS, va)
            em.wait("vector", XRS, xl)
            em.wait("vector", HST, last_hst)
            last_stt = em.op("vector", lambda e, tsub=tsub: e.scalar_tensor_tensor(
                xo_sb[:, :], y_sb[:, tsub * 2048:(tsub + 1) * 2048], stat[:, 2:3],
                xres[:, :], OP.mult, OP.add), inc=DVE)
            em.wait("scalar", DVE, last_stt)
            last_vln = emit_ln(em, xo_sb, junk_bf, stat, ACTS, DVE)
            em.wait("gpsimd", DVE, last_vln)
            last_hst = em.dma("gpsimd", x_out[tt * 128:(tt + 1) * 128, :], xo_sb[:, :],
                              HST)
    em.wait("gpsimd", HST)
    fstack.close()





def emit_ternary_tile(em, w_sb, wt_bf, scr, ngroups, gwidth=128):
    absj, st = scr["absj"], scr["st"]
    for g in range(ngroups):
        sl = slice(g * gwidth, (g + 1) * gwidth)
        em.wait("scalar", "E_dve")
        va = em.op("scalar", lambda e, sl=sl: e.activation(
            absj[:, 0:gwidth], w_sb[:, sl], AF.Abs, accum_out=st[:, 0:1]))
        em.wait("vector", "E_act", va)
        em.op("vector", lambda e: e.tensor_scalar(
            st[:, 1:2], st[:, 0:1], 1.0 / gwidth, 1e-8, OP.mult, OP.max))
        em.op("vector", lambda e: e.tensor_scalar(
            st[:, 2:3], st[:, 1:2], 0.5, None, OP.mult))
        em.op("vector", lambda e: e.tensor_scalar(
            st[:, 3:4], st[:, 2:3], -1.0, None, OP.mult))
        em.op("vector", lambda e, sl=sl: e.tensor_scalar(
            absj[:, 0:gwidth], w_sb[:, sl], st[:, 2:3], None, OP.is_gt))
        em.op("vector", lambda e, sl=sl: e.tensor_scalar(
            wt_bf[:, sl], w_sb[:, sl], st[:, 3:4], None, OP.is_lt))
        em.op("vector", lambda e, sl=sl: e.tensor_tensor(
            absj[:, 0:gwidth], absj[:, 0:gwidth], wt_bf[:, sl], OP.subtract))
        em.op("vector", lambda e, sl=sl: e.tensor_scalar(
            wt_bf[:, sl], absj[:, 0:gwidth], st[:, 1:2], None, OP.mult))


def emit_stageC(em, x_in, x_out):
    xbf_d = em.dram_t("cx_bf", [TLOC, 2048], BF16)
    x2all = em.dram_t("cx_all", [TOK, 2048], BF16)
    yT_a2a_in = em.dram_t("cyT_ain", [2048, TLOC], BF16)
    yT_tok = em.dram_t("cyT_tok", [2048, TLOC], BF16)
    woT_loc = em.dram_t("cwoT_loc", [256, 2048], BF16)
    woT = em.dram_t("cwoT", [2048, 2048], BF16)

    wqkv_in = [em.dram[k].ap() for k in ("C_wq", "C_wk", "C_wv")]
    wo_in = em.dram["C_wo"].ap()
    tabs_in = [em.dram[k].ap() for k in ("C_cosq", "C_sinq", "C_cosk", "C_sink")]
    permT_in = em.dram["C_permT"].ap()
    dmask_in = em.dram["C_dmask"].ap()
    ident = em.sb["ident_bf"]

    LD, SWX, CC = "c_ld", "c_swx", "c_cc"
    PE, DVE, ACTS = "c_pe", "E_dve", "E_act"
    TRD, YST, WOS, YLD, HST, XRS = "c_trd", "c_yst", "c_wos", "c_yld", "c_hst", "c_xrs"

    for _eng in ("tensor", "vector", "scalar"):
        em.wait(_eng, "c_ld")
    em.dma("gpsimd", xbf_d[:, :], x_in[:, :], SWX)
    em.wait("gpsimd", SWX)
    em.collective("AllGather", OP.bypass, [list(range(8))],
                  [xbf_d.ap().opt()], [x2all.ap().opt()], CC)
    cc_x = em.cnt[CC]

    wqT = em.sbuf("cwqT", [128, 16 * 256], BF16)
    wkT = em.sbuf("cwkT", [128, 16 * 256], BF16)
    wvT = em.sbuf("cwvT", [128, 16 * 256], BF16)
    dstack = ExitStack()
    w_f32 = em.sbuf("cw_f32", [128, 2 * 2048], F32, dstack)
    wo_f32 = em.sbuf("cwo_f32", [128, 16 * 256], F32, dstack)
    wt_bf = em.sbuf("cwt_bf", [128, 2048], BF16, dstack)
    absj = em.sbuf("cabsj", [128, 128], F32, dstack)
    stt = em.sbuf("cstt", [128, 8], F32, dstack)
    scr = {"absj": absj, "st": stt}
    tr_ps = [em.psum("ctr_ps0", [128, 1024], BF16, dstack),
             em.psum("ctr_ps1", [128, 1024], BF16, dstack)]
    woT_sb = em.sbuf("cwoT_sb", [128, 2 * 2048], BF16, dstack)

    last_ev = {0: 0, 1: 0}

    def tr_evac(srcs, dsts, par):
        em.wait("tensor", ACTS, last_ev[par])
        pe_c = 0
        for j, s in enumerate(srcs):
            pe_c = em.op("tensor", lambda e, s=s, j=j, par=par: e.transpose(
                tr_ps[par][:, j * 128:(j + 1) * 128], s, ident[:, :]), inc=PE)
        em.wait("scalar", PE, pe_c)
        ac = 0
        for j, d in enumerate(dsts):
            ac = em.op("scalar", lambda e, d=d, j=j, par=par: e.copy(
                d, tr_ps[par][:, j * 128:(j + 1) * 128]), inc=ACTS)
        last_ev[par] = ac
        return ac

    par = 0
    for mi, (w_in, wT_sb) in enumerate(((wqkv_in[0], wqT), (wqkv_in[1], wkT),
                                        (wqkv_in[2], wvT))):
        em.wait("sync", ACTS)
        em.wait("sync", DVE)
        ld = em.dma("sync", w_f32[:, :].rearrange("p (t w) -> p t w", w=2048),
                    _tiled(w_in), "c_ldw")
        for ot in range(2):
            em.wait("vector", "c_ldw", ld)
            em.wait("scalar", "c_ldw", ld)
            em.wait("vector", PE)
            emit_ternary_tile(em, w_f32[:, ot * 2048:(ot + 1) * 2048], wt_bf, scr, 16)
            em.wait("tensor", DVE)
            for g8 in range(2):
                srcs = [wt_bf[:, (g8 * 8 + j) * 128:(g8 * 8 + j + 1) * 128]
                        for j in range(8)]
                dsts = [wT_sb[:, (g8 * 8 + j) * 256 + ot * 128:
                               (g8 * 8 + j) * 256 + (ot + 1) * 128] for j in range(8)]
                tr_evac(srcs, dsts, par % 2)
                par += 1
    em.wait("sync", ACTS)
    em.wait("sync", DVE)
    ld = em.dma("sync", wo_f32[:, :].rearrange("p (t w) -> p t w", w=256),
                _tiled(wo_in), "c_ldw")
    for ot in range(16):
        em.wait("vector", "c_ldw", ld)
        em.wait("scalar", "c_ldw", ld)
        em.wait("vector", PE)
        emit_ternary_tile(em, wo_f32[:, ot * 256:(ot + 1) * 256], wt_bf, scr, 2)
        em.wait("tensor", DVE)
        srcs = [wt_bf[:, r * 128:(r + 1) * 128] for r in range(2)]
        dsts = [woT_sb[:, r * 2048 + ot * 128: r * 2048 + (ot + 1) * 128]
                for r in range(2)]
        tr_evac(srcs, dsts, par % 2)
        par += 1
    em.wait("gpsimd", ACTS)
    em.dma("gpsimd", _tiled(woT_loc.ap(), p=128),
           woT_sb[:, :].rearrange("p (r o) -> p r o", o=2048), WOS)
    em.wait("gpsimd", WOS)
    em.collective("AllGather", OP.bypass, [list(range(8))],
                  [woT_loc.ap().opt()], [woT.ap().opt()], CC)
    cc_wo = em.cnt[CC]
    dstack.close()

    astack = ExitStack()
    xT_b = em.sbuf("cxT_b", [128, 16 * 2048], BF16, astack, is_transpose_dest=True)
    tabs = [em.sbuf(f"ctab{i}", [128, 2048], F32, astack) for i in range(4)]
    permT = em.sbuf("cpermT", [128, 128], BF16, astack)
    dmask = em.sbuf("cdmask", [128, 128], F32, astack)
    qp = em.sbuf("cqp", [128, 2048], BF16, astack)
    kp = em.sbuf("ckp", [128, 2048], BF16, astack)
    vT = em.sbuf("cvT", [128, 2048], BF16, astack)
    vN = em.sbuf("cvN", [128, 2048], BF16, astack)
    qraw = em.sbuf("cqraw", [128, 2048], BF16, astack)
    t1 = em.sbuf("ct1", [128, 2048], F32, astack)
    t2 = em.sbuf("ct2", [128, 512], F32, astack)
    pbf = em.sbuf("cpbf", [128, 2048], BF16, astack)
    ptT = em.sbuf("cptT", [128, 2048], BF16, astack)
    yT_sb = em.sbuf("cyT_sb", [128, 2048], BF16, astack)
    sm = em.sbuf("csm", [128, 8], F32, astack)
    ps_a = em.psum("cps_a", [128, 512], F32, astack)
    ps_b = em.psum("cps_b", [128, 512], F32, astack)
    ps_s = em.psum("cps_s", [128, 2048], F32, astack)
    ps_t2 = [em.psum("cps_t20", [128, 1024], BF16, astack),
             em.psum("cps_t21", [128, 1024], BF16, astack)]

    for i in range(4):
        em.dma("sync", tabs[i][:, :], tabs_in[i], "c_ldt")
    em.dma("sync", permT[:, :], permT_in, "c_ldt")
    ld_tab = em.dma("sync", dmask[:, :], dmask_in, "c_ldt")
    em.wait("sync", CC, cc_x)
    em.wait("tensor", "c_ldt", ld_tab)
    em.wait("vector", "c_ldt", ld_tab)

    pv_evac = 0
    str_evac = {0: 0, 1: 0}

    def project(dst, wT_sb, h_loc, cos_t, sin_t):
        nonlocal pv_evac
        em.wait("tensor", DVE)
        for tb4 in range(4):
            em.wait("tensor", ACTS)
            pe_c = 0
            for d in range(16):
                pe_c = em.op("tensor", lambda e, d=d, tb4=tb4, h_loc=h_loc, wT_sb=wT_sb:
                             e.matmul(
                    ps_a[:, :],
                    wT_sb[:, d * 256 + h_loc * 128: d * 256 + (h_loc + 1) * 128],
                    xT_b[:, d * 2048 + tb4 * 512: d * 2048 + (tb4 + 1) * 512],
                    start=(d == 0), stop=(d == 15)), inc=PE if d == 15 else None)
            em.wait("scalar", PE, pe_c)
            tgt = qraw if cos_t is not None else dst
            em.op("scalar", lambda e, tb4=tb4, tgt=tgt: e.copy(
                tgt[:, tb4 * 512:(tb4 + 1) * 512], ps_a[:, :]), inc=ACTS)
        if cos_t is None:
            return
        ac_all = em.cnt[ACTS]
        em.wait("tensor", ACTS, ac_all)
        em.wait("vector", ACTS, ac_all)
        for tb4 in range(4):
            em.wait("tensor", DVE)
            pe_c = em.op("tensor", lambda e, tb4=tb4: e.matmul(
                ps_a[:, :], permT[:, :], qraw[:, tb4 * 512:(tb4 + 1) * 512],
                start=True, stop=True), inc=PE)
            em.op("vector", lambda e, tb4=tb4, cos_t=cos_t: e.tensor_tensor(
                t1[:, tb4 * 512:(tb4 + 1) * 512], qraw[:, tb4 * 512:(tb4 + 1) * 512],
                cos_t[:, tb4 * 512:(tb4 + 1) * 512], OP.mult))
            em.wait("vector", PE, pe_c)
            em.op("vector", lambda e, sin_t=sin_t, tb4=tb4: e.tensor_tensor(
                t2[:, :], ps_a[:, :], sin_t[:, tb4 * 512:(tb4 + 1) * 512], OP.mult),
                inc=DVE)
            em.op("vector", lambda e, tb4=tb4, dst=dst: e.tensor_tensor(
                dst[:, tb4 * 512:(tb4 + 1) * 512], t1[:, tb4 * 512:(tb4 + 1) * 512],
                t2[:, :], OP.add), inc=DVE)

    last_yst = 0
    em.wait("sync", WOS)
    for b in range(B):
        em.wait("sync", PE)
        for i in range(16):
            em.dma("sync", xT_b[:, i * 2048:(i + 1) * 2048],
                   x2all.ap()[b * 2048:(b + 1) * 2048, i * 128:(i + 1) * 128], TRD,
                   transpose=True)
        trd = em.cnt[TRD]
        em.wait("tensor", TRD, trd)
        for h_loc in range(2):
            project(qp, wqT, h_loc, tabs[0], tabs[1])
            project(kp, wkT, h_loc, tabs[2], tabs[3])
            project(vT, wvT, h_loc, None, None)
            em.wait("tensor", ACTS)
            for g8 in range(2):
                par = g8 % 2
                em.wait("tensor", DVE, str_evac[par])
                pe_c = 0
                for j in range(8):
                    kb = g8 * 8 + j
                    pe_c = em.op("tensor", lambda e, kb=kb, j=j, par=par: e.transpose(
                        ps_t2[par][:, j * 128:(j + 1) * 128],
                        vT[:, kb * 128:(kb + 1) * 128], ident[:, :]), inc=PE)
                em.wait("vector", PE, pe_c)
                ev = 0
                for j in range(8):
                    kb = g8 * 8 + j
                    ev = em.op("vector", lambda e, kb=kb, j=j, par=par: e.tensor_copy(
                        vN[:, kb * 128:(kb + 1) * 128],
                        ps_t2[par][:, j * 128:(j + 1) * 128]), inc=DVE)
                str_evac[par] = ev
            em.wait("tensor", DVE)
            prev_exp = 0
            for qt in range(16):
                L = (qt + 1) * 128
                nblk = (L + 511) // 512
                em.wait("tensor", ACTS, prev_exp)
                pe_s = 0
                for blk in range(nblk):
                    w = min(512, L - blk * 512)
                    pe_s = em.op("tensor", lambda e, blk=blk, w=w, qt=qt: e.matmul(
                        ps_s[:, blk * 512: blk * 512 + w],
                        qp[:, qt * 128:(qt + 1) * 128],
                        kp[:, blk * 512: blk * 512 + w],
                        start=True, stop=True), inc=PE)
                em.wait("vector", PE, pe_s)
                em.op("vector", lambda e, qt=qt: e.tensor_tensor(
                    ps_s[:, qt * 128:(qt + 1) * 128], ps_s[:, qt * 128:(qt + 1) * 128],
                    dmask[:, :], OP.add))
                em.op("vector", lambda e, L=L: e.tensor_reduce(
                    sm[:, 0:1], ps_s[:, 0:L], mybir.AxisListType.X, OP.max))
                dv_m = em.op("vector", lambda e: e.tensor_scalar(
                    sm[:, 1:2], sm[:, 0:1], -1.0, None, OP.mult), inc=DVE)
                em.wait("scalar", DVE, dv_m)
                em.wait("scalar", PE)
                prev_exp = em.op("scalar", lambda e, L=L: e.activation(
                    pbf[:, 0:L], ps_s[:, 0:L], AF.Exp, bias=sm[:, 1:2],
                    accum_out=sm[:, 2:3]), inc=ACTS)
                em.wait("vector", ACTS, prev_exp)
                em.op("vector", lambda e: e.reciprocal(sm[:, 3:4], sm[:, 2:3]))
                dv_p = em.op("vector", lambda e, L=L: e.tensor_scalar(
                    pbf[:, 0:L], pbf[:, 0:L], sm[:, 3:4], None, OP.mult), inc=DVE)
                em.wait("tensor", DVE, dv_p)
                pvps = ps_a if qt % 2 == 0 else ps_b
                em.wait("tensor", DVE, pv_evac)
                tr_done = {}
                for kb in range(qt + 2):
                    if kb <= qt:
                        par = kb % 2
                        em.wait("tensor", DVE, str_evac[par])
                        pe_t = em.op("tensor", lambda e, kb=kb, par=par: e.transpose(
                            ps_t2[par][:, (kb % 8) * 128:(kb % 8) * 128 + 128],
                            pbf[:, kb * 128:(kb + 1) * 128], ident[:, :]), inc=PE)
                        tr_done[kb] = pe_t
                    if kb >= 1:
                        k0 = kb - 1
                        par = k0 % 2
                        em.wait("vector", PE, tr_done[k0])
                        ev = em.op("vector", lambda e, k0=k0, par=par: e.tensor_copy(
                            ptT[:, k0 * 128:(k0 + 1) * 128],
                            ps_t2[par][:, (k0 % 8) * 128:(k0 % 8) * 128 + 128]),
                            inc=DVE)
                        str_evac[par] = ev
                        em.wait("tensor", DVE, ev)
                        em.op("tensor", lambda e, k0=k0, qt=qt, pvps=pvps: e.matmul(
                            pvps[:, 0:128], vN[:, k0 * 128:(k0 + 1) * 128],
                            ptT[:, k0 * 128:(k0 + 1) * 128],
                            start=(k0 == 0), stop=(k0 == qt)),
                            inc=PE if k0 == qt else None)
                pe_pv = em.cnt[PE]
                em.wait("vector", PE, pe_pv)
                pv_evac = em.op("vector", lambda e, qt=qt, pvps=pvps: e.tensor_copy(
                    yT_sb[:, qt * 128:(qt + 1) * 128], pvps[:, 0:128]), inc=DVE)
            em.wait("gpsimd", DVE, pv_evac)
            for half in range(2):
                j = 2 * b + half
                last_yst = em.dma(
                    "gpsimd",
                    yT_a2a_in.ap()[j * 256 + h_loc * 128: j * 256 + (h_loc + 1) * 128, :],
                    yT_sb[:, half * 1024:(half + 1) * 1024], YST)
            em.wait("vector", YST, last_yst)
    em.wait("gpsimd", YST, last_yst)
    em.collective("AllToAll", OP.bypass, [list(range(8))],
                  [yT_a2a_in.ap().opt()], [yT_tok.ap().opt()], CC)
    cc_y = em.cnt[CC]
    astack.close()

    fstack = ExitStack()
    yT_t = em.sbuf("cyT_t", [128, 16 * 128], BF16, fstack)
    wo_full = em.sbuf("cwo_full", [128, 16 * 2048], BF16, fstack)
    yacc = em.sbuf("cyacc", [128, 2048], F32, fstack)
    junk_bf = em.sbuf("cjunk", [128, 2048], BF16, fstack)
    stat = em.sbuf("cstat", [128, 8], F32, fstack)
    xres = em.sbuf("cxres", [128, 2048], F32, fstack)
    ps_y = [em.psum("cps_y0", [128, 512], F32, fstack),
            em.psum("cps_y1", [128, 512], F32, fstack)]

    em.wait("sync", CC, cc_wo)
    em.wait("sync", PE)
    em.wait("sync", DVE)
    em.wait("sync", ACTS)
    ld_wo = em.dma("sync", wo_full[:, :].rearrange("p (t o) -> p t o", o=2048),
                   _tiled(woT.ap()), "c_wol")
    em.wait("sync", CC, cc_y)
    em.wait("tensor", "c_wol", ld_wo)
    last_hst = 0
    last_yev = 0
    for tt in range(8):
        em.wait("sync", PE)
        ld_y = em.dma("sync", yT_t[:, :].rearrange("p (t o) -> p t o", o=128),
                      _tiled(yT_tok.ap()[:, tt * 128:(tt + 1) * 128]), YLD)
        em.wait("sync", DVE)
        xl = em.dma("sync", xres[:, :], x_in[tt * 128:(tt + 1) * 128, :], XRS)
        em.wait("tensor", YLD, ld_y)
        em.wait("vector", HST, last_hst)
        for ob in range(4):
            par = ob % 2
            em.wait("tensor", DVE)
            pe_c = 0
            for d in range(16):
                pe_c = em.op("tensor", lambda e, d=d, ob=ob, par=par: e.matmul(
                    ps_y[par][:, :], yT_t[:, d * 128:(d + 1) * 128],
                    wo_full[:, d * 2048 + ob * 512:(d * 2048) + (ob + 1) * 512],
                    start=(d == 0), stop=(d == 15)), inc=PE if d == 15 else None)
            em.wait("vector", PE, pe_c)
            last_yev = em.op("vector", lambda e, ob=ob, par=par: e.tensor_copy(
                yacc[:, ob * 512:(ob + 1) * 512], ps_y[par][:, :]), inc=DVE)
        em.wait("vector", XRS, xl)
        em.wait("vector", HST, last_hst)
        em.op("vector", lambda e: e.tensor_tensor(
            yacc[:, :], yacc[:, :], xres[:, :], OP.add), inc=DVE)
        em.wait("scalar", DVE)
        vln = emit_ln(em, yacc, junk_bf, stat)
        em.wait("gpsimd", DVE, vln)
        last_hst = em.dma("gpsimd", x_out[tt * 128:(tt + 1) * 128, :], yacc[:, :], HST)
    em.wait("gpsimd", HST)
    fstack.close()


def _host_dn_scales(down_scales, core_id):
    out = np.zeros((2048, 11), dtype=np.float32)
    for b in range(11):
        g = (704 * core_id + 64 * b) // 128
        out[:, b] = down_scales[g, :].astype(np.float32)
    return out


def build_stageB():
    nc = bass.Bass("TRN2", target_bir_lowering=False)
    em = Em(nc)
    x_in = em.dram_t("bx_in", [TLOC, 2048], F32, kind="ExternalInput").ap()
    em.dram_t("B_gup", [2 * FF_I, 32], I32, kind="ExternalInput")
    em.dram_t("B_gus", [2 * FF_I, 2], F16, kind="ExternalInput")
    em.dram_t("B_dnp", [2048, 88], I32, kind="ExternalInput")
    em.dram_t("B_dns", [2048, 11], F32, kind="ExternalInput")
    ident_in = em.dram_t("ident_in", [128, 128], BF16, kind="ExternalInput").ap()
    constv_in = em.dram_t("constv_in", [128, 32], F32, kind="ExternalInput").ap()
    x_out = em.dram_t("bx_out", [TLOC, 2048], F32, kind="ExternalOutput").ap()
    with em.ctx:
        em.sbuf("ident_bf", [128, 128], BF16)
        em.sbuf("constv", [128, 32], F32)
        em.dma("sync", em.sb["ident_bf"][:, :], ident_in, "b_ld")
        em.dma("sync", em.sb["constv"][:, :], constv_in, "b_ld")
        emit_stageB(em, 0, x_in, x_out)
        em.emit()
    return em


def consts_array(em):
    vals = getattr(em, "const_vals", [])
    arr = np.zeros((128, 32), dtype=np.float32)
    arr[:, :len(vals)] = np.asarray(vals, dtype=np.float32)[None, :]
    return arr


def stageB_inputs(em, x1_full, inputs, core_id):
    c = core_id
    return {
        "constv_in": consts_array(em),
        "bx_in": np.ascontiguousarray(x1_full[c * TLOC:(c + 1) * TLOC]),
        "B_gup": np.ascontiguousarray(inputs["L0_ffn_gate_up_packed"][:, 32 * c:32 * (c + 1)]),
        "B_gus": np.ascontiguousarray(
            np.asarray(inputs["L0_ffn_gate_up_scales"])[2 * c:2 * c + 2, :].T),
        "B_dnp": np.ascontiguousarray(inputs["L0_ffn_down_packed"][:, 88 * c:88 * (c + 1)]),
        "B_dns": _host_dn_scales(np.asarray(inputs["L0_ffn_down_scales"]), c),
        "ident_in": np.eye(128, dtype=ml_dtypes.bfloat16),
    }


def run_stageB(x1_full, inputs, sim=False):
    em = build_stageB()
    in_maps = [stageB_inputs(em, x1_full, inputs, c) for c in range(NCORES)]
    if sim:
        import concourse.bass_interp as bass_interp
        msim = bass_interp.MultiCoreSim(em.nc, NCORES)
        for c in range(NCORES):
            for k, v in in_maps[c].items():
                msim.cores[c].tensor(k)[:] = v
        msim.simulate()
        return np.concatenate(
            [np.asarray(msim.cores[c].mem_tensor("bx_out")) for c in range(NCORES)],
            axis=0)
    res = run_bass_kernel_spmd(em.nc, in_maps, core_ids=list(range(NCORES)))
    return np.concatenate([np.asarray(res.results[c]["bx_out"]) for c in range(NCORES)],
                          axis=0)




def build_stageC():
    nc = bass.Bass("TRN2", target_bir_lowering=False)
    em = Em(nc)
    x_in = em.dram_t("cx_in", [TLOC, 2048], F32, kind="ExternalInput").ap()
    em.dram_t("C_wq", [256, 2048], F32, kind="ExternalInput")
    em.dram_t("C_wk", [256, 2048], F32, kind="ExternalInput")
    em.dram_t("C_wv", [256, 2048], F32, kind="ExternalInput")
    em.dram_t("C_wo", [2048, 256], F32, kind="ExternalInput")
    em.dram_t("C_cosq", [128, 2048], F32, kind="ExternalInput")
    em.dram_t("C_sinq", [128, 2048], F32, kind="ExternalInput")
    em.dram_t("C_cosk", [128, 2048], F32, kind="ExternalInput")
    em.dram_t("C_sink", [128, 2048], F32, kind="ExternalInput")
    em.dram_t("C_permT", [128, 128], BF16, kind="ExternalInput")
    em.dram_t("C_dmask", [128, 128], F32, kind="ExternalInput")
    ident_in = em.dram_t("ident_in", [128, 128], BF16, kind="ExternalInput").ap()
    constv_in = em.dram_t("constv_in", [128, 32], F32, kind="ExternalInput").ap()
    x_out = em.dram_t("cx_out", [TLOC, 2048], F32, kind="ExternalOutput").ap()
    with em.ctx:
        em.sbuf("ident_bf", [128, 128], BF16)
        em.sbuf("constv", [128, 32], F32)
        em.dma("sync", em.sb["ident_bf"][:, :], ident_in, "c_ld")
        em.dma("sync", em.sb["constv"][:, :], constv_in, "c_ld")
        emit_stageC(em, x_in, x_out)
        em.emit()
    return em


def rope_tables():
    half = 64
    inv = 1.0 / 10000.0 ** (np.arange(half) / half)
    t = np.arange(S)
    fr = np.outer(inv, t)
    cos = np.cos(fr)
    sin = np.sin(fr)
    COS = np.concatenate([cos, cos], axis=0).astype(np.float32)
    SIN = np.concatenate([sin, sin], axis=0).astype(np.float32)
    return COS, SIN


def stageC_inputs(em, x2_full, inputs, core_id):
    c = core_id
    COS, SIN = rope_tables()
    sc = float(HD) ** -0.5
    permT = np.zeros((128, 128), dtype=ml_dtypes.bfloat16)
    for d in range(64):
        permT[d + 64, d] = -1.0
        permT[d, d + 64] = 1.0
    dmask = np.triu(np.full((128, 128), -1e30, dtype=np.float32), 1)
    return {
        "cx_in": np.ascontiguousarray(x2_full[c * TLOC:(c + 1) * TLOC]),
        "C_wq": np.ascontiguousarray(np.asarray(inputs["L1_attn_w_q"])[256 * c:256 * (c + 1)]),
        "C_wk": np.ascontiguousarray(np.asarray(inputs["L1_attn_w_k"])[256 * c:256 * (c + 1)]),
        "C_wv": np.ascontiguousarray(np.asarray(inputs["L1_attn_w_v"])[256 * c:256 * (c + 1)]),
        "C_wo": np.ascontiguousarray(np.asarray(inputs["L1_attn_w_o"])[:, 256 * c:256 * (c + 1)]),
        "C_cosq": COS * sc, "C_sinq": SIN * sc, "C_cosk": COS, "C_sink": SIN,
        "C_permT": permT, "C_dmask": dmask,
        "ident_in": np.eye(128, dtype=ml_dtypes.bfloat16),
        "constv_in": consts_array(em),
    }


def run_stageC(x2_full, inputs, sim=False):
    em = build_stageC()
    in_maps = [stageC_inputs(em, x2_full, inputs, c) for c in range(NCORES)]
    if sim:
        import concourse.bass_interp as bass_interp
        msim = bass_interp.MultiCoreSim(em.nc, NCORES)
        for c in range(NCORES):
            for k, v in in_maps[c].items():
                msim.cores[c].tensor(k)[:] = v
        msim.simulate()
        return np.concatenate(
            [np.asarray(msim.cores[c].mem_tensor("cx_out")) for c in range(NCORES)],
            axis=0)
    res = run_bass_kernel_spmd(em.nc, in_maps, core_ids=list(range(NCORES)))
    return np.concatenate([np.asarray(res.results[c]["cx_out"]) for c in range(NCORES)],
                          axis=0)



def emit_stageD(em, x_in, x_out):
    xbf_d = em.dram_t("dx_bf", [TLOC, 2048], BF16)
    x4all = em.dram_t("dx_all", [TOK, 2048], BF16)
    dy_part = em.dram_t("dy_part", [TOK, 2048], F32)
    dy_mine = em.dram_t("dy_mine", [TLOC, 2048], F32)
    w1bf = em.dram_t("dw1bf", [2, 2048, MOE_I], BF16)
    w3bf = em.dram_t("dw3bf", [2, 2048, MOE_I], BF16)
    w2bf = em.dram_t("dw2bf", [2, MOE_I, 2048], BF16)
    ws1bf = em.dram_t("dws1bf", [2048, 352], BF16)
    ws3bf = em.dram_t("dws3bf", [2048, 352], BF16)
    ws2bf = em.dram_t("dws2bf", [352, 2048], BF16)

    LD, SWX, SWW, CC = "d_ld", "d_swx", "d_sww", "d_cc"
    PE, DVE, ACTS = "d_pe", "E_dve", "E_act"
    TRD, XRS, HST, YLD = "d_trd", "d_xrs", "d_hst", "d_yld"

    em.dma("gpsimd", xbf_d[:, :], x_in[:, :], SWX)
    em.wait("gpsimd", SWX)
    em.collective("AllGather", OP.bypass, [list(range(8))],
                  [xbf_d.ap().opt()], [x4all.ap().opt()], CC)
    cc_x = em.cnt[CC]
    for (dst, srcname) in ((w1bf, "D_w1T"), (w3bf, "D_w3T"), (w2bf, "D_w2T"),
                           (ws1bf, "D_ws1T"), (ws3bf, "D_ws3T"), (ws2bf, "D_ws2T")):
        em.dma("gpsimd", dst.ap(), em.dram[srcname].ap(), SWW)
    sww_done = em.cnt[SWW]

    xT_tb = em.sbuf("dxT_tb", [128, 16 * 1024], BF16, is_transpose_dest=True)
    wgT_bf = em.sbuf("dwgT", [128, 16 * 16], BF16)
    selm = em.sbuf("dselm", [128, 2 * 16], F32)
    ones16 = em.sbuf("dones1", [128, 16], F32)
    onesr = em.sbuf("donesr", [128, 128], BF16)
    P_sb = em.sbuf("dP", [128, 16], F32)
    msel = em.sbuf("dmsel", [128, 16], F32)
    gsx = em.sbuf("dgs", [128, 4], F32)
    grk = em.sbuf("dgrk", [128, 4], F32)
    mw = em.sbuf("dmw", [128, 16], F32)
    mw0 = em.sbuf("dmw0", [128, 16], F32)
    myc = em.sbuf("dmyc", [128, 2], F32)
    mycb = em.sbuf("dmycb", [128, 2], BF16)
    cbrow = em.sbuf("dcbrow", [128, 256], BF16)
    cb = em.sbuf("dcb", [128, 2 * 1024], BF16)
    sm = em.sbuf("dsm", [128, 8], F32)
    w1buf = em.sbuf("dw1buf", [128, 2 * 2048], BF16)
    w3buf = em.sbuf("dw3buf", [128, 2 * 2048], BF16)
    hT = em.sbuf("dhT", [128, 25 * 1024], BF16)
    silu_sb = em.sbuf("dsilu", [128, 512], BF16)
    w2sb = em.sbuf("dw2sb", [128, 2 * 25 * 256], BF16)
    ystg = em.sbuf("dystg", [128, 8 * 256], F32)
    junk_bf = em.sbuf("djunk", [128, 2048], BF16)
    stat = em.sbuf("dstat", [128, 8], F32)
    xres = em.sbuf("dxres", [128, 2048], F32)
    yfin = em.sbuf("dyfin", [128, 2048], F32)
    ps_g = em.psum("dps_g", [128, 512], F32)
    ps_u = em.psum("dps_u", [128, 512], F32)
    ps_l = em.psum("dps_l", [128, 512], F32)
    ps_y = [em.psum("dps_y0", [128, 512], F32),
            em.psum("dps_y1", [128, 512], F32)]
    ps_t = em.psum("dps_t", [128, 1024], BF16)

    ident = em.sb["ident_bf"]
    ld_wg = em.dma("gpsimd", wgT_bf[:, :].rearrange("p (t w) -> p t w", w=16),
                   _tiled(em.dram["D_wgT"].ap()), "d_ldwg")
    ld_sel = em.dma("sync", selm[:, :], em.dram["D_selm"].ap(), LD)
    for _eng in ("tensor", "vector", "scalar"):
        em.wait(_eng, LD, ld_sel)
    em.op("vector", lambda e: e.memset(ones16[:, :], 1.0))
    em.op("vector", lambda e: e.memset(onesr[0:1, :], 1.0))
    em.wait("tensor", "d_ldwg", ld_wg)
    em.wait("sync", CC, cc_x)
    em.wait("tensor", SWW, sww_done)
    em.wait("sync", SWW, sww_done)

    NIT = {0: 11, 1: 11, 2: 3}
    IBASE = {0: 0, 1: 11, 2: 22}
    W13SRC = {0: (w1bf, w3bf, 0), 1: (w1bf, w3bf, 1), 2: (ws1bf, ws3bf, None)}

    last_pst = 0
    w1_pe = {0: 0, 1: 0}
    w2_pe = {0: 0, 1: 0}
    w1par = 0
    last_yst = 0
    for tb in range(8):
        em.wait("sync", PE)
        for i in range(16):
            em.dma("sync", xT_tb[:, i * 1024:(i + 1) * 1024],
                   x4all.ap()[tb * 1024:(tb + 1) * 1024, i * 128:(i + 1) * 128], TRD,
                   transpose=True)
        trd = em.cnt[TRD]
        em.wait("tensor", TRD, trd)
        for tt in range(8):
            em.wait("tensor", DVE)
            pe_c = 0
            for d in range(16):
                pe_c = em.op("tensor", lambda e, d=d, tt=tt: e.matmul(
                    ps_l[:, 0:16],
                    xT_tb[:, d * 1024 + tt * 128: d * 1024 + (tt + 1) * 128],
                    wgT_bf[:, d * 16:(d + 1) * 16],
                    start=(d == 0), stop=(d == 15)), inc=PE if d == 15 else None)
            em.wait("vector", PE, pe_c)
            em.op("vector", lambda e: e.tensor_reduce(
                sm[:, 0:1], ps_l[:, 0:16], mybir.AxisListType.X, OP.max))
            dv = em.op("vector", lambda e: e.tensor_scalar(
                sm[:, 1:2], sm[:, 0:1], -1.0, None, OP.mult), inc=DVE)
            em.wait("scalar", DVE, dv)
            em.wait("scalar", PE, pe_c)
            ac = em.op("scalar", lambda e: e.activation(
                P_sb[:, :], ps_l[:, 0:16], AF.Exp, bias=sm[:, 1:2],
                accum_out=sm[:, 2:3]), inc=ACTS)
            em.wait("vector", ACTS, ac)
            em.op("vector", lambda e: e.reciprocal(sm[:, 3:4], sm[:, 2:3]))
            em.op("vector", lambda e: e.tensor_scalar(
                P_sb[:, :], P_sb[:, :], sm[:, 3:4], None, OP.mult))
            em.op("vector", lambda e: e.tensor_tensor(
                gsx[:, :], P_sb[:, 0:16:4], P_sb[:, 1:16:4], OP.max))
            em.op("vector", lambda e: e.tensor_tensor(
                mw[:, 0:4], P_sb[:, 2:16:4], P_sb[:, 3:16:4], OP.max))
            em.op("vector", lambda e: e.tensor_tensor(
                gsx[:, :], gsx[:, :], mw[:, 0:4], OP.max))
            for g in range(4):
                em.op("vector", lambda e, g=g: e.tensor_scalar(
                    mw[:, 0:4], gsx[:, :], gsx[:, g:g + 1], None, OP.is_gt))
                em.op("vector", lambda e, g=g: e.tensor_reduce(
                    grk[:, g:g + 1], mw[:, 0:4], mybir.AxisListType.X, OP.add))
            em.op("vector", lambda e: e.tensor_scalar(
                grk[:, :], grk[:, :], 2.0, None, OP.is_lt))
            for g in range(4):
                em.op("vector", lambda e, g=g: e.tensor_scalar(
                    msel[:, 4 * g:4 * (g + 1)], ones16[:, 0:4], grk[:, g:g + 1],
                    None, OP.mult))
            em.op("vector", lambda e: e.tensor_scalar(
                mw[:, :], msel[:, :], 1e30, -1e30, OP.mult, OP.add))
            em.op("vector", lambda e: e.tensor_tensor(
                mw[:, :], P_sb[:, :], mw[:, :], OP.add))
            em.op("vector", lambda e: e.tensor_copy(mw0[:, :], mw[:, :]))
            for it4 in range(3):
                em.op("vector", lambda e: e.tensor_reduce(
                    sm[:, 4:5], mw[:, :], mybir.AxisListType.X, OP.max))
                em.op("vector", lambda e: e.tensor_scalar(
                    msel[:, :], mw[:, :], sm[:, 4:5], None, OP.is_ge))
                em.op("vector", lambda e: e.scalar_tensor_tensor(
                    mw[:, :], msel[:, :], -2e30, mw[:, :], OP.mult, OP.add))
            em.op("vector", lambda e: e.tensor_reduce(
                sm[:, 4:5], mw[:, :], mybir.AxisListType.X, OP.max))
            em.op("vector", lambda e: e.tensor_scalar(
                msel[:, :], mw0[:, :], sm[:, 4:5], None, OP.is_ge))
            em.op("vector", lambda e: e.tensor_tensor(
                msel[:, :], msel[:, :], P_sb[:, :], OP.mult))
            for el in range(2):
                em.op("vector", lambda e, el=el: e.tensor_tensor(
                    mw[:, :], msel[:, :], selm[:, el * 16:(el + 1) * 16], OP.mult))
                em.op("vector", lambda e, el=el: e.tensor_reduce(
                    myc[:, el:el + 1], mw[:, :], mybir.AxisListType.X, OP.add))
            dv = em.op("vector", lambda e: e.tensor_copy(mycb[:, :], myc[:, :]),
                       inc=DVE)
            em.wait("tensor", DVE, dv)
            em.wait("tensor", ACTS, last_pst)
            for el in range(2):
                em.op("tensor", lambda e, el=el: e.transpose(
                    ps_t[0:1, el * 128:(el + 1) * 128], mycb[:, el:el + 1],
                    ident[:, :]), inc=PE)
            pe_t = em.cnt[PE]
            em.wait("scalar", PE, pe_t)
            for el in range(2):
                last_pst = em.op("scalar", lambda e, el=el: e.copy(
                    cbrow[0:1, el * 128:(el + 1) * 128],
                    ps_t[0:1, el * 128:(el + 1) * 128]), inc=ACTS)
            em.wait("tensor", ACTS, last_pst)
            for el in range(2):
                em.op("tensor", lambda e, el=el: e.matmul(
                    ps_l[:, 128 + el * 128: 128 + (el + 1) * 128],
                    onesr[0:1, :], cbrow[0:1, el * 128:(el + 1) * 128],
                    start=True, stop=True), inc=PE)
            pe_b = em.cnt[PE]
            em.wait("vector", PE, pe_b)
            for el in range(2):
                em.op("vector", lambda e, el=el, tt=tt: e.tensor_copy(
                    cb[:, el * 1024 + tt * 128: el * 1024 + (tt + 1) * 128],
                    ps_l[:, 128 + el * 128: 128 + (el + 1) * 128]), inc=DVE)
        cb_done = em.cnt[DVE]
        em.wait("tensor", DVE, cb_done)
        em.wait("vector", PE)
        for sl in range(3):
            w1s, w3s, ei = W13SRC[sl]
            idim = MOE_I if sl < 2 else 352
            for ib in range(NIT[sl]):
                iw = min(128, idim - ib * 128)
                buf = w1par % 2
                w1par += 1
                em.wait("sync", PE, w1_pe[buf])
                if ei is not None:
                    s1 = w1s.ap()[ei, :, ib * 128:ib * 128 + iw]
                    s3 = w3s.ap()[ei, :, ib * 128:ib * 128 + iw]
                else:
                    s1 = w1s.ap()[:, ib * 128:ib * 128 + iw]
                    s3 = w3s.ap()[:, ib * 128:ib * 128 + iw]
                l1 = em.dma("sync",
                            w1buf[:, buf * 2048:buf * 2048 + 16 * iw].rearrange(
                                "p (t o) -> p t o", o=iw), _tiled(s1), f"d_w1l{buf}")
                l3 = em.dma("sync",
                            w3buf[:, buf * 2048:buf * 2048 + 16 * iw].rearrange(
                                "p (t o) -> p t o", o=iw), _tiled(s3), f"d_w3l{buf}")
                em.wait("tensor", f"d_w1l{buf}", l1)
                em.wait("tensor", f"d_w3l{buf}", l3)
                for th in range(2):
                    em.wait("tensor", ACTS)
                    pe_g = 0
                    for d in range(16):
                        pe_g = em.op("tensor", lambda e, d=d, th=th, buf=buf, iw=iw:
                                     e.matmul(
                            ps_g[0:iw, :],
                            w1buf[:, buf * 2048 + d * iw: buf * 2048 + (d + 1) * iw],
                            xT_tb[:, d * 1024 + th * 512: d * 1024 + (th + 1) * 512],
                            start=(d == 0), stop=(d == 15)),
                            inc=PE if d == 15 else None)
                    em.wait("tensor", DVE)
                    pe_u = 0
                    for d in range(16):
                        pe_u = em.op("tensor", lambda e, d=d, th=th, buf=buf, iw=iw:
                                     e.matmul(
                            ps_u[0:iw, :],
                            w3buf[:, buf * 2048 + d * iw: buf * 2048 + (d + 1) * iw],
                            xT_tb[:, d * 1024 + th * 512: d * 1024 + (th + 1) * 512],
                            start=(d == 0), stop=(d == 15)),
                            inc=PE if d == 15 else None)
                    em.wait("scalar", pe_g if False else PE, pe_g)
                    em.wait("scalar", DVE)
                    ac_s = em.op("scalar", lambda e, iw=iw: e.activation(
                        silu_sb[0:iw, :], ps_g[0:iw, :], AF.Silu), inc=ACTS)
                    em.wait("vector", ACTS, ac_s)
                    em.wait("vector", PE, pe_u)
                    em.op("vector", lambda e, iw=iw: e.tensor_tensor(
                        silu_sb[0:iw, :], silu_sb[0:iw, :], ps_u[0:iw, :], OP.mult))
                    hslot = (IBASE[sl] + ib) * 1024 + th * 512
                    if sl < 2:
                        em.op("vector", lambda e, iw=iw, hslot=hslot, sl=sl, th=th:
                              e.tensor_tensor(
                            hT[0:iw, hslot:hslot + 512], silu_sb[0:iw, :],
                            cb[0:iw, sl * 1024 + th * 512: sl * 1024 + (th + 1) * 512],
                            OP.mult), inc=DVE)
                    else:
                        em.op("vector", lambda e, iw=iw, hslot=hslot: e.tensor_copy(
                            hT[0:iw, hslot:hslot + 512], silu_sb[0:iw, :]), inc=DVE)
                w1_pe[buf] = em.cnt[PE]
        hT_done = em.cnt[DVE]
        for ob in range(8):
            buf = ob % 2
            em.wait("sync", PE, w2_pe[buf])
            l2a = em.dma("sync",
                         w2sb[:, buf * 6400: buf * 6400 + 11 * 256].rearrange(
                             "p (t o) -> p t o", o=256),
                         _tiled(w2bf.ap()[0, :, ob * 256:(ob + 1) * 256]),
                         f"d_w2a{buf}")
            l2b = em.dma("sync",
                         w2sb[:, buf * 6400 + 11 * 256: buf * 6400 + 22 * 256].rearrange(
                             "p (t o) -> p t o", o=256),
                         _tiled(w2bf.ap()[1, :, ob * 256:(ob + 1) * 256]),
                         f"d_w2b{buf}")
            l2c = em.dma("sync",
                         w2sb[:, buf * 6400 + 22 * 256: buf * 6400 + 24 * 256].rearrange(
                             "p (t o) -> p t o", o=256),
                         _tiled(ws2bf.ap()[0:256, ob * 256:(ob + 1) * 256]),
                         f"d_w2c{buf}")
            l2d = em.dma("sync",
                         w2sb[0:96, buf * 6400 + 24 * 256: buf * 6400 + 25 * 256],
                         ws2bf.ap()[256:352, ob * 256:(ob + 1) * 256], f"d_w2d{buf}")
            em.wait("tensor", f"d_w2a{buf}", l2a)
            em.wait("tensor", f"d_w2b{buf}", l2b)
            em.wait("tensor", f"d_w2c{buf}", l2c)
            em.wait("tensor", f"d_w2d{buf}", l2d)
            em.wait("tensor", DVE, hT_done)
            for tt in range(8):
                par = (ob * 8 + tt) % 2
                em.wait("tensor", DVE)
                pe_c = 0
                nsl = 0
                for sl in range(3):
                    idim = MOE_I if sl < 2 else 352
                    for ib in range(NIT[sl]):
                        iw = min(128, idim - ib * 128)
                        it = IBASE[sl] + ib
                        nsl += 1
                        pe_c = em.op("tensor", lambda e, it=it, iw=iw, tt=tt, par=par,
                                     buf=buf, first=(nsl == 1), last=(nsl == 25):
                                     e.matmul(
                            ps_y[par][:, 0:256],
                            hT[0:iw, it * 1024 + tt * 128: it * 1024 + (tt + 1) * 128],
                            w2sb[0:iw, buf * 6400 + it * 256: buf * 6400 + (it + 1) * 256],
                            start=first, stop=last), inc=PE if nsl == 25 else None)
                em.wait("vector", PE, pe_c)
                em.op("vector", lambda e, tt=tt, par=par: e.tensor_copy(
                    ystg[:, tt * 256:(tt + 1) * 256], ps_y[par][:, 0:256]), inc=DVE)
            w2_pe[buf] = em.cnt[PE]
            dv = em.cnt[DVE]
            em.wait("gpsimd", DVE, dv)
            last_yst = em.dma(
                "gpsimd",
                _tiled(dy_part.ap()[tb * 1024:(tb + 1) * 1024,
                                    ob * 256:(ob + 1) * 256]),
                ystg[:, :].rearrange("p (t o) -> p t o", o=256), "d_yst")
            em.wait("vector", "d_yst", last_yst)
    em.wait("gpsimd", "d_yst", last_yst)
    em.collective("ReduceScatter", OP.add, [list(range(8))],
                  [dy_part.ap().opt()], [dy_mine.ap().opt()], CC)
    cc_y = em.cnt[CC]
    em.wait("sync", CC, cc_y)
    last_hst = 0
    for tt in range(8):
        em.wait("sync", DVE)
        em.wait("sync", HST, last_hst)
        yl = em.dma("sync", yfin[:, :], dy_mine.ap()[tt * 128:(tt + 1) * 128, :], YLD)
        xl = em.dma("sync", xres[:, :], x_in[tt * 128:(tt + 1) * 128, :], XRS)
        em.wait("vector", YLD, yl)
        em.wait("vector", XRS, xl)
        em.wait("vector", HST, last_hst)
        em.op("vector", lambda e: e.tensor_tensor(
            yfin[:, :], yfin[:, :], xres[:, :], OP.add), inc=DVE)
        em.wait("scalar", DVE)
        vln = emit_ln(em, yfin, junk_bf, stat)
        em.wait("gpsimd", DVE, vln)
        last_hst = em.dma("gpsimd", x_out[tt * 128:(tt + 1) * 128, :], yfin[:, :], HST)
    em.wait("gpsimd", HST)


def build_stageD():
    nc = bass.Bass("TRN2", target_bir_lowering=False)
    em = Em(nc)
    x_in = em.dram_t("dx_in", [TLOC, 2048], F32, kind="ExternalInput").ap()
    em.dram_t("D_w1T", [2, 2048, MOE_I], F32, kind="ExternalInput")
    em.dram_t("D_w3T", [2, 2048, MOE_I], F32, kind="ExternalInput")
    em.dram_t("D_w2T", [2, MOE_I, 2048], F32, kind="ExternalInput")
    em.dram_t("D_ws1T", [2048, 352], F32, kind="ExternalInput")
    em.dram_t("D_ws3T", [2048, 352], F32, kind="ExternalInput")
    em.dram_t("D_ws2T", [352, 2048], F32, kind="ExternalInput")
    em.dram_t("D_wgT", [2048, 16], F32, kind="ExternalInput")
    em.dram_t("D_selm", [128, 32], F32, kind="ExternalInput")
    ident_in = em.dram_t("ident_in", [128, 128], BF16, kind="ExternalInput").ap()
    constv_in = em.dram_t("constv_in", [128, 32], F32, kind="ExternalInput").ap()
    x_out = em.dram_t("dx_out", [TLOC, 2048], F32, kind="ExternalOutput").ap()
    with em.ctx:
        em.sbuf("ident_bf", [128, 128], BF16)
        em.sbuf("constv", [128, 32], F32)
        em.dma("sync", em.sb["ident_bf"][:, :], ident_in, "d_ld")
        em.dma("sync", em.sb["constv"][:, :], constv_in, "d_ld")
        emit_stageD(em, x_in, x_out)
        em.emit()
    return em


def stageD_inputs(em, x3_full, inputs, core_id):
    c = core_id
    f32 = np.float32
    W1 = np.asarray(inputs["L1_ffn_W1"], f32)
    W3 = np.asarray(inputs["L1_ffn_W3"], f32)
    W2 = np.asarray(inputs["L1_ffn_W2"], f32)
    Ws1 = np.asarray(inputs["L1_ffn_Ws1"], f32)
    Ws3 = np.asarray(inputs["L1_ffn_Ws3"], f32)
    Ws2 = np.asarray(inputs["L1_ffn_Ws2"], f32)
    Wg = np.asarray(inputs["L1_ffn_Wg"], f32)
    selm = np.zeros((128, 32), dtype=f32)
    selm[:, 2 * c] = 1.0
    selm[:, 16 + 2 * c + 1] = 1.0
    return {
        "dx_in": np.ascontiguousarray(x3_full[c * TLOC:(c + 1) * TLOC]),
        "D_w1T": np.ascontiguousarray(W1[2 * c:2 * c + 2].transpose(0, 2, 1)),
        "D_w3T": np.ascontiguousarray(W3[2 * c:2 * c + 2].transpose(0, 2, 1)),
        "D_w2T": np.ascontiguousarray(W2[2 * c:2 * c + 2].transpose(0, 2, 1)),
        "D_ws1T": np.ascontiguousarray(Ws1[352 * c:352 * (c + 1)].T),
        "D_ws3T": np.ascontiguousarray(Ws3[352 * c:352 * (c + 1)].T),
        "D_ws2T": np.ascontiguousarray(Ws2[:, 352 * c:352 * (c + 1)].T),
        "D_wgT": np.ascontiguousarray(Wg.T),
        "D_selm": selm,
        "ident_in": np.eye(128, dtype=ml_dtypes.bfloat16),
        "constv_in": consts_array(em),
    }


def run_stageD(x3_full, inputs, sim=False):
    em = build_stageD()
    in_maps = [stageD_inputs(em, x3_full, inputs, c) for c in range(NCORES)]
    if sim:
        import concourse.bass_interp as bass_interp
        msim = bass_interp.MultiCoreSim(em.nc, NCORES)
        for c in range(NCORES):
            for k, v in in_maps[c].items():
                msim.cores[c].tensor(k)[:] = v
        msim.simulate()
        return np.concatenate(
            [np.asarray(msim.cores[c].mem_tensor("dx_out")) for c in range(NCORES)],
            axis=0)
    res = run_bass_kernel_spmd(em.nc, in_maps, core_ids=list(range(NCORES)))
    return np.concatenate([np.asarray(res.results[c]["dx_out"]) for c in range(NCORES)],
                          axis=0)




def emit_stageA1(em):
    xT_in = em.dram["A_xT"].ap()
    wq_in = em.dram["A_wqkvz"].ap()
    wba_in = em.dram["A_wbaP"].ap()
    conv_in = em.dram["A_convw"].ap()
    alog_in = em.dram["A_alog"].ap()
    dt_in = em.dram["A_dt"].ap()
    qT_out = em.dram["a1_qT"].ap()
    kT_out = em.dram["a1_kT"].ap()
    vT_out = em.dram["a1_vT"].ap()
    z_out = em.dram["a1_z"].ap()
    gb_out = em.dram["a1_gb"].ap()

    wqkvz_loc = em.dram_t("a1_wloc", [256, 12288], BF16)
    wqkvz_bf = em.dram_t("a1_wbf", [2048, 12288], BF16)

    LD, SWX, SWW, CC = "a1_ld", "a1_swx", "a1_sww", "a1_cc"
    PE, DVE, ACTS = "a1_pe", "E_dve", "E_act"
    QST = "a1_qst"

    em.dma("gpsimd", wqkvz_loc.ap(), wq_in, SWW)
    em.wait("gpsimd", SWW)
    em.collective("AllGather", OP.bypass, [list(range(8))],
                  [wqkvz_loc.ap().opt()], [wqkvz_bf.ap().opt()], CC)
    cc_w = em.cnt[CC]

    xT_bf = em.sbuf("a1_xT", [128, 16 * 1027], BF16)
    wba_bf = em.sbuf("a1_wba", [128, 16 * 64], BF16)
    conv_sb = em.sbuf("a1_conv", [128, 64 * 4], F32)
    alog = em.sbuf("a1_alog", [128, 2], F32)
    negA = em.sbuf("a1_negA", [128, 2], F32)
    wbuf = em.sbuf("a1_wbuf", [128, 2 * 4096], BF16)
    raw = em.sbuf("a1_raw", [128, 4 * 1027], F32)
    sil = em.sbuf("a1_sil", [128, 4 * 1027], F32)
    sq = em.sbuf("a1_sq", [128, 1024], F32)
    onesc = em.sbuf("a1_onesc", [128, 1], F32)
    onesrow = em.sbuf("a1_onesrow", [128, 128], F32)
    rowb = em.sbuf("a1_rowb", [128, 2 * 1024], F32)
    obf = em.sbuf("a1_obf", [128, 4 * 1024], BF16)
    zbuf = em.sbuf("a1_zbuf", [128, 2048], BF16)
    ba_sb = em.sbuf("a1_ba", [128, 1024], F32)
    spl = em.sbuf("a1_spl", [128, 1024], F32)
    gbo = em.sbuf("a1_gbo", [128, 1024], F32)
    ps_a = em.psum("a1_ps_a", [128, 512], F32)
    ps_b = em.psum("a1_ps_b", [128, 512], F32)
    ps_r = em.psum("a1_ps_r", [128, 1024], F32)

    em.dma("gpsimd", xT_bf[:, :].rearrange("p (t w) -> p t w", w=1027),
           _tiled(xT_in), SWX)
    em.dma("gpsimd", wba_bf[:, :].rearrange("p (t w) -> p t w", w=64),
           _tiled(wba_in), SWX)
    ld_c = em.dma("sync", conv_sb[:, :].rearrange("p (t w) -> p t w", w=4),
                  _tiled(conv_in), LD)
    em.dma("sync", alog[32:64, 0:1], alog_in, LD)
    ld_al = em.dma("sync", alog[32:64, 1:2], dt_in, LD)
    swx_done = em.cnt[SWX]
    em.wait("vector", LD, ld_al)
    em.wait("vector", SWX, swx_done)
    em.op("vector", lambda e: e.memset(onesc[:, :], 1.0))
    em.op("vector", lambda e: e.memset(onesrow[0:1, :], 1.0))
    em.wait("scalar", LD, ld_al)
    ac = em.op("scalar", lambda e: e.activation(negA[32:64, 0:1], alog[32:64, 0:1],
                                                AF.Exp), inc=ACTS)
    em.wait("vector", ACTS, ac)
    em.op("vector", lambda e: e.tensor_scalar(
        negA[32:64, 0:1], negA[32:64, 0:1], -1.0, None, OP.mult))
    em.wait("tensor", SWX, swx_done)
    em.wait("tensor", CC, cc_w)
    em.wait("tensor", "a1_ld", ld_al)

    INV = float(DK) ** -0.5
    em.wait("sync", CC, cc_w)
    wb_pe = {0: 0, 1: 0}
    last_qst = 0
    for h in range(NK):
        for ot in range(4):
            buf = (h * 4 + ot) % 2
            ocol = 768 * h + ot * 128
            em.wait("sync", PE, wb_pe[buf])
            wl = em.dma("sync",
                        wbuf[:, buf * 4096:buf * 4096 + 2048].rearrange(
                            "p (t o) -> p t o", o=128),
                        _tiled(wqkvz_bf.ap()[:, ocol:ocol + 128]), f"a1_wl{buf}")
            em.wait("tensor", f"a1_wl{buf}", wl)
            em.wait("tensor", ACTS)
            for tb, (t0, tw) in enumerate(((0, 512), (512, 512), (1024, 3))):
                ps = ps_a if tb % 2 == 0 else ps_b
                em.wait("tensor", ACTS)
                pe_c = 0
                for d in range(16):
                    pe_c = em.op("tensor", lambda e, d=d, t0=t0, tw=tw, buf=buf,
                                 ps=ps: e.matmul(
                        ps[:, 0:tw],
                        wbuf[:, buf * 4096 + d * 128:buf * 4096 + (d + 1) * 128],
                        xT_bf[:, d * 1027 + t0: d * 1027 + t0 + tw],
                        start=(d == 0), stop=(d == 15)), inc=PE if d == 15 else None)
                em.wait("scalar", PE, pe_c)
                em.wait("scalar", DVE)
                em.op("scalar", lambda e, ot=ot, t0=t0, tw=tw, ps=ps: e.copy(
                    raw[:, ot * 1027 + t0: ot * 1027 + t0 + tw], ps[:, 0:tw]),
                    inc=ACTS)
            wb_pe[buf] = em.cnt[PE]
        ac_raw = em.cnt[ACTS]
        em.wait("vector", ACTS, ac_raw)
        for ot in range(4):
            cidx = {0: h, 1: 16 + h, 2: 32 + 2 * h, 3: 32 + 2 * h + 1}[ot]
            em.op("vector", lambda e, ot=ot, cidx=cidx: e.tensor_scalar(
                sil[:, ot * 1027:ot * 1027 + 1024], raw[:, ot * 1027:ot * 1027 + 1024],
                conv_sb[:, cidx * 4:cidx * 4 + 1], None, OP.mult))
            for j in range(1, 4):
                em.op("vector", lambda e, ot=ot, cidx=cidx, j=j: e.scalar_tensor_tensor(
                    sil[:, ot * 1027:ot * 1027 + 1024],
                    raw[:, ot * 1027 + j: ot * 1027 + j + 1024],
                    conv_sb[:, cidx * 4 + j:cidx * 4 + j + 1],
                    sil[:, ot * 1027:ot * 1027 + 1024], OP.mult, OP.add))
        dv_conv = em.cnt[DVE]
        em.wait("scalar", DVE, dv_conv)
        for ot in range(4):
            em.op("scalar", lambda e, ot=ot: e.activation(
                sil[:, ot * 1027:ot * 1027 + 1024], sil[:, ot * 1027:ot * 1027 + 1024],
                AF.Silu), inc=ACTS)
        ac_sil = em.cnt[ACTS]
        for ot in range(2):
            em.wait("scalar", ACTS, ac_sil)
            em.wait("scalar", PE)
            sq_a = em.op("scalar", lambda e, ot=ot: e.activation(
                sq[:, 0:1024], sil[:, ot * 1027:ot * 1027 + 1024], AF.Square),
                inc=ACTS)
            em.wait("tensor", ACTS, sq_a)
            em.wait("tensor", DVE)
            for tb in range(2):
                em.op("tensor", lambda e, tb=tb: e.matmul(
                    ps_r[0:1, tb * 512:(tb + 1) * 512], sq[:, tb * 512:(tb + 1) * 512],
                    onesc[:, 0:1], start=True, stop=True) if False else e.matmul(
                    ps_r[0:1, tb * 512:(tb + 1) * 512], onesc[:, 0:1],
                    sq[:, tb * 512:(tb + 1) * 512], start=True, stop=True), inc=PE)
            pe_s = em.cnt[PE]
            em.wait("scalar", PE, pe_s)
            cfold = math.log(INV * INV) if ot == 0 else math.log(INV)
            em.op("scalar", lambda e: e.activation(
                rowb[0:1, 0:1024], ps_r[0:1, 0:1024], AF.Ln, scale=1.0 / DK,
                bias=em.cv(1e-6, 1)))
            ac_r = em.op("scalar", lambda e, cfold=cfold: e.activation(
                rowb[0:1, 0:1024], rowb[0:1, 0:1024], AF.Exp, scale=-0.5,
                bias=em.cv(cfold, 1)), inc=ACTS)
            em.wait("tensor", ACTS, ac_r)
            for tb in range(2):
                em.op("tensor", lambda e, tb=tb: e.matmul(
                    ps_r[:, tb * 512:(tb + 1) * 512], onesrow[0:1, :],
                    rowb[0:1, tb * 512:(tb + 1) * 512], start=True, stop=True),
                    inc=PE)
            pe_b2 = em.cnt[PE]
            em.wait("vector", PE, pe_b2)
            em.op("vector", lambda e, ot=ot: e.tensor_tensor(
                obf[:, ot * 1024:(ot + 1) * 1024], sil[:, ot * 1027:ot * 1027 + 1024],
                ps_r[:, 0:1024], OP.mult), inc=DVE)
        em.wait("vector", ACTS, ac_sil)
        for ot in (2, 3):
            em.op("vector", lambda e, ot=ot: e.tensor_copy(
                obf[:, ot * 1024:(ot + 1) * 1024],
                sil[:, ot * 1027:ot * 1027 + 1024]), inc=DVE)
        dv_o = em.cnt[DVE]
        em.wait("gpsimd", DVE, dv_o)
        last_qst = em.dma("gpsimd", qT_out[h * 128:(h + 1) * 128, :],
                          obf[:, 0:1024], QST)
        last_qst = em.dma("gpsimd", kT_out[h * 128:(h + 1) * 128, :],
                          obf[:, 1024:2048], QST)
        last_qst = em.dma("gpsimd", vT_out[2 * h * 128:(2 * h + 1) * 128, :],
                          obf[:, 2048:3072], QST)
        last_qst = em.dma("gpsimd", vT_out[(2 * h + 1) * 128:(2 * h + 2) * 128, :],
                          obf[:, 3072:4096], QST)
        em.wait("vector", QST, last_qst)
        em.wait("scalar", QST, last_qst)

    em.wait("sync", PE)
    z_pe = {0: 0, 1: 0}
    for zb in range(8):
        pass
    for h in range(NK):
        buf = h % 2
        ocol = 768 * h + 512
        em.wait("sync", PE, z_pe[buf])
        wl = em.dma("sync",
                    wbuf[:, buf * 4096:buf * 4096 + 16 * 256].rearrange(
                        "p (t o) -> p t o", o=256),
                    _tiled(wqkvz_bf.ap()[:, ocol:ocol + 256]), f"a1_zl{buf}")
        em.wait("tensor", f"a1_zl{buf}", wl)
        for tt in range(8):
            par = tt % 2
            em.wait("tensor", DVE)
            pe_c = 0
            for d in range(16):
                pe_c = em.op("tensor", lambda e, d=d, tt=tt, buf=buf, par=par:
                             e.matmul(
                    (ps_a if par == 0 else ps_b)[:, 0:256],
                    xT_bf[:, d * 1027 + 3 + tt * 128: d * 1027 + 3 + (tt + 1) * 128],
                    wbuf[:, buf * 4096 + d * 256: buf * 4096 + (d + 1) * 256],
                    start=(d == 0), stop=(d == 15)), inc=PE if d == 15 else None)
            em.wait("vector", PE, pe_c)
            em.wait("vector", "a1_zst")
            dv_z = em.op("vector", lambda e, par=par: e.tensor_copy(
                zbuf[:, 0:256], (ps_a if par == 0 else ps_b)[:, 0:256]), inc=DVE)
            em.wait("gpsimd", DVE, dv_z)
            em.dma("gpsimd", z_out[tt * 128:(tt + 1) * 128, h * 256:(h + 1) * 256],
                   zbuf[:, 0:256], "a1_zst")
        z_pe[buf] = em.cnt[PE]

    em.wait("tensor", SWX, swx_done)
    for tb in range(2):
        ps = ps_a if tb % 2 == 0 else ps_b
        em.wait("tensor", DVE)
        pe_c = 0
        for d in range(16):
            pe_c = em.op("tensor", lambda e, d=d, tb=tb, ps=ps: e.matmul(
                ps[0:64, :],
                wba_bf[:, d * 64:(d + 1) * 64],
                xT_bf[:, d * 1027 + 3 + tb * 512: d * 1027 + 3 + (tb + 1) * 512],
                start=(d == 0), stop=(d == 15)), inc=PE if d == 15 else None)
        em.wait("scalar", PE, pe_c)
        em.op("scalar", lambda e, tb=tb, ps=ps: e.copy(
            ba_sb[0:64, tb * 512:(tb + 1) * 512], ps[0:64, :]), inc=ACTS)
    ac_ba = em.cnt[ACTS]
    em.wait("scalar", ACTS, ac_ba)
    em.op("scalar", lambda e: e.activation(
        gbo[0:32, :], ba_sb[0:32, :], AF.Sigmoid), inc=ACTS)
    em.wait("vector", ACTS)
    em.op("vector", lambda e: e.tensor_scalar(
        ba_sb[32:64, :], ba_sb[32:64, :], alog[32:64, 1:2], None, OP.add), inc=DVE)
    em.wait("scalar", DVE)
    em.op("scalar", lambda e: e.activation(spl[32:64, :], ba_sb[32:64, :], AF.Abs))
    em.op("scalar", lambda e: e.activation(spl[32:64, :], spl[32:64, :], AF.Exp,
                                           scale=-1.0))
    ac_sp = em.op("scalar", lambda e: e.activation(
        spl[32:64, :], spl[32:64, :], AF.Ln, bias=em.cv(1.0, 32, 32)), inc=ACTS)
    em.wait("vector", ACTS, ac_sp)
    em.op("vector", lambda e: e.tensor_scalar(
        gbo[32:64, :], ba_sb[32:64, :], 0.0, None, OP.max))
    em.op("vector", lambda e: e.tensor_tensor(
        gbo[32:64, :], gbo[32:64, :], spl[32:64, :], OP.add))
    dv_g = em.op("vector", lambda e: e.tensor_scalar(
        gbo[32:64, :], gbo[32:64, :], negA[32:64, 0:1], None, OP.mult), inc=DVE)
    em.wait("gpsimd", DVE, dv_g)
    em.wait("gpsimd", ACTS)
    em.dma("gpsimd", gb_out[:, :], gbo[0:64, :], QST)
    em.wait("gpsimd", QST)
    em.wait("gpsimd", "a1_zst")


def build_stageA1():
    nc = bass.Bass("TRN2", target_bir_lowering=False)
    em = Em(nc)
    em.dram_t("A_xT", [2048, 1027], F32, kind="ExternalInput")
    em.dram_t("A_wqkvz", [256, 12288], F32, kind="ExternalInput")
    em.dram_t("A_wbaP", [2048, 64], F32, kind="ExternalInput")
    em.dram_t("A_convw", [8192, 4], F32, kind="ExternalInput")
    em.dram_t("A_alog", [32, 1], F32, kind="ExternalInput")
    em.dram_t("A_dt", [32, 1], F32, kind="ExternalInput")
    ident_in = em.dram_t("ident_in", [128, 128], BF16, kind="ExternalInput").ap()
    constv_in = em.dram_t("constv_in", [128, 32], F32, kind="ExternalInput").ap()
    em.dram_t("a1_qT", [2048, TLOC], BF16, kind="ExternalOutput")
    em.dram_t("a1_kT", [2048, TLOC], BF16, kind="ExternalOutput")
    em.dram_t("a1_vT", [4096, TLOC], BF16, kind="ExternalOutput")
    em.dram_t("a1_z", [TLOC, 4096], BF16, kind="ExternalOutput")
    em.dram_t("a1_gb", [64, TLOC], F32, kind="ExternalOutput")
    with em.ctx:
        em.sbuf("ident_bf", [128, 128], BF16)
        em.sbuf("constv", [128, 32], F32)
        em.dma("sync", em.sb["ident_bf"][:, :], ident_in, "a1_ld")
        em.dma("sync", em.sb["constv"][:, :], constv_in, "a1_ld")
        emit_stageA1(em)
        em.emit()
    return em


def stageA1_inputs(em, inputs, core_id):
    c = core_id
    x = np.asarray(inputs["L0_attn_x"], np.float32).reshape(TOK, 2048)
    xs = np.zeros((TLOC + 3, 2048), dtype=np.float32)
    t0 = c * TLOC
    xs[3:] = x[t0:t0 + TLOC]
    if t0 % S != 0:
        xs[0:3] = x[t0 - 3:t0]
    wba = np.asarray(inputs["L0_attn_W_ba"], np.float32).reshape(2048, 16, 4)
    wbaP = np.concatenate([wba[:, :, 0:2].reshape(2048, 32),
                           wba[:, :, 2:4].reshape(2048, 32)], axis=1)
    return {
        "A_xT": np.ascontiguousarray(xs.T),
        "A_wqkvz": np.ascontiguousarray(
            np.asarray(inputs["L0_attn_W_qkvz"], np.float32)[256 * c:256 * (c + 1)]),
        "A_wbaP": np.ascontiguousarray(wbaP),
        "A_convw": np.asarray(inputs["L0_attn_conv_w"], np.float32),
        "A_alog": np.asarray(inputs["L0_attn_A_log"], np.float32).reshape(32, 1),
        "A_dt": np.asarray(inputs["L0_attn_dt_bias"], np.float32).reshape(32, 1),
        "ident_in": np.eye(128, dtype=ml_dtypes.bfloat16),
        "constv_in": consts_array(em),
    }


def run_stageA1(inputs, sim=False):
    em = build_stageA1()
    in_maps = [stageA1_inputs(em, inputs, c) for c in range(NCORES)]
    outs = ["a1_qT", "a1_kT", "a1_vT", "a1_z", "a1_gb"]
    if sim:
        import concourse.bass_interp as bass_interp
        msim = bass_interp.MultiCoreSim(em.nc, NCORES)
        for c in range(NCORES):
            for k, v in in_maps[c].items():
                msim.cores[c].tensor(k)[:] = v
        msim.simulate()
        return [{k: np.asarray(msim.cores[c].mem_tensor(k)) for k in outs}
                for c in range(NCORES)]
    res = run_bass_kernel_spmd(em.nc, in_maps, core_ids=list(range(NCORES)))
    return [{k: np.asarray(res.results[c][k]) for k in outs} for c in range(NCORES)]




A2_LIM = {"b": B, "c": 16, "kp": 2, "vi": 2, "db": 7, "stage": 9}


def emit_stageA2(em):
    qT_in = [em.dram[f"A2_qT{k}"].ap() for k in range(2)]
    kT_in = [em.dram[f"A2_kT{k}"].ap() for k in range(2)]
    vT_in = [em.dram[f"A2_vT{v}"].ap() for v in range(4)]
    z_in = em.dram["A2_z"].ap()
    g_in = em.dram["A2_g"].ap()
    b_in = em.dram["A2_b"].ap()
    nw_in = em.dram["A2_normw"].ap()
    masks_in = em.dram["A2_masks"].ap()
    gT_out = em.dram["a2_gT"].ap()

    LD, GLD, PE, DVE, ACTS = "a2_ld", "a2_gld", "a2_pe", "E_dve", "E_act"
    QLD, GST = "a2_qld", "a2_gst"
    ident = em.sb["ident_bf"]

    qTb = [em.sbuf(f"a2_qTb{k}", [128, 2048], BF16) for k in range(2)]
    kTb = [em.sbuf(f"a2_kTb{k}", [128, 2048], BF16) for k in range(2)]
    vTb = [em.sbuf(f"a2_vTb{v}", [128, 2048], BF16) for v in range(4)]
    gch = em.sbuf("a2_gch", [128, 256], F32)
    masks = em.sbuf("a2_masks", [128, 384], F32)
    onesrow = em.sbuf("a2_onesrow", [128, 128], F32)
    normw = em.sbuf("a2_normw", [128, 128], F32)
    nwrow = em.sbuf("a2_nwrow", [128, 128], F32)
    G_colT = em.sbuf("a2_Gcol", [128, 4], F32)
    expGc = em.sbuf("a2_expG", [128, 4], F32)
    bcolT = em.sbuf("a2_bcol", [128, 4], F32)
    bgcol = em.sbuf("a2_bgcol", [128, 4], F32)
    scal = em.sbuf("a2_scal", [128, 8], F32)
    grow = em.sbuf("a2_grow", [128, 384], F32)
    KKtM = em.sbuf("a2_KKtM", [128, 128], BF16)
    KQtM = em.sbuf("a2_KQtM", [128, 128], BF16)
    K_nat = em.sbuf("a2_Knat", [128, 128], BF16)
    gm = em.sbuf("a2_gm", [128, 128], F32)
    dec = em.sbuf("a2_dec", [128, 128], F32)
    decu = em.sbuf("a2_decu", [128, 128], F32)
    decui = em.sbuf("a2_decui", [128, 128], F32)
    bn = [em.sbuf(f"a2_bn{i}", [128, 128], BF16) for i in range(2)]
    bt = [em.sbuf(f"a2_bt{i}", [128, 128], BF16) for i in range(2)]
    pt = [em.sbuf(f"a2_pt{i}", [128, 128], BF16) for i in range(2)]
    WK = em.sbuf("a2_WK", [128, 128], BF16)
    WTn = em.sbuf("a2_WTn", [128, 128], BF16)
    Urhs = em.sbuf("a2_Urhs", [128, 128], BF16)
    Rbf = em.sbuf("a2_Rbf", [128, 128], BF16)
    qtl = em.sbuf("a2_qtl", [128, 128], BF16)
    DTt = em.sbuf("a2_DT", [128, 128], BF16)
    Kbr = em.sbuf("a2_Kbr", [128, 128], BF16)
    Sst = em.sbuf("a2_S", [128, 4 * 2 * 128], BF16)
    zl = em.sbuf("a2_zl", [128, 2 * 128], BF16)
    silz = em.sbuf("a2_silz", [128, 128], BF16)
    gt1 = em.sbuf("a2_gt1", [128, 128], F32)
    sqj = em.sbuf("a2_sqj", [128, 128], BF16)
    gTo = [em.sbuf(f"a2_gTo{v}", [128, 2048], BF16) for v in range(4)]
    ps_c = em.psum("a2_ps_c", [128, 512], F32)
    ps_f = em.psum("a2_ps_f", [128, 512], F32)
    ps_m = em.psum("a2_ps_m", [128, 512], F32)
    ps_t = em.psum("a2_ps_t", [128, 1024], BF16)
    ps_ft = em.psum("a2_ps_ft", [128, 512], F32)

    ld_m = em.dma("sync", masks[:, :], masks_in, LD)
    ld_nw = em.dma("sync", nwrow[0:1, :], nw_in, LD)
    for _eng in ("tensor", "vector", "scalar"):
        em.wait(_eng, LD, ld_nw)
        em.wait(_eng, "a2_ld0")
    em.op("vector", lambda e: e.memset(onesrow[0:1, :], 1.0))
    dv0 = em.op("vector", lambda e: e.memset(Sst[:, :], 0.0), inc=DVE)
    lo01 = masks[:, 0:128]
    up01 = masks[:, 128:256]
    idf = masks[:, 256:384]
    em.wait("tensor", DVE, dv0)
    pnw = em.op("tensor", lambda e: e.matmul(
        ps_f[:, 0:128], onesrow[0:1, :], nwrow[0:1, :], start=True, stop=True),
        inc=PE)
    em.wait("vector", PE, pnw)
    em.op("vector", lambda e: e.tensor_copy(normw[:, :], ps_f[:, 0:128]), inc=DVE)

    last_gst = 0
    zpar = 0
    zld_c = {0: 0, 1: 0}

    def wboth(eng):
        em.wait(eng, DVE)
        em.wait(eng, ACTS)

    for b in range(A2_LIM["b"]):
        em.wait("sync", PE)
        qlds = 0
        for k in range(2):
            em.dma("sync", qTb[k][:, :], qT_in[k][:, b * 2048:(b + 1) * 2048], QLD)
            qlds = em.dma("sync", kTb[k][:, :], kT_in[k][:, b * 2048:(b + 1) * 2048],
                          QLD)
        for v in range(4):
            qlds = em.dma("sync", vTb[v][:, :], vT_in[v][:, b * 2048:(b + 1) * 2048],
                          QLD)
        em.wait("tensor", QLD, qlds)
        em.wait("vector", PE)
        em.wait("vector", GST, last_gst)
        em.wait("scalar", GST, last_gst)
        dv0 = em.op("vector", lambda e: e.memset(Sst[:, :], 0.0), inc=DVE)
        em.wait("tensor", DVE, dv0)
        for c in range(A2_LIM["c"]):
            par = c % 2
            csl = slice(c * 128, (c + 1) * 128)
            em.wait("sync", PE)
            gld = em.dma("sync", gch[0:4, 0:128],
                         g_in[0:4, b * 2048 + c * 128: b * 2048 + (c + 1) * 128], GLD)
            gld = em.dma("sync", gch[0:4, 128:256],
                         b_in[0:4, b * 2048 + c * 128: b * 2048 + (c + 1) * 128], GLD)
            em.wait("tensor", GLD, gld)
            wboth("tensor")
            em.op("tensor", lambda e: e.transpose(ps_ft[:, 0:4], gch[0:4, 0:128],
                                                  idf[0:4, 0:4]))
            p1 = em.op("tensor", lambda e: e.transpose(ps_ft[:, 4:8],
                                                       gch[0:4, 128:256],
                                                       idf[0:4, 0:4]), inc=PE)
            em.wait("vector", PE, p1)
            em.op("vector", lambda e: e.tensor_copy(grow[:, 0:4], ps_ft[:, 0:4]))
            dvg = em.op("vector", lambda e: e.tensor_copy(bcolT[:, :], ps_ft[:, 4:8]),
                        inc=DVE)
            em.wait("tensor", DVE, dvg)
            p2 = em.op("tensor", lambda e: e.matmul(
                ps_ft[:, 256:260], up01, grow[:, 0:4], start=True, stop=True), inc=PE)
            em.wait("vector", PE, p2)
            dvG = em.op("vector", lambda e: e.tensor_copy(G_colT[:, :],
                                                          ps_ft[:, 256:260]), inc=DVE)
            em.wait("scalar", DVE, dvG)
            acG = em.op("scalar", lambda e: e.activation(expGc[:, :], G_colT[:, :],
                                                         AF.Exp), inc=ACTS)
            em.wait("vector", ACTS, acG)
            em.op("vector", lambda e: e.tensor_tensor(bgcol[:, :], bcolT[:, :],
                                                      expGc[:, :], OP.mult), inc=DVE)
            for kp in range(A2_LIM["kp"]):
                kT_c = kTb[kp][:, csl]
                qT_c = qTb[kp][:, csl]
                wboth("tensor")
                em.op("tensor", lambda e, kT_c=kT_c: e.matmul(
                    ps_c[:, 0:128], kT_c, kT_c, start=True, stop=True))
                em.op("tensor", lambda e, kT_c=kT_c, qT_c=qT_c: e.matmul(
                    ps_c[:, 128:256], kT_c, qT_c, start=True, stop=True))
                pkt = em.op("tensor", lambda e, kT_c=kT_c: e.transpose(
                    ps_t[:, 0:128], kT_c, ident[:, :]), inc=PE)
                em.wait("vector", PE, pkt)
                em.op("vector", lambda e: e.tensor_tensor(
                    KKtM[:, :], ps_c[:, 0:128], lo01, OP.mult))
                em.op("vector", lambda e: e.tensor_tensor(
                    KQtM[:, :], ps_c[:, 128:256], up01, OP.mult))
                dvk = em.op("vector", lambda e: e.tensor_copy(K_nat[:, :],
                                                              ps_t[:, 0:128]), inc=DVE)
                for vi in range(A2_LIM["vi"]):
                    vl = 2 * kp + vi
                    Scur = Sst[:, (vl * 2 + par) * 128:(vl * 2 + par + 1) * 128]
                    Snew = Sst[:, (vl * 2 + 1 - par) * 128:(vl * 2 + 2 - par) * 128]
                    zp = zpar % 2
                    zpar += 1
                    em.wait("sync", ACTS)
                    zld_c[zp] = em.dma(
                        "sync", zl[:, zp * 128:(zp + 1) * 128],
                        z_in[b * 2048 + c * 128: b * 2048 + (c + 1) * 128,
                             vl * 128:(vl + 1) * 128], f"a2_zl{zp}")
                    em.op("vector", lambda e, vl=vl: e.tensor_scalar(
                        scal[:, 0:1], G_colT[:, vl:vl + 1], -1.0, None, OP.mult))
                    dvs = em.op("vector", lambda e, vl=vl: e.tensor_scalar(
                        scal[:, 1:2], bcolT[:, vl:vl + 1], -1.0, None, OP.mult),
                        inc=DVE)
                    if A2_LIM["stage"] < 1:
                        continue
                    wboth("tensor")
                    pgr = em.op("tensor", lambda e, vl=vl: e.transpose(
                        ps_ft[0:1, 384:512], G_colT[:, vl:vl + 1], idf), inc=PE)
                    em.wait("scalar", PE, pgr)
                    em.wait("scalar", DVE, dvs)
                    acr = em.op("scalar", lambda e: e.copy(
                        grow[0:1, 128:256], ps_ft[0:1, 384:512]), inc=ACTS)
                    em.wait("vector", ACTS, acr)
                    dvr = em.op("vector", lambda e: e.tensor_scalar(
                        grow[0:1, 256:384], grow[0:1, 128:256], -1.0, None, OP.mult),
                        inc=DVE)
                    ace = em.op("scalar", lambda e: e.activation(
                        grow[0:1, 0:128], grow[0:1, 128:256], AF.Exp), inc=ACTS)
                    if A2_LIM["stage"] < 2:
                        continue
                    em.wait("tensor", DVE, dvr)
                    em.wait("tensor", ACTS, ace)
                    em.op("tensor", lambda e: e.matmul(
                        ps_f[:, 0:128], onesrow[0:1, :], grow[0:1, 256:384],
                        start=True, stop=True))
                    em.op("tensor", lambda e: e.matmul(
                        ps_f[:, 128:256], onesrow[0:1, :], grow[0:1, 128:256],
                        start=True, stop=True))
                    pou = em.op("tensor", lambda e: e.matmul(
                        ps_f[:, 256:384], onesrow[0:1, :], grow[0:1, 0:128],
                        start=True, stop=True), inc=PE)
                    em.wait("vector", PE, pou)
                    em.op("vector", lambda e, vl=vl: e.tensor_scalar(
                        gm[:, :], ps_f[:, 0:128], G_colT[:, vl:vl + 1], 0.0,
                        OP.add, OP.min))
                    em.op("vector", lambda e: e.tensor_scalar(
                        decui[:, :], ps_f[:, 128:256], scal[:, 0:1], 0.0,
                        OP.add, OP.min))
                    dvq = em.op("vector", lambda e, qT_c=qT_c: e.tensor_tensor(
                        qtl[:, :], qT_c, ps_f[:, 256:384], OP.mult), inc=DVE)
                    if A2_LIM["stage"] < 3:
                        continue
                    em.wait("scalar", DVE, dvq)
                    em.op("scalar", lambda e: e.activation(dec[:, :], gm[:, :], AF.Exp))
                    acd = em.op("scalar", lambda e: e.activation(
                        decu[:, :], decui[:, :], AF.Exp), inc=ACTS)
                    em.wait("vector", ACTS, acd)
                    if A2_LIM.get("s4", 5) >= 1:
                        em.op("vector", lambda e: e.scalar_tensor_tensor(
                            bn[0][:, :], dec[:, :], scal[:, 1:2], KKtM[:, :],
                            OP.mult, OP.mult))
                    if A2_LIM.get("s4", 5) >= 2:
                        em.op("vector", lambda e: e.tensor_tensor(
                            DTt[:, :], KQtM[:, :], decu[:, :], OP.mult))
                    if A2_LIM.get("s4", 5) >= 3:
                        em.op("vector", lambda e: e.tensor_scalar(
                            Kbr[:, :], K_nat[:, :], decu[:, 127:128], None, OP.mult))
                    if A2_LIM.get("s4", 5) >= 4:
                        em.op("vector", lambda e, vl=vl: e.tensor_tensor(
                            scal[:, 2:3], decu[:, 127:128], expGc[:, vl:vl + 1],
                            OP.mult))
                    dvb = em.op("vector", lambda e, vl=vl: e.tensor_scalar(
                        WK[:, :], K_nat[:, :], bgcol[:, vl:vl + 1], None, OP.mult),
                        inc=DVE)
                    if A2_LIM["stage"] < 4:
                        continue
                    em.wait("tensor", DVE, dvb)
                    em.wait("tensor", ACTS)
                    pbt = em.op("tensor", lambda e: e.transpose(
                        ps_t[:, 128:256], bn[0][:, :], ident[:, :]), inc=PE)
                    em.wait("vector", PE, pbt)
                    em.wait("scalar", PE, pbt)
                    em.op("vector", lambda e: e.tensor_copy(bt[0][:, :],
                                                            ps_t[:, 128:256]))
                    em.op("vector", lambda e: e.tensor_copy(pt[0][:, :],
                                                            ps_t[:, 128:256]))
                    em.op("vector", lambda e: e.tensor_tensor(
                        pt[0][:, :], pt[0][:, :], ident[:, :], OP.add), inc=DVE)
                    if A2_LIM["stage"] < 5:
                        continue
                    cur = 0
                    for kdb in range(1, A2_LIM["db"]):
                        wboth("tensor")
                        if kdb < 6:
                            em.op("tensor", lambda e, cur=cur: e.matmul(
                                ps_m[:, 0:128], bt[cur][:, :], bn[cur][:, :],
                                start=True, stop=True))
                            psq = em.op("tensor", lambda e, cur=cur: e.matmul(
                                ps_m[:, 128:256], bn[cur][:, :], bt[cur][:, :],
                                start=True, stop=True), inc=PE)
                        else:
                            psq = em.op("tensor", lambda e, cur=cur: e.matmul(
                                ps_m[:, 0:128], bt[cur][:, :], bn[cur][:, :],
                                start=True, stop=True), inc=PE)
                        em.wait("scalar", PE, psq)
                        acq = em.op("scalar", lambda e, cur=cur: e.copy(
                            bn[1 - cur][:, :], ps_m[:, 0:128]), inc=ACTS)
                        if kdb < 6:
                            acq = em.op("scalar", lambda e, cur=cur: e.copy(
                                bt[1 - cur][:, :], ps_m[:, 128:256]), inc=ACTS)
                        em.wait("tensor", ACTS, acq)
                        em.wait("tensor", DVE)
                        pup = em.op("tensor", lambda e, cur=cur: (
                            e.matmul(ps_m[:, 256:384], bn[1 - cur][:, :],
                                     pt[cur][:, :], start=True, stop=False),
                            e.matmul(ps_m[:, 256:384], ident[:, :], pt[cur][:, :],
                                     start=False, stop=True))[-1], inc=PE)
                        em.wait("vector", PE, pup)
                        em.op("vector", lambda e, cur=cur: e.tensor_copy(
                            pt[1 - cur][:, :], ps_m[:, 256:384]), inc=DVE)
                        cur = 1 - cur
                    if A2_LIM["stage"] < 6:
                        continue
                    wboth("tensor")
                    pw = em.op("tensor", lambda e, cur=cur: e.matmul(
                        ps_m[:, 0:128], WK[:, :], pt[cur][:, :],
                        start=True, stop=True), inc=PE)
                    em.wait("vector", PE, pw)
                    em.op("vector", lambda e: e.tensor_scalar(
                        WTn[:, :], ps_m[:, 0:128], -1.0, None, OP.mult), inc=DVE)
                    if A2_LIM["stage"] < 6:
                        continue
                    wboth("tensor")
                    pv = em.op("tensor", lambda e, vl=vl, csl=csl: e.transpose(
                        ps_t[:, 256:384], vTb[vl][:, csl], ident[:, :]), inc=PE)
                    em.wait("vector", PE, pv)
                    em.op("vector", lambda e, vl=vl: e.tensor_scalar(
                        Urhs[:, :], ps_t[:, 256:384], bcolT[:, vl:vl + 1],
                        None, OP.mult), inc=DVE)
                    if A2_LIM["stage"] < 7:
                        continue
                    wboth("tensor")
                    pr = em.op("tensor", lambda e, cur=cur, Scur=Scur: (
                        e.matmul(ps_m[:, 128:256], pt[cur][:, :], Urhs[:, :],
                                 start=True, stop=False),
                        e.matmul(ps_m[:, 128:256], WTn[:, :], Scur,
                                 start=False, stop=True))[-1], inc=PE)
                    em.wait("vector", PE, pr)
                    em.op("vector", lambda e: e.tensor_copy(Rbf[:, :],
                                                            ps_m[:, 128:256]), inc=DVE)
                    if A2_LIM["stage"] < 8:
                        continue
                    wboth("tensor")
                    po = em.op("tensor", lambda e, Scur=Scur: (
                        e.matmul(ps_m[:, 256:384], qtl[:, :], Scur,
                                 start=True, stop=False),
                        e.matmul(ps_m[:, 256:384], DTt[:, :], Rbf[:, :],
                                 start=False, stop=True))[-1], inc=PE)
                    if A2_LIM["stage"] < 8:
                        continue
                    em.wait("scalar", PE, po)
                    em.wait("scalar", f"a2_zl{zp}", zld_c[zp])
                    em.op("scalar", lambda e: e.activation(
                        sqj[:, :], ps_m[:, 256:384], AF.Square,
                        accum_out=scal[:, 3:4]))
                    em.op("scalar", lambda e: e.activation(
                        scal[:, 4:5], scal[:, 3:4], AF.Ln, scale=1.0 / DV,
                        bias=em.cv(1e-6)))
                    em.op("scalar", lambda e: e.activation(
                        scal[:, 4:5], scal[:, 4:5], AF.Exp, scale=-0.5))
                    acz = em.op("scalar", lambda e, zp=zp: e.activation(
                        silz[:, :], zl[:, zp * 128:(zp + 1) * 128], AF.Silu),
                        inc=ACTS)
                    em.wait("vector", ACTS, acz)
                    em.wait("vector", PE, po)
                    em.op("vector", lambda e: e.tensor_scalar(
                        gt1[:, :], ps_m[:, 256:384], scal[:, 4:5], None, OP.mult))
                    em.op("vector", lambda e: e.tensor_tensor(
                        gt1[:, :], gt1[:, :], normw[:, :], OP.mult))
                    dvga = em.op("vector", lambda e: e.tensor_tensor(
                        qtl[:, :], gt1[:, :], silz[:, :], OP.mult), inc=DVE)
                    em.wait("tensor", DVE, dvga)
                    pga = em.op("tensor", lambda e: e.transpose(
                        ps_t[:, 384:512], qtl[:, :], ident[:, :]), inc=PE)
                    em.wait("scalar", PE, pga)
                    em.op("scalar", lambda e, vl=vl, csl=csl: e.copy(
                        gTo[vl][:, csl], ps_t[:, 384:512]), inc=ACTS)
                    if A2_LIM["stage"] < 9:
                        continue
                    wboth("tensor")
                    pkr = em.op("tensor", lambda e: e.matmul(
                        ps_c[:, 256:384], Kbr[:, :], Rbf[:, :],
                        start=True, stop=True), inc=PE)
                    em.wait("vector", PE, pkr)
                    em.op("vector", lambda e, Scur=Scur, Snew=Snew: e.scalar_tensor_tensor(
                        Snew, Scur, scal[:, 2:3], ps_c[:, 256:384],
                        OP.mult, OP.add), inc=DVE)
        dvall = em.cnt[ACTS]
        em.wait("gpsimd", ACTS, dvall)
        em.wait("gpsimd", DVE)
        for v in range(4):
            last_gst = em.dma("gpsimd", gT_out[v * 128:(v + 1) * 128,
                                               b * 2048:(b + 1) * 2048],
                              gTo[v][:, :], GST)
    em.wait("gpsimd", GST, last_gst)


def build_stageA2():
    nc = bass.Bass("TRN2", target_bir_lowering=False)
    em = Em(nc)
    for k in range(2):
        em.dram_t(f"A2_qT{k}", [128, TOK], BF16, kind="ExternalInput")
        em.dram_t(f"A2_kT{k}", [128, TOK], BF16, kind="ExternalInput")
    for v in range(4):
        em.dram_t(f"A2_vT{v}", [128, TOK], BF16, kind="ExternalInput")
    em.dram_t("A2_z", [TOK, 512], BF16, kind="ExternalInput")
    em.dram_t("A2_g", [4, TOK], F32, kind="ExternalInput")
    em.dram_t("A2_b", [4, TOK], F32, kind="ExternalInput")
    em.dram_t("A2_normw", [1, 128], F32, kind="ExternalInput")
    em.dram_t("A2_masks", [128, 384], F32, kind="ExternalInput")
    ident_in = em.dram_t("ident_in", [128, 128], BF16, kind="ExternalInput").ap()
    constv_in = em.dram_t("constv_in", [128, 32], F32, kind="ExternalInput").ap()
    em.dram_t("a2_gT", [512, TOK], BF16, kind="ExternalOutput")
    with em.ctx:
        em.sbuf("ident_bf", [128, 128], BF16)
        em.sbuf("constv", [128, 32], F32)
        em.dma("sync", em.sb["ident_bf"][:, :], ident_in, "a2_ld0")
        em.dma("sync", em.sb["constv"][:, :], constv_in, "a2_ld0")
        emit_stageA2(em)
        em.emit()
    return em


def stageA2_inputs(em, a1_outs, inputs, core_id):
    c = core_id
    qT = np.concatenate([o["a1_qT"] for o in a1_outs], axis=1)
    kT = np.concatenate([o["a1_kT"] for o in a1_outs], axis=1)
    vT = np.concatenate([o["a1_vT"] for o in a1_outs], axis=1)
    zf = np.concatenate([o["a1_z"] for o in a1_outs], axis=0)
    gb = np.concatenate([o["a1_gb"] for o in a1_outs], axis=1)
    lo01 = np.tril(np.ones((128, 128), np.float32), -1)
    up01 = np.triu(np.ones((128, 128), np.float32), 0)
    idf = np.eye(128, dtype=np.float32)
    d = {
        "A2_z": np.ascontiguousarray(zf[:, 512 * c:512 * (c + 1)]),
        "A2_g": np.ascontiguousarray(gb[32 + 4 * c:32 + 4 * (c + 1), :]),
        "A2_b": np.ascontiguousarray(gb[4 * c:4 * (c + 1), :]),
        "A2_normw": np.asarray(inputs["L0_attn_norm_w"], np.float32).reshape(1, 128),
        "A2_masks": np.concatenate([lo01, up01, idf], axis=1),
        "ident_in": np.eye(128, dtype=ml_dtypes.bfloat16),
        "constv_in": consts_array(em),
    }
    for k in range(2):
        kh = 2 * c + k
        d[f"A2_qT{k}"] = np.ascontiguousarray(qT[kh * 128:(kh + 1) * 128])
        d[f"A2_kT{k}"] = np.ascontiguousarray(kT[kh * 128:(kh + 1) * 128])
    for v in range(4):
        vh = 4 * c + v
        d[f"A2_vT{v}"] = np.ascontiguousarray(vT[vh * 128:(vh + 1) * 128])
    return d


def run_stageA2(a1_outs, inputs, sim=False):
    em = build_stageA2()
    in_maps = [stageA2_inputs(em, a1_outs, inputs, c) for c in range(NCORES)]
    if sim:
        import concourse.bass_interp as bass_interp
        msim = bass_interp.MultiCoreSim(em.nc, NCORES)
        for c in range(NCORES):
            for k, v in in_maps[c].items():
                msim.cores[c].tensor(k)[:] = v
        msim.simulate()
        return [np.asarray(msim.cores[c].mem_tensor("a2_gT")) for c in range(NCORES)]
    res = run_bass_kernel_spmd(em.nc, in_maps, core_ids=list(range(NCORES)))
    return [np.asarray(res.results[c]["a2_gT"]) for c in range(NCORES)]




def emit_stageA3(em):
    gT_in = em.dram["A3_gT"].ap()
    wout_in = em.dram["A3_wout"].ap()
    x_in = em.dram["A3_x"].ap()
    x_out = em.dram["a3_x1"].ap()
    wout_loc = em.dram_t("a3_wloc", [512, 2048], BF16)
    wout_bf = em.dram_t("a3_wbf", [4096, 2048], BF16)

    SWW, CC, PE, DVE, ACTS = "a3_sww", "a3_cc", "a3_pe", "E_dve", "E_act"
    GLD, XRS, HST = "a3_gld", "a3_xrs", "a3_hst"

    for _eng in ("tensor", "vector", "scalar"):
        em.wait(_eng, "a3_ld")
    em.dma("gpsimd", wout_loc.ap(), wout_in, SWW)
    em.wait("gpsimd", SWW)
    em.collective("AllGather", OP.bypass, [list(range(8))],
                  [wout_loc.ap().opt()], [wout_bf.ap().opt()], CC)
    cc_w = em.cnt[CC]

    gt_sb = em.sbuf("a3_gt", [128, 32 * 128], BF16)
    wo_sb = em.sbuf("a3_wo", [128, 2 * 32 * 512], BF16)
    yacc = em.sbuf("a3_yacc", [128, 2048], F32)
    junk_bf = em.sbuf("a3_junk", [128, 2048], BF16)
    stat = em.sbuf("a3_stat", [128, 8], F32)
    xres = em.sbuf("a3_xres", [128, 2048], F32)
    ps_y = [em.psum("a3_ps_y0", [128, 512], F32),
            em.psum("a3_ps_y1", [128, 512], F32)]

    em.wait("sync", CC, cc_w)
    wo_pe = {0: 0, 1: 0}
    last_hst = 0
    last_stt = 0
    gld = 0
    for tt in range(8):
        em.wait("sync", PE)
        gld = em.dma("sync", gt_sb[:, :].rearrange("p (t o) -> p t o", o=128),
                     _tiled(gT_in[:, tt * 128:(tt + 1) * 128]), GLD)
        em.wait("sync", DVE, last_stt)
        xl = em.dma("sync", xres[:, :], x_in[tt * 128:(tt + 1) * 128, :], XRS)
        em.wait("tensor", GLD, gld)
        em.wait("vector", HST, last_hst)
        for ob in range(4):
            buf = ob % 2
            em.wait("sync", PE, wo_pe[buf])
            wl = em.dma(
                "sync", wo_sb[:, buf * 16384:(buf + 1) * 16384].rearrange(
                    "p (t o) -> p t o", o=512),
                _tiled(wout_bf.ap()[:, ob * 512:(ob + 1) * 512]), f"a3_wl{buf}")
            em.wait("tensor", f"a3_wl{buf}", wl)
            par = ob % 2
            em.wait("tensor", DVE)
            pe_c = 0
            for d in range(32):
                pe_c = em.op("tensor", lambda e, d=d, buf=buf, par=par: e.matmul(
                    ps_y[par][:, :], gt_sb[:, d * 128:(d + 1) * 128],
                    wo_sb[:, buf * 16384 + d * 512:buf * 16384 + (d + 1) * 512],
                    start=(d == 0), stop=(d == 31)), inc=PE if d == 31 else None)
            wo_pe[buf] = pe_c
            em.wait("vector", PE, pe_c)
            em.op("vector", lambda e, ob=ob, par=par: e.tensor_copy(
                yacc[:, ob * 512:(ob + 1) * 512], ps_y[par][:, :]), inc=DVE)
        em.wait("vector", XRS, xl)
        last_stt = em.op("vector", lambda e: e.tensor_tensor(
            yacc[:, :], yacc[:, :], xres[:, :], OP.add), inc=DVE)
        em.wait("scalar", DVE, last_stt)
        vln = emit_ln(em, yacc, junk_bf, stat)
        em.wait("gpsimd", DVE, vln)
        last_hst = em.dma("gpsimd", x_out[tt * 128:(tt + 1) * 128, :], yacc[:, :], HST)
    em.wait("gpsimd", HST)


def build_stageA3():
    nc = bass.Bass("TRN2", target_bir_lowering=False)
    em = Em(nc)
    em.dram_t("A3_gT", [4096, TLOC], BF16, kind="ExternalInput")
    em.dram_t("A3_wout", [512, 2048], F32, kind="ExternalInput")
    em.dram_t("A3_x", [TLOC, 2048], F32, kind="ExternalInput")
    ident_in = em.dram_t("ident_in", [128, 128], BF16, kind="ExternalInput").ap()
    constv_in = em.dram_t("constv_in", [128, 32], F32, kind="ExternalInput").ap()
    em.dram_t("a3_x1", [TLOC, 2048], F32, kind="ExternalOutput")
    with em.ctx:
        em.sbuf("ident_bf", [128, 128], BF16)
        em.sbuf("constv", [128, 32], F32)
        em.dma("sync", em.sb["ident_bf"][:, :], ident_in, "a3_ld")
        em.dma("sync", em.sb["constv"][:, :], constv_in, "a3_ld")
        emit_stageA3(em)
        em.emit()
    return em


def stageA3_inputs(em, a2_gT, inputs, core_id):
    c = core_id
    gT_full = np.concatenate(a2_gT, axis=0)
    x = np.asarray(inputs["L0_attn_x"], np.float32).reshape(TOK, 2048)
    return {
        "A3_gT": np.ascontiguousarray(gT_full[:, c * TLOC:(c + 1) * TLOC]),
        "A3_wout": np.ascontiguousarray(
            np.asarray(inputs["L0_attn_W_out"], np.float32)[512 * c:512 * (c + 1)]),
        "A3_x": np.ascontiguousarray(x[c * TLOC:(c + 1) * TLOC]),
        "ident_in": np.eye(128, dtype=ml_dtypes.bfloat16),
        "constv_in": consts_array(em),
    }


def run_stageA3(a2_gT, inputs, sim=False):
    em = build_stageA3()
    in_maps = [stageA3_inputs(em, a2_gT, inputs, c) for c in range(NCORES)]
    if sim:
        import concourse.bass_interp as bass_interp
        msim = bass_interp.MultiCoreSim(em.nc, NCORES)
        for c in range(NCORES):
            for k, v in in_maps[c].items():
                msim.cores[c].tensor(k)[:] = v
        msim.simulate()
        return np.concatenate(
            [np.asarray(msim.cores[c].mem_tensor("a3_x1")) for c in range(NCORES)],
            axis=0)
    res = run_bass_kernel_spmd(em.nc, in_maps, core_ids=list(range(NCORES)))
    return np.concatenate([np.asarray(res.results[c]["a3_x1"]) for c in range(NCORES)],
                          axis=0)



def kernel(**inputs):
    a1 = run_stageA1(inputs)
    gT = run_stageA2(a1, inputs)
    x1 = run_stageA3(gT, inputs)
    x2 = run_stageB(x1, inputs)
    x3 = run_stageC(x2, inputs)
    x4 = run_stageD(x3, inputs)
    return x4.reshape(B, S, 2048).astype(np.float32)


# version 04
# speedup vs baseline: 1.0025x; 1.0025x over previous
import math
import numpy as np
import ml_dtypes
from contextlib import ExitStack

import os
os.environ.setdefault("TRN_ALLOW_ICI_ALL_TO_ALL", "1")
import concourse.bass as bass
import concourse.mybir as mybir
from concourse.bass_utils import run_bass_kernel_spmd

F32 = mybir.dt.float32
BF16 = mybir.dt.bfloat16
F16 = mybir.dt.float16
I32 = mybir.dt.int32
AF = mybir.ActivationFunctionType
OP = mybir.AluOpType

NCORES = 8
B, S, D = 4, 2048, 2048
TOK = B * S
TLOC = TOK // NCORES
LN2 = math.log(2.0)

NK, NV, DK, DV = 16, 32, 128, 128
KEY_DIM, VALUE_DIM = NK * DK, NV * DV
CONV_DIM = 2 * KEY_DIM + VALUE_DIM
FF_I = 5632
NH, HD = 16, 128
MOE_E, MOE_I = 16, 1408
SH_I = 2816


class Em:
    ENGS = ("sync", "scalar", "vector", "tensor", "gpsimd")

    def __init__(self, nc):
        self.nc = nc
        self.ctx = ExitStack()
        self.script = {e: [] for e in self.ENGS}
        self.sems = {}
        self.cnt = {}
        self.sb = {}
        self.dram = {}

    def sem(self, name):
        if name not in self.sems:
            self.sems[name] = self.ctx.enter_context(self.nc.semaphore(name))
            self.cnt[name] = 0
        return self.sems[name]

    def sbuf(self, name, shape, dtype, stack=None, **kw):
        t = (stack or self.ctx).enter_context(self.nc.sbuf_tensor(name, shape, dtype, **kw))
        self.sb[name] = t
        return t

    def psum(self, name, shape, dtype, stack=None):
        t = (stack or self.ctx).enter_context(self.nc.psum_tensor(name, shape, dtype))
        return t

    def dram_t(self, name, shape, dtype, **kw):
        t = self.nc.dram_tensor(name, shape, dtype, **kw)
        self.dram[name] = t
        return t

    def cv(self, val, rows=128, base=0):
        val = float(val)
        if not hasattr(self, "const_vals"):
            self.const_vals = []
        if val not in self.const_vals:
            self.const_vals.append(val)
        k = self.const_vals.index(val)
        return self.sb["constv"][base:base + rows, k:k + 1]

    def wait(self, eng, sem_name, val=None):
        h = self.sem(sem_name)
        v = self.cnt[sem_name] if val is None else val
        self.script[eng].append(lambda e, h=h, v=v: e.wait_ge(h, v))
        return v

    AUTO = {"vector": "E_dve", "scalar": "E_act", "gpsimd": "E_pool"}

    def op(self, eng, fn, inc=None, inc_by=1):
        if eng in self.AUTO:
            own = self.AUTO[eng]
            h = self.sem(own)
            prev = self.cnt[own]
            self.cnt[own] = prev + 1
            if inc is not None and inc != own:
                h2 = self.sem(inc)
                self.cnt[inc] += inc_by

                def step(e, fn=fn, h=h, prev=prev, h2=h2, k=inc_by):
                    e.wait_ge(h, prev)
                    fn(e).then_inc(h, 1).then_inc(h2, k)
            else:
                def step(e, fn=fn, h=h, prev=prev):
                    e.wait_ge(h, prev)
                    fn(e).then_inc(h, 1)
            self.script[eng].append(step)
            return self.cnt[own]
        if inc is not None:
            h = self.sem(inc)
            self.cnt[inc] += inc_by
            self.script[eng].append(
                lambda e, fn=fn, h=h, k=inc_by: fn(e).then_inc(h, k))
            return self.cnt[inc]
        self.script[eng].append(fn)
        return None

    def dma(self, eng, out, in_, inc, **kw):
        if eng in self.AUTO:
            own = self.AUTO[eng]
            self.wait(eng, own)
        h = self.sem(inc)
        self.cnt[inc] += 16
        self.script[eng].append(
            lambda e, o=out, i=in_, kw=kw, h=h: e.dma_start(out=o, in_=i, **kw).then_inc(h, 16))
        return self.cnt[inc]

    def collective(self, kind, op, groups, ins, outs, inc):
        self.wait("gpsimd", self.AUTO["gpsimd"])
        h = self.sem(inc)
        self.cnt[inc] += 1
        self.script["gpsimd"].append(
            lambda e, h=h: e.collective_compute(
                kind, op, replica_groups=groups, ins=ins, outs=outs).then_inc(h, 1))
        return self.cnt[inc]

    def stage_barrier(self):
        names = list(self.sems.keys())
        for eng in self.ENGS:
            for s in names:
                self.wait(eng, s)

    def emit(self):
        block = self.ctx.enter_context(self.nc.Block())
        for eng in self.ENGS:
            steps = self.script[eng]
            if not steps:
                continue

            def body(e, steps=steps):
                for f in steps:
                    f(e)
            getattr(block, eng)(body)



def _tiled(dram_ap, p=128):
    return dram_ap.rearrange("(t p) w -> p t w", p=p)


def emit_fp4_tile(em, eng, pk_ap, dq, W, esem, act_sem):
    n = 8 * W
    nb, nf, ge8, p, m_i, m, pm = (dq[k] for k in ("nb", "nf", "ge8", "p", "m_i", "m", "pm"))
    pw, base, z, msk, vp, sgn, val = (dq[k] for k in ("pw", "base", "z", "msk", "vp",
                                                      "sgn", "val"))
    for j in range(8):
        em.op(eng, lambda e, j=j: e.tensor_scalar(
            nb[:, j:n:8], pk_ap, 4 * j, 15, OP.logical_shift_right, OP.bitwise_and))
    em.op(eng, lambda e: e.tensor_copy(nf[:, :n], nb[:, :n]))
    em.op(eng, lambda e: e.tensor_scalar(m_i[:, :n], nb[:, :n], 1, None, OP.bitwise_and))
    em.op(eng, lambda e: e.tensor_copy(m[:, :n], m_i[:, :n]))
    em.op(eng, lambda e: e.tensor_scalar(ge8[:, :n], nf[:, :n], 8.0, None, OP.is_ge))
    em.op(eng, lambda e: e.scalar_tensor_tensor(
        p[:, :n], ge8[:, :n], -8.0, nf[:, :n], OP.mult, OP.add))
    v = em.op(eng, lambda e: e.tensor_tensor(pm[:, :n], p[:, :n], m[:, :n],
                                             OP.subtract), inc=esem)
    em.wait("scalar", esem, v)
    va = em.op("scalar", lambda e: e.activation(
        pw[:, :n], pm[:, :n], AF.Exp, scale=LN2 / 2, bias=em.cv(-2 * LN2)), inc=act_sem)
    em.wait(eng, act_sem, va)
    em.op(eng, lambda e: e.scalar_tensor_tensor(
        base[:, :n], m[:, :n], 2.0, pw[:, :n], OP.add, OP.mult))
    em.op(eng, lambda e: e.tensor_scalar(z[:, :n], m[:, :n], 0.5, None, OP.mult))
    em.op(eng, lambda e: e.tensor_scalar(msk[:, :n], p[:, :n], 2.0, None, OP.is_lt))
    em.op(eng, lambda e: e.tensor_tensor(z[:, :n], z[:, :n], base[:, :n], OP.subtract))
    em.op(eng, lambda e: e.tensor_tensor(z[:, :n], msk[:, :n], z[:, :n], OP.mult))
    em.op(eng, lambda e: e.tensor_tensor(vp[:, :n], base[:, :n], z[:, :n], OP.add))
    em.op(eng, lambda e: e.tensor_scalar(sgn[:, :n], ge8[:, :n], -2.0, 1.0,
                                         OP.mult, OP.add))
    em.op(eng, lambda e: e.tensor_tensor(val[:, :n], vp[:, :n], sgn[:, :n], OP.mult))


def _mk_dq(em, pref, width, stack=None):
    dq = {"nb": em.sbuf(f"dq_{pref}_nb", [128, width], I32, stack),
          "m_i": em.sbuf(f"dq_{pref}_mi", [128, width], I32, stack)}
    for nm in ("nf", "ge8", "p", "m", "pm", "pw", "base", "z", "msk", "vp", "sgn", "val"):
        dq[nm] = em.sbuf(f"dq_{pref}_{nm}", [128, width], F32, stack)
    dq["bf"] = em.sbuf(f"dq_{pref}_bf", [128, width], BF16, stack)
    return dq


def _dn_ranges(core_id):
    base = 704 * core_id
    g0 = base // 128
    ranges, j = [], 0
    while j < 704:
        g = (base + j) // 128
        hi = min(704, (g + 1) * 128 - base)
        ranges.append((j, hi, g - g0))
        j = hi
    return ranges


def emit_ln(em, xacc, junk_bf, stat, ACTS="E_act", DVE="E_dve"):
    em.op("scalar", lambda e: e.activation(
        junk_bf[:, :], xacc[:, :], AF.Copy, accum_out=stat[:, 3:4]))
    va = em.op("scalar", lambda e: e.activation(
        junk_bf[:, :], xacc[:, :], AF.Square, accum_out=stat[:, 4:5]), inc=ACTS)
    em.wait("vector", ACTS, va)
    em.op("vector", lambda e: e.tensor_scalar(
        stat[:, 5:6], stat[:, 3:4], 1.0 / 2048, None, OP.mult))
    em.op("vector", lambda e: e.tensor_tensor(
        stat[:, 6:7], stat[:, 5:6], stat[:, 5:6], OP.mult))
    vv = em.op("vector", lambda e: e.scalar_tensor_tensor(
        stat[:, 7:8], stat[:, 4:5], 1.0 / 2048, stat[:, 6:7],
        OP.mult, OP.subtract), inc=DVE)
    em.wait("scalar", DVE, vv)
    em.op("scalar", lambda e: e.activation(
        stat[:, 0:1], stat[:, 7:8], AF.Ln, bias=em.cv(1e-5)))
    va = em.op("scalar", lambda e: e.activation(
        stat[:, 1:2], stat[:, 0:1], AF.Exp, scale=-0.5), inc=ACTS)
    em.wait("vector", ACTS, va)
    return em.op("vector", lambda e: e.tensor_scalar(
        xacc[:, :], xacc[:, :], stat[:, 5:6], stat[:, 1:2],
        OP.subtract, OP.mult), inc=DVE)


def emit_stageB(em, core_id, x_in, x_out):
    nc = em.nc
    wguT_loc = em.dram_t("wguT_loc", [256, 2 * FF_I], BF16)
    wguT = em.dram_t("wguT", [2048, 2 * FF_I], BF16)
    wdnT_loc = em.dram_t("wdnT_loc", [704, 2048], BF16)
    wdnT = em.dram_t("wdnT", [FF_I, 2048], BF16)
    xbf_d = em.dram_t("bx_bf", [TLOC, 2048], BF16)

    gup = em.dram["B_gup"].ap()
    gus = em.dram["B_gus"].ap()
    dnp = em.dram["B_dnp"].ap()
    dns = em.dram["B_dns"].ap()
    ident = em.sb["ident_bf"]

    LD, SW, SWX, TRD, CC = "b_ld", "b_sw", "b_swx", "b_trd", "b_cc"
    PE, DVE, ACTS, POOL = "b_pe", "E_dve", "E_act", "E_pool"
    WST0, WST1, WDN, XRS, HST = "b_wst0", "b_wst1", "b_wdn", "b_xrs", "b_hst"

    dstack = ExitStack()
    pk_gu = em.sbuf("pk_gu", [128, 88 * 32], I32, dstack)
    pk_dn = em.sbuf("pk_dn", [128, 16 * 88], I32, dstack)
    scl_gu16 = em.sbuf("scl_gu16", [128, 88 * 2], F16, dstack)
    scl_gu = em.sbuf("scl_gu", [128, 88 * 2], F32, dstack)
    scl_dn = em.sbuf("scl_dn", [128, 16 * 11], F32, dstack)
    dqv = _mk_dq(em, "v", 704, dstack)
    dqg = _mk_dq(em, "g", 704, dstack)
    wguT_sb = em.sbuf("wguT_sb", [128, 2 * 2 * FF_I], BF16, dstack)
    wdnT_sb = em.sbuf("wdnT_sb", [128, 6 * 2048], BF16, dstack)
    tr_ps = [em.psum("tr_ps0", [128, 1024], BF16, dstack),
             em.psum("tr_ps1", [128, 1024], BF16, dstack)]

    em.dma("sync", pk_gu[:, :].rearrange("p (t w) -> p t w", w=32), _tiled(gup), LD)
    em.dma("sync", pk_dn[:, :].rearrange("p (t w) -> p t w", w=88), _tiled(dnp), LD)
    em.dma("sync", scl_gu16[:, :].rearrange("p (t w) -> p t w", w=2), _tiled(gus), LD)
    em.dma("sync", scl_dn[:, :].rearrange("p (t w) -> p t w", w=11), _tiled(dns), LD)
    ld_done = em.cnt[LD]
    em.dma("gpsimd", xbf_d[:, :], x_in[:, :], SWX)
    sw_x = em.cnt[SWX]

    em.wait("vector", LD, ld_done)
    em.op("vector", lambda e: e.tensor_copy(scl_gu[:, :], scl_gu16[:, :]), inc=DVE)

    last_tr = {"v": 0, "g": 0}
    last_ev = {0: 0, 1: 0}
    n_t = 0

    def dq_tile(tile_kind, t, n_t):
        eng = "vector" if True else "gpsimd"
        pref = "v" if eng == "vector" else "g"
        dq = dqv if pref == "v" else dqg
        esem = DVE if eng == "vector" else POOL
        par = n_t % 2
        if n_t < 2:
            em.wait(eng, LD, ld_done)
            if eng == "gpsimd":
                em.wait(eng, DVE, 1)
        else:
            em.wait(eng, PE, last_tr[pref])
        if tile_kind == "gu":
            pk_ap = pk_gu[:, 32 * t:32 * (t + 1)]
            W, nrow = 32, 2
        else:
            pk_ap = pk_dn[:, 88 * t:88 * (t + 1)]
            W, nrow = 88, 6
        emit_fp4_tile(em, eng, pk_ap, dq, W, esem, ACTS)
        if tile_kind == "gu":
            em.op(eng, lambda e: e.tensor_scalar(
                dq["bf"][:, 0:128], dq["val"][:, 0:128],
                scl_gu[:, 2 * t:2 * t + 1], None, OP.mult))
            vd = em.op(eng, lambda e: e.tensor_scalar(
                dq["bf"][:, 128:256], dq["val"][:, 128:256],
                scl_gu[:, 2 * t + 1:2 * t + 2], None, OP.mult), inc=esem)
        else:
            for b in range(11):
                vd = em.op(eng, lambda e, b=b: e.tensor_scalar(
                    dq["bf"][:, 64 * b:64 * (b + 1)], dq["val"][:, 64 * b:64 * (b + 1)],
                    scl_dn[:, 11 * t + b:11 * t + b + 1], None, OP.mult),
                    inc=esem if b == 10 else None)
        em.wait("tensor", esem, vd)
        em.wait("tensor", ACTS, last_ev[par])
        for r in range(nrow):
            w = min(128, 8 * W - r * 128)
            pe_c = em.op("tensor", lambda e, r=r, w=w: e.transpose(
                tr_ps[par][0:w, r * 128:r * 128 + 128],
                dq["bf"][:, r * 128:r * 128 + w], ident[:, :]), inc=PE)
        last_tr[pref] = pe_c
        em.wait("scalar", PE, pe_c)
        for r in range(nrow):
            w = min(128, 8 * W - r * 128)
            if tile_kind == "gu":
                dst = wguT_sb[:, r * 2 * FF_I + 128 * t: r * 2 * FF_I + 128 * (t + 1)]
            else:
                dst = wdnT_sb[0:w, r * 2048 + 128 * t: r * 2048 + 128 * (t + 1)]
            ac = em.op("scalar", lambda e, dst=dst, r=r, w=w: e.copy(
                dst, tr_ps[par][0:w, r * 128:r * 128 + 128]), inc=ACTS)
        last_ev[par] = ac
        return n_t + 1

    for t in range(88):
        n_t = dq_tile("gu", t, n_t)
    for t in range(16):
        n_t = dq_tile("dn", t, n_t)

    ev_all = em.cnt[ACTS]
    em.wait("gpsimd", ACTS, ev_all)
    em.dma("gpsimd", _tiled(wguT_loc.ap(), p=128),
           wguT_sb[:, :].rearrange("p (r o) -> p r o", o=2 * FF_I), SW)
    em.dma("gpsimd", _tiled(wdnT_loc.ap()[0:640, :], p=128),
           wdnT_sb[:, 0:5 * 2048].rearrange("p (r o) -> p r o", o=2048), SW)
    em.dma("gpsimd", wdnT_loc.ap()[640:704, :], wdnT_sb[0:64, 5 * 2048:6 * 2048], SW)
    em.wait("gpsimd", SW)
    em.collective("AllGather", OP.bypass, [list(range(8))],
                  [wguT_loc.ap().opt()], [wguT.ap().opt()], CC)
    em.collective("AllGather", OP.bypass, [list(range(8))],
                  [wdnT_loc.ap().opt()], [wdnT.ap().opt()], CC)
    cc_done = em.cnt[CC]
    dstack.close()

    fstack = ExitStack()
    xT_sb = em.sbuf("bxT_sb", [128, 16 * 512], BF16, fstack, is_transpose_dest=True)
    hT_sb = em.sbuf("bhT_sb", [128, 44 * 512], BF16, fstack)
    w1buf = em.sbuf("bw1buf", [128, 2 * 16 * 512], BF16, fstack)
    wdn_ob = em.sbuf("bwdn_ob", [128, 44 * 256], BF16, fstack)
    silu_sb = em.sbuf("bsilu", [128, 4 * 512], BF16, fstack)
    y_sb = em.sbuf("by_sb", [128, 4 * 2048], F32, fstack)
    junk_bf = em.sbuf("bjunk", [128, 2048], BF16, fstack)
    stat = em.sbuf("bstat", [128, 8], F32, fstack)
    xres = em.sbuf("bxres", [128, 2048], F32, fstack)
    xo_sb = em.sbuf("bxo", [128, 2048], F32, fstack)
    ps_g = em.psum("bps_g", [128, 512], F32, fstack)
    ps_u = em.psum("bps_u", [128, 512], F32, fstack)
    ps_y = [em.psum("bps_y0", [128, 512], F32, fstack),
            em.psum("bps_y1", [128, 512], F32, fstack)]

    em.wait("sync", SWX, sw_x)
    em.wait("sync", CC, cc_done)
    em.wait("tensor", CC, cc_done)

    last_hmul = 0
    last_silu = 0
    yev_done = {0: 0, 1: 0}
    last_stt = 0
    last_hst = 0
    last_vln = 0
    xT_used = 0
    for tb in range(2):
        em.wait("sync", PE, xT_used)
        for i in range(16):
            em.dma("sync", xT_sb[:, i * 512:(i + 1) * 512],
                   xbf_d[tb * 512:(tb + 1) * 512, i * 128:(i + 1) * 128], TRD,
                   transpose=True)
        trd_done = em.cnt[TRD]
        em.wait("tensor", TRD, trd_done)
        w1_ld = {0: 0, 1: 0}
        pe_w1 = {0: 0, 1: 0}
        par = 0
        for og in range(11):
            for half in (0, 1):
                buf = par % 2
                par += 1
                ocol = half * FF_I + og * 512
                em.wait("sync", PE, pe_w1[buf])
                wsem = WST0 if buf == 0 else WST1
                w1_ld[buf] = em.dma(
                    "sync",
                    w1buf[:, buf * 8192:(buf + 1) * 8192].rearrange(
                        "p (t o) -> p t o", o=512),
                    _tiled(wguT.ap()[:, ocol:ocol + 512]), wsem)
                ps = ps_g if half == 0 else ps_u
                em.wait("tensor", wsem, w1_ld[buf])
                for j in range(4):
                    it = og * 4 + j
                    if half == 0:
                        em.wait("tensor", ACTS, last_silu)
                    else:
                        em.wait("tensor", DVE, last_hmul)
                    for d in range(16):
                        pe_c = em.op("tensor", lambda e, d=d, j=j, buf=buf, ps=ps: e.matmul(
                            ps[:, :],
                            w1buf[:, buf * 8192 + d * 512 + j * 128:
                                  buf * 8192 + d * 512 + (j + 1) * 128],
                            xT_sb[:, d * 512:(d + 1) * 512],
                            start=(d == 0), stop=(d == 15)),
                            inc=PE if d == 15 else None)
                    if half == 0:
                        em.wait("scalar", PE, pe_c)
                        em.wait("scalar", DVE, last_hmul)
                        last_silu = em.op("scalar", lambda e, j=j: e.activation(
                            silu_sb[:, j * 512:(j + 1) * 512], ps_g[:, :], AF.Silu),
                            inc=ACTS)
                    else:
                        em.wait("vector", ACTS, last_silu)
                        em.wait("vector", PE, pe_c)
                        last_hmul = em.op("vector", lambda e, it=it, j=j: e.tensor_tensor(
                            hT_sb[:, it * 512:(it + 1) * 512],
                            silu_sb[:, j * 512:(j + 1) * 512],
                            ps_u[:, :], OP.mult), inc=DVE)
                pe_w1[buf] = pe_c
        xT_used = em.cnt[PE]
        hT_ready = em.cnt[DVE]
        wdn_ld = 0
        pe_wdn = 0
        for ob in range(8):
            em.wait("sync", PE, pe_wdn)
            wdn_ld = em.dma(
                "sync", wdn_ob[:, :].rearrange("p (t o) -> p t o", o=256),
                _tiled(wdnT.ap()[:, ob * 256:(ob + 1) * 256]), WDN)
            em.wait("tensor", WDN, wdn_ld)
            for tsub in range(4):
                pidx = (ob * 4 + tsub) % 2
                psy = ps_y[pidx]
                em.wait("tensor", DVE, max(hT_ready, yev_done[pidx]))
                for it in range(44):
                    pe_c = em.op("tensor", lambda e, it=it, tsub=tsub, psy=psy: e.matmul(
                        psy[:, 0:256],
                        hT_sb[:, it * 512 + tsub * 128: it * 512 + (tsub + 1) * 128],
                        wdn_ob[:, it * 256:(it + 1) * 256],
                        start=(it == 0), stop=(it == 43)),
                        inc=PE if it == 43 else None)
                em.wait("vector", PE, pe_c)
                yev_done[pidx] = em.op(
                    "vector", lambda e, tsub=tsub, ob=ob, psy=psy: e.tensor_copy(
                        y_sb[:, tsub * 2048 + ob * 256: tsub * 2048 + (ob + 1) * 256],
                        psy[:, 0:256]), inc=DVE)
            pe_wdn = em.cnt[PE]
        y_ready = em.cnt[DVE]
        for tsub in range(4):
            tt = tb * 4 + tsub
            em.wait("scalar", DVE, y_ready)
            em.wait("scalar", DVE, last_vln)
            em.op("scalar", lambda e, tsub=tsub: e.activation(
                junk_bf[:, :], y_sb[:, tsub * 2048:(tsub + 1) * 2048], AF.Square,
                accum_out=stat[:, 0:1]))
            em.op("scalar", lambda e: e.activation(
                stat[:, 1:2], stat[:, 0:1], AF.Ln, scale=1.0 / 2048, bias=em.cv(1e-6)))
            va = em.op("scalar", lambda e: e.activation(
                stat[:, 2:3], stat[:, 1:2], AF.Exp, scale=-0.5), inc=ACTS)
            em.wait("sync", DVE, last_stt)
            xl = em.dma("sync", xres[:, :], x_in[tt * 128:(tt + 1) * 128, :], XRS)
            em.wait("vector", ACTS, va)
            em.wait("vector", XRS, xl)
            em.wait("vector", HST, last_hst)
            last_stt = em.op("vector", lambda e, tsub=tsub: e.scalar_tensor_tensor(
                xo_sb[:, :], y_sb[:, tsub * 2048:(tsub + 1) * 2048], stat[:, 2:3],
                xres[:, :], OP.mult, OP.add), inc=DVE)
            em.wait("scalar", DVE, last_stt)
            last_vln = emit_ln(em, xo_sb, junk_bf, stat, ACTS, DVE)
            em.wait("gpsimd", DVE, last_vln)
            last_hst = em.dma("gpsimd", x_out[tt * 128:(tt + 1) * 128, :], xo_sb[:, :],
                              HST)
    em.wait("gpsimd", HST)
    fstack.close()





def emit_ternary_tile(em, w_sb, wt_bf, scr, ngroups, gwidth=128):
    absj, st = scr["absj"], scr["st"]
    for g in range(ngroups):
        sl = slice(g * gwidth, (g + 1) * gwidth)
        em.wait("scalar", "E_dve")
        va = em.op("scalar", lambda e, sl=sl: e.activation(
            absj[:, 0:gwidth], w_sb[:, sl], AF.Abs, accum_out=st[:, 0:1]))
        em.wait("vector", "E_act", va)
        em.op("vector", lambda e: e.tensor_scalar(
            st[:, 1:2], st[:, 0:1], 1.0 / gwidth, 1e-8, OP.mult, OP.max))
        em.op("vector", lambda e: e.tensor_scalar(
            st[:, 2:3], st[:, 1:2], 0.5, None, OP.mult))
        em.op("vector", lambda e: e.tensor_scalar(
            st[:, 3:4], st[:, 2:3], -1.0, None, OP.mult))
        em.op("vector", lambda e, sl=sl: e.tensor_scalar(
            absj[:, 0:gwidth], w_sb[:, sl], st[:, 2:3], None, OP.is_gt))
        em.op("vector", lambda e, sl=sl: e.tensor_scalar(
            wt_bf[:, sl], w_sb[:, sl], st[:, 3:4], None, OP.is_lt))
        em.op("vector", lambda e, sl=sl: e.tensor_tensor(
            absj[:, 0:gwidth], absj[:, 0:gwidth], wt_bf[:, sl], OP.subtract))
        em.op("vector", lambda e, sl=sl: e.tensor_scalar(
            wt_bf[:, sl], absj[:, 0:gwidth], st[:, 1:2], None, OP.mult))


def emit_stageC(em, x_in, x_out):
    xbf_d = em.dram_t("cx_bf", [TLOC, 2048], BF16)
    x2all = em.dram_t("cx_all", [TOK, 2048], BF16)
    yT_a2a_in = em.dram_t("cyT_ain", [2048, TLOC], BF16)
    yT_tok = em.dram_t("cyT_tok", [2048, TLOC], BF16)
    woT_loc = em.dram_t("cwoT_loc", [256, 2048], BF16)
    woT = em.dram_t("cwoT", [2048, 2048], BF16)

    wqkv_in = [em.dram[k].ap() for k in ("C_wq", "C_wk", "C_wv")]
    wo_in = em.dram["C_wo"].ap()
    tabs_in = [em.dram[k].ap() for k in ("C_cosq", "C_sinq", "C_cosk", "C_sink")]
    permT_in = em.dram["C_permT"].ap()
    dmask_in = em.dram["C_dmask"].ap()
    ident = em.sb["ident_bf"]

    LD, SWX, CC = "c_ld", "c_swx", "c_cc"
    PE, DVE, ACTS = "c_pe", "E_dve", "E_act"
    TRD, YST, WOS, YLD, HST, XRS = "c_trd", "c_yst", "c_wos", "c_yld", "c_hst", "c_xrs"

    for _eng in ("tensor", "vector", "scalar"):
        em.wait(_eng, "c_ld")
    em.dma("gpsimd", xbf_d[:, :], x_in[:, :], SWX)
    em.wait("gpsimd", SWX)
    em.collective("AllGather", OP.bypass, [list(range(8))],
                  [xbf_d.ap().opt()], [x2all.ap().opt()], CC)
    cc_x = em.cnt[CC]

    cwstack = ExitStack()
    wqT = em.sbuf("cwqT", [128, 16 * 256], BF16, cwstack)
    wkT = em.sbuf("cwkT", [128, 16 * 256], BF16, cwstack)
    wvT = em.sbuf("cwvT", [128, 16 * 256], BF16, cwstack)
    dstack = ExitStack()
    w_f32 = em.sbuf("cw_f32", [128, 2 * 2048], F32, dstack)
    wo_f32 = em.sbuf("cwo_f32", [128, 16 * 256], F32, dstack)
    wt_bf = em.sbuf("cwt_bf", [128, 2048], BF16, dstack)
    absj = em.sbuf("cabsj", [128, 128], F32, dstack)
    stt = em.sbuf("cstt", [128, 8], F32, dstack)
    scr = {"absj": absj, "st": stt}
    tr_ps = [em.psum("ctr_ps0", [128, 1024], BF16, dstack),
             em.psum("ctr_ps1", [128, 1024], BF16, dstack)]
    woT_sb = em.sbuf("cwoT_sb", [128, 2 * 2048], BF16, dstack)

    last_ev = {0: 0, 1: 0}

    def tr_evac(srcs, dsts, par):
        em.wait("tensor", ACTS, last_ev[par])
        pe_c = 0
        for j, s in enumerate(srcs):
            pe_c = em.op("tensor", lambda e, s=s, j=j, par=par: e.transpose(
                tr_ps[par][:, j * 128:(j + 1) * 128], s, ident[:, :]), inc=PE)
        em.wait("scalar", PE, pe_c)
        ac = 0
        for j, d in enumerate(dsts):
            ac = em.op("scalar", lambda e, d=d, j=j, par=par: e.copy(
                d, tr_ps[par][:, j * 128:(j + 1) * 128]), inc=ACTS)
        last_ev[par] = ac
        return ac

    par = 0
    for mi, (w_in, wT_sb) in enumerate(((wqkv_in[0], wqT), (wqkv_in[1], wkT),
                                        (wqkv_in[2], wvT))):
        em.wait("sync", ACTS)
        em.wait("sync", DVE)
        ld = em.dma("sync", w_f32[:, :].rearrange("p (t w) -> p t w", w=2048),
                    _tiled(w_in), "c_ldw")
        for ot in range(2):
            em.wait("vector", "c_ldw", ld)
            em.wait("scalar", "c_ldw", ld)
            em.wait("vector", PE)
            emit_ternary_tile(em, w_f32[:, ot * 2048:(ot + 1) * 2048], wt_bf, scr, 16)
            em.wait("tensor", DVE)
            for g8 in range(2):
                srcs = [wt_bf[:, (g8 * 8 + j) * 128:(g8 * 8 + j + 1) * 128]
                        for j in range(8)]
                dsts = [wT_sb[:, (g8 * 8 + j) * 256 + ot * 128:
                               (g8 * 8 + j) * 256 + (ot + 1) * 128] for j in range(8)]
                tr_evac(srcs, dsts, par % 2)
                par += 1
    em.wait("sync", ACTS)
    em.wait("sync", DVE)
    ld = em.dma("sync", wo_f32[:, :].rearrange("p (t w) -> p t w", w=256),
                _tiled(wo_in), "c_ldw")
    for ot in range(16):
        em.wait("vector", "c_ldw", ld)
        em.wait("scalar", "c_ldw", ld)
        em.wait("vector", PE)
        emit_ternary_tile(em, wo_f32[:, ot * 256:(ot + 1) * 256], wt_bf, scr, 2)
        em.wait("tensor", DVE)
        srcs = [wt_bf[:, r * 128:(r + 1) * 128] for r in range(2)]
        dsts = [woT_sb[:, r * 2048 + ot * 128: r * 2048 + (ot + 1) * 128]
                for r in range(2)]
        tr_evac(srcs, dsts, par % 2)
        par += 1
    em.wait("gpsimd", ACTS)
    em.dma("gpsimd", _tiled(woT_loc.ap(), p=128),
           woT_sb[:, :].rearrange("p (r o) -> p r o", o=2048), WOS)
    em.wait("gpsimd", WOS)
    em.collective("AllGather", OP.bypass, [list(range(8))],
                  [woT_loc.ap().opt()], [woT.ap().opt()], CC)
    cc_wo = em.cnt[CC]
    dstack.close()

    astack = ExitStack()
    xT_b = em.sbuf("cxT_b", [128, 16 * 2048], BF16, astack, is_transpose_dest=True)
    tabs = [em.sbuf(f"ctab{i}", [128, 2048], F32, astack) for i in range(4)]
    permT = em.sbuf("cpermT", [128, 128], BF16, astack)
    dmask = em.sbuf("cdmask", [128, 128], F32, astack)
    qp = em.sbuf("cqp", [128, 2048], BF16, astack)
    kp = em.sbuf("ckp", [128, 2048], BF16, astack)
    vT = em.sbuf("cvT", [128, 2048], BF16, astack)
    vN = em.sbuf("cvN", [128, 2048], BF16, astack)
    qraw = em.sbuf("cqraw", [128, 2048], BF16, astack)
    t1 = em.sbuf("ct1", [128, 2048], F32, astack)
    t2 = em.sbuf("ct2", [128, 512], F32, astack)
    pbf = em.sbuf("cpbf", [128, 2048], BF16, astack)
    ptT = em.sbuf("cptT", [128, 2048], BF16, astack)
    yT_sb = em.sbuf("cyT_sb", [128, 2048], BF16, astack)
    sm = em.sbuf("csm", [128, 8], F32, astack)
    ps_a = em.psum("cps_a", [128, 512], F32, astack)
    ps_b = em.psum("cps_b", [128, 512], F32, astack)
    ps_s = em.psum("cps_s", [128, 2048], F32, astack)
    ps_t2 = [em.psum("cps_t20", [128, 1024], BF16, astack),
             em.psum("cps_t21", [128, 1024], BF16, astack)]

    for i in range(4):
        em.dma("sync", tabs[i][:, :], tabs_in[i], "c_ldt")
    em.dma("sync", permT[:, :], permT_in, "c_ldt")
    ld_tab = em.dma("sync", dmask[:, :], dmask_in, "c_ldt")
    em.wait("sync", CC, cc_x)
    em.wait("tensor", "c_ldt", ld_tab)
    em.wait("vector", "c_ldt", ld_tab)

    pv_evac = 0
    str_evac = {0: 0, 1: 0}

    def project(dst, wT_sb, h_loc, cos_t, sin_t):
        nonlocal pv_evac
        em.wait("tensor", DVE)
        for tb4 in range(4):
            em.wait("tensor", ACTS)
            pe_c = 0
            for d in range(16):
                pe_c = em.op("tensor", lambda e, d=d, tb4=tb4, h_loc=h_loc, wT_sb=wT_sb:
                             e.matmul(
                    ps_a[:, :],
                    wT_sb[:, d * 256 + h_loc * 128: d * 256 + (h_loc + 1) * 128],
                    xT_b[:, d * 2048 + tb4 * 512: d * 2048 + (tb4 + 1) * 512],
                    start=(d == 0), stop=(d == 15)), inc=PE if d == 15 else None)
            em.wait("scalar", PE, pe_c)
            tgt = qraw if cos_t is not None else dst
            em.op("scalar", lambda e, tb4=tb4, tgt=tgt: e.copy(
                tgt[:, tb4 * 512:(tb4 + 1) * 512], ps_a[:, :]), inc=ACTS)
        if cos_t is None:
            return
        ac_all = em.cnt[ACTS]
        em.wait("tensor", ACTS, ac_all)
        em.wait("vector", ACTS, ac_all)
        for tb4 in range(4):
            em.wait("tensor", DVE)
            pe_c = em.op("tensor", lambda e, tb4=tb4: e.matmul(
                ps_a[:, :], permT[:, :], qraw[:, tb4 * 512:(tb4 + 1) * 512],
                start=True, stop=True), inc=PE)
            em.op("vector", lambda e, tb4=tb4, cos_t=cos_t: e.tensor_tensor(
                t1[:, tb4 * 512:(tb4 + 1) * 512], qraw[:, tb4 * 512:(tb4 + 1) * 512],
                cos_t[:, tb4 * 512:(tb4 + 1) * 512], OP.mult))
            em.wait("vector", PE, pe_c)
            em.op("vector", lambda e, sin_t=sin_t, tb4=tb4: e.tensor_tensor(
                t2[:, :], ps_a[:, :], sin_t[:, tb4 * 512:(tb4 + 1) * 512], OP.mult),
                inc=DVE)
            em.op("vector", lambda e, tb4=tb4, dst=dst: e.tensor_tensor(
                dst[:, tb4 * 512:(tb4 + 1) * 512], t1[:, tb4 * 512:(tb4 + 1) * 512],
                t2[:, :], OP.add), inc=DVE)

    last_yst = 0
    em.wait("sync", WOS)
    for b in range(B):
        em.wait("sync", PE)
        for i in range(16):
            em.dma("sync", xT_b[:, i * 2048:(i + 1) * 2048],
                   x2all.ap()[b * 2048:(b + 1) * 2048, i * 128:(i + 1) * 128], TRD,
                   transpose=True)
        trd = em.cnt[TRD]
        em.wait("tensor", TRD, trd)
        for h_loc in range(2):
            project(qp, wqT, h_loc, tabs[0], tabs[1])
            project(kp, wkT, h_loc, tabs[2], tabs[3])
            project(vT, wvT, h_loc, None, None)
            em.wait("tensor", ACTS)
            for g8 in range(2):
                par = g8 % 2
                em.wait("tensor", DVE, str_evac[par])
                pe_c = 0
                for j in range(8):
                    kb = g8 * 8 + j
                    pe_c = em.op("tensor", lambda e, kb=kb, j=j, par=par: e.transpose(
                        ps_t2[par][:, j * 128:(j + 1) * 128],
                        vT[:, kb * 128:(kb + 1) * 128], ident[:, :]), inc=PE)
                em.wait("vector", PE, pe_c)
                ev = 0
                for j in range(8):
                    kb = g8 * 8 + j
                    ev = em.op("vector", lambda e, kb=kb, j=j, par=par: e.tensor_copy(
                        vN[:, kb * 128:(kb + 1) * 128],
                        ps_t2[par][:, j * 128:(j + 1) * 128]), inc=DVE)
                str_evac[par] = ev
            em.wait("tensor", DVE)
            prev_exp = 0
            for qt in range(16):
                L = (qt + 1) * 128
                nblk = (L + 511) // 512
                em.wait("tensor", ACTS, prev_exp)
                pe_s = 0
                for blk in range(nblk):
                    w = min(512, L - blk * 512)
                    pe_s = em.op("tensor", lambda e, blk=blk, w=w, qt=qt: e.matmul(
                        ps_s[:, blk * 512: blk * 512 + w],
                        qp[:, qt * 128:(qt + 1) * 128],
                        kp[:, blk * 512: blk * 512 + w],
                        start=True, stop=True), inc=PE)
                em.wait("vector", PE, pe_s)
                em.op("vector", lambda e, qt=qt: e.tensor_tensor(
                    ps_s[:, qt * 128:(qt + 1) * 128], ps_s[:, qt * 128:(qt + 1) * 128],
                    dmask[:, :], OP.add))
                em.op("vector", lambda e, L=L: e.tensor_reduce(
                    sm[:, 0:1], ps_s[:, 0:L], mybir.AxisListType.X, OP.max))
                dv_m = em.op("vector", lambda e: e.tensor_scalar(
                    sm[:, 1:2], sm[:, 0:1], -1.0, None, OP.mult), inc=DVE)
                em.wait("scalar", DVE, dv_m)
                em.wait("scalar", PE)
                prev_exp = em.op("scalar", lambda e, L=L: e.activation(
                    pbf[:, 0:L], ps_s[:, 0:L], AF.Exp, bias=sm[:, 1:2],
                    accum_out=sm[:, 2:3]), inc=ACTS)
                em.wait("vector", ACTS, prev_exp)
                em.op("vector", lambda e: e.reciprocal(sm[:, 3:4], sm[:, 2:3]))
                dv_p = em.op("vector", lambda e, L=L: e.tensor_scalar(
                    pbf[:, 0:L], pbf[:, 0:L], sm[:, 3:4], None, OP.mult), inc=DVE)
                em.wait("tensor", DVE, dv_p)
                pvps = ps_a if qt % 2 == 0 else ps_b
                em.wait("tensor", DVE, pv_evac)
                tr_done = {}
                for kb in range(qt + 2):
                    if kb <= qt:
                        par = kb % 2
                        em.wait("tensor", DVE, str_evac[par])
                        pe_t = em.op("tensor", lambda e, kb=kb, par=par: e.transpose(
                            ps_t2[par][:, (kb % 8) * 128:(kb % 8) * 128 + 128],
                            pbf[:, kb * 128:(kb + 1) * 128], ident[:, :]), inc=PE)
                        tr_done[kb] = pe_t
                    if kb >= 1:
                        k0 = kb - 1
                        par = k0 % 2
                        em.wait("vector", PE, tr_done[k0])
                        ev = em.op("vector", lambda e, k0=k0, par=par: e.tensor_copy(
                            ptT[:, k0 * 128:(k0 + 1) * 128],
                            ps_t2[par][:, (k0 % 8) * 128:(k0 % 8) * 128 + 128]),
                            inc=DVE)
                        str_evac[par] = ev
                        em.wait("tensor", DVE, ev)
                        em.op("tensor", lambda e, k0=k0, qt=qt, pvps=pvps: e.matmul(
                            pvps[:, 0:128], vN[:, k0 * 128:(k0 + 1) * 128],
                            ptT[:, k0 * 128:(k0 + 1) * 128],
                            start=(k0 == 0), stop=(k0 == qt)),
                            inc=PE if k0 == qt else None)
                pe_pv = em.cnt[PE]
                em.wait("vector", PE, pe_pv)
                pv_evac = em.op("vector", lambda e, qt=qt, pvps=pvps: e.tensor_copy(
                    yT_sb[:, qt * 128:(qt + 1) * 128], pvps[:, 0:128]), inc=DVE)
            em.wait("gpsimd", DVE, pv_evac)
            for half in range(2):
                j = 2 * b + half
                last_yst = em.dma(
                    "gpsimd",
                    yT_a2a_in.ap()[j * 256 + h_loc * 128: j * 256 + (h_loc + 1) * 128, :],
                    yT_sb[:, half * 1024:(half + 1) * 1024], YST)
            em.wait("vector", YST, last_yst)
    em.wait("gpsimd", YST, last_yst)
    em.collective("AllToAll", OP.bypass, [list(range(8))],
                  [yT_a2a_in.ap().opt()], [yT_tok.ap().opt()], CC)
    cc_y = em.cnt[CC]
    astack.close()
    cwstack.close()

    fstack = ExitStack()
    yT_t = em.sbuf("cyT_t", [128, 16 * 128], BF16, fstack)
    wo_full = em.sbuf("cwo_full", [128, 16 * 2048], BF16, fstack)
    yacc = em.sbuf("cyacc", [128, 2048], F32, fstack)
    junk_bf = em.sbuf("cjunk", [128, 2048], BF16, fstack)
    stat = em.sbuf("cstat", [128, 8], F32, fstack)
    xres = em.sbuf("cxres", [128, 2048], F32, fstack)
    ps_y = [em.psum("cps_y0", [128, 512], F32, fstack),
            em.psum("cps_y1", [128, 512], F32, fstack)]

    em.wait("sync", CC, cc_wo)
    em.wait("sync", PE)
    em.wait("sync", DVE)
    em.wait("sync", ACTS)
    ld_wo = em.dma("sync", wo_full[:, :].rearrange("p (t o) -> p t o", o=2048),
                   _tiled(woT.ap()), "c_wol")
    em.wait("sync", CC, cc_y)
    em.wait("tensor", "c_wol", ld_wo)
    last_hst = 0
    last_yev = 0
    for tt in range(8):
        em.wait("sync", PE)
        ld_y = em.dma("sync", yT_t[:, :].rearrange("p (t o) -> p t o", o=128),
                      _tiled(yT_tok.ap()[:, tt * 128:(tt + 1) * 128]), YLD)
        em.wait("sync", DVE)
        xl = em.dma("sync", xres[:, :], x_in[tt * 128:(tt + 1) * 128, :], XRS)
        em.wait("tensor", YLD, ld_y)
        em.wait("vector", HST, last_hst)
        for ob in range(4):
            par = ob % 2
            em.wait("tensor", DVE)
            pe_c = 0
            for d in range(16):
                pe_c = em.op("tensor", lambda e, d=d, ob=ob, par=par: e.matmul(
                    ps_y[par][:, :], yT_t[:, d * 128:(d + 1) * 128],
                    wo_full[:, d * 2048 + ob * 512:(d * 2048) + (ob + 1) * 512],
                    start=(d == 0), stop=(d == 15)), inc=PE if d == 15 else None)
            em.wait("vector", PE, pe_c)
            last_yev = em.op("vector", lambda e, ob=ob, par=par: e.tensor_copy(
                yacc[:, ob * 512:(ob + 1) * 512], ps_y[par][:, :]), inc=DVE)
        em.wait("vector", XRS, xl)
        em.wait("vector", HST, last_hst)
        em.op("vector", lambda e: e.tensor_tensor(
            yacc[:, :], yacc[:, :], xres[:, :], OP.add), inc=DVE)
        em.wait("scalar", DVE)
        vln = emit_ln(em, yacc, junk_bf, stat)
        em.wait("gpsimd", DVE, vln)
        last_hst = em.dma("gpsimd", x_out[tt * 128:(tt + 1) * 128, :], yacc[:, :], HST)
    em.wait("gpsimd", HST)
    fstack.close()


def _host_dn_scales(down_scales, core_id):
    out = np.zeros((2048, 11), dtype=np.float32)
    for b in range(11):
        g = (704 * core_id + 64 * b) // 128
        out[:, b] = down_scales[g, :].astype(np.float32)
    return out


def build_stageB():
    nc = bass.Bass("TRN2", target_bir_lowering=False)
    em = Em(nc)
    x_in = em.dram_t("bx_in", [TLOC, 2048], F32, kind="ExternalInput").ap()
    em.dram_t("B_gup", [2 * FF_I, 32], I32, kind="ExternalInput")
    em.dram_t("B_gus", [2 * FF_I, 2], F16, kind="ExternalInput")
    em.dram_t("B_dnp", [2048, 88], I32, kind="ExternalInput")
    em.dram_t("B_dns", [2048, 11], F32, kind="ExternalInput")
    ident_in = em.dram_t("ident_in", [128, 128], BF16, kind="ExternalInput").ap()
    constv_in = em.dram_t("constv_in", [128, 32], F32, kind="ExternalInput").ap()
    x_out = em.dram_t("bx_out", [TLOC, 2048], F32, kind="ExternalOutput").ap()
    with em.ctx:
        em.sbuf("ident_bf", [128, 128], BF16)
        em.sbuf("constv", [128, 32], F32)
        em.dma("sync", em.sb["ident_bf"][:, :], ident_in, "b_ld")
        em.dma("sync", em.sb["constv"][:, :], constv_in, "b_ld")
        emit_stageB(em, 0, x_in, x_out)
        em.emit()
    return em


def consts_array(em):
    vals = getattr(em, "const_vals", [])
    arr = np.zeros((128, 32), dtype=np.float32)
    arr[:, :len(vals)] = np.asarray(vals, dtype=np.float32)[None, :]
    return arr


def stageB_inputs(em, x1_full, inputs, core_id):
    c = core_id
    return {
        "constv_in": consts_array(em),
        "bx_in": np.ascontiguousarray(x1_full[c * TLOC:(c + 1) * TLOC]),
        "B_gup": np.ascontiguousarray(inputs["L0_ffn_gate_up_packed"][:, 32 * c:32 * (c + 1)]),
        "B_gus": np.ascontiguousarray(
            np.asarray(inputs["L0_ffn_gate_up_scales"])[2 * c:2 * c + 2, :].T),
        "B_dnp": np.ascontiguousarray(inputs["L0_ffn_down_packed"][:, 88 * c:88 * (c + 1)]),
        "B_dns": _host_dn_scales(np.asarray(inputs["L0_ffn_down_scales"]), c),
        "ident_in": np.eye(128, dtype=ml_dtypes.bfloat16),
    }


def run_stageB(x1_full, inputs, sim=False):
    em = build_stageB()
    in_maps = [stageB_inputs(em, x1_full, inputs, c) for c in range(NCORES)]
    if sim:
        import concourse.bass_interp as bass_interp
        msim = bass_interp.MultiCoreSim(em.nc, NCORES)
        for c in range(NCORES):
            for k, v in in_maps[c].items():
                msim.cores[c].tensor(k)[:] = v
        msim.simulate()
        return np.concatenate(
            [np.asarray(msim.cores[c].mem_tensor("bx_out")) for c in range(NCORES)],
            axis=0)
    res = run_bass_kernel_spmd(em.nc, in_maps, core_ids=list(range(NCORES)))
    return np.concatenate([np.asarray(res.results[c]["bx_out"]) for c in range(NCORES)],
                          axis=0)




def build_stageC():
    nc = bass.Bass("TRN2", target_bir_lowering=False)
    em = Em(nc)
    x_in = em.dram_t("cx_in", [TLOC, 2048], F32, kind="ExternalInput").ap()
    em.dram_t("C_wq", [256, 2048], F32, kind="ExternalInput")
    em.dram_t("C_wk", [256, 2048], F32, kind="ExternalInput")
    em.dram_t("C_wv", [256, 2048], F32, kind="ExternalInput")
    em.dram_t("C_wo", [2048, 256], F32, kind="ExternalInput")
    em.dram_t("C_cosq", [128, 2048], F32, kind="ExternalInput")
    em.dram_t("C_sinq", [128, 2048], F32, kind="ExternalInput")
    em.dram_t("C_cosk", [128, 2048], F32, kind="ExternalInput")
    em.dram_t("C_sink", [128, 2048], F32, kind="ExternalInput")
    em.dram_t("C_permT", [128, 128], BF16, kind="ExternalInput")
    em.dram_t("C_dmask", [128, 128], F32, kind="ExternalInput")
    ident_in = em.dram_t("ident_in", [128, 128], BF16, kind="ExternalInput").ap()
    constv_in = em.dram_t("constv_in", [128, 32], F32, kind="ExternalInput").ap()
    x_out = em.dram_t("cx_out", [TLOC, 2048], F32, kind="ExternalOutput").ap()
    with em.ctx:
        em.sbuf("ident_bf", [128, 128], BF16)
        em.sbuf("constv", [128, 32], F32)
        em.dma("sync", em.sb["ident_bf"][:, :], ident_in, "c_ld")
        em.dma("sync", em.sb["constv"][:, :], constv_in, "c_ld")
        emit_stageC(em, x_in, x_out)
        em.emit()
    return em


def rope_tables():
    half = 64
    inv = 1.0 / 10000.0 ** (np.arange(half) / half)
    t = np.arange(S)
    fr = np.outer(inv, t)
    cos = np.cos(fr)
    sin = np.sin(fr)
    COS = np.concatenate([cos, cos], axis=0).astype(np.float32)
    SIN = np.concatenate([sin, sin], axis=0).astype(np.float32)
    return COS, SIN


def stageC_inputs(em, x2_full, inputs, core_id):
    c = core_id
    COS, SIN = rope_tables()
    sc = float(HD) ** -0.5
    permT = np.zeros((128, 128), dtype=ml_dtypes.bfloat16)
    for d in range(64):
        permT[d + 64, d] = -1.0
        permT[d, d + 64] = 1.0
    dmask = np.triu(np.full((128, 128), -1e30, dtype=np.float32), 1)
    return {
        "cx_in": np.ascontiguousarray(x2_full[c * TLOC:(c + 1) * TLOC]),
        "C_wq": np.ascontiguousarray(np.asarray(inputs["L1_attn_w_q"])[256 * c:256 * (c + 1)]),
        "C_wk": np.ascontiguousarray(np.asarray(inputs["L1_attn_w_k"])[256 * c:256 * (c + 1)]),
        "C_wv": np.ascontiguousarray(np.asarray(inputs["L1_attn_w_v"])[256 * c:256 * (c + 1)]),
        "C_wo": np.ascontiguousarray(np.asarray(inputs["L1_attn_w_o"])[:, 256 * c:256 * (c + 1)]),
        "C_cosq": COS * sc, "C_sinq": SIN * sc, "C_cosk": COS, "C_sink": SIN,
        "C_permT": permT, "C_dmask": dmask,
        "ident_in": np.eye(128, dtype=ml_dtypes.bfloat16),
        "constv_in": consts_array(em),
    }


def run_stageC(x2_full, inputs, sim=False):
    em = build_stageC()
    in_maps = [stageC_inputs(em, x2_full, inputs, c) for c in range(NCORES)]
    if sim:
        import concourse.bass_interp as bass_interp
        msim = bass_interp.MultiCoreSim(em.nc, NCORES)
        for c in range(NCORES):
            for k, v in in_maps[c].items():
                msim.cores[c].tensor(k)[:] = v
        msim.simulate()
        return np.concatenate(
            [np.asarray(msim.cores[c].mem_tensor("cx_out")) for c in range(NCORES)],
            axis=0)
    res = run_bass_kernel_spmd(em.nc, in_maps, core_ids=list(range(NCORES)))
    return np.concatenate([np.asarray(res.results[c]["cx_out"]) for c in range(NCORES)],
                          axis=0)



def emit_stageD(em, x_in, x_out):
    xbf_d = em.dram_t("dx_bf", [TLOC, 2048], BF16)
    x4all = em.dram_t("dx_all", [TOK, 2048], BF16)
    dy_part = em.dram_t("dy_part", [TOK, 2048], F32)
    dy_mine = em.dram_t("dy_mine", [TLOC, 2048], F32)
    w1bf = em.dram_t("dw1bf", [2, 2048, MOE_I], BF16)
    w3bf = em.dram_t("dw3bf", [2, 2048, MOE_I], BF16)
    w2bf = em.dram_t("dw2bf", [2, MOE_I, 2048], BF16)
    ws1bf = em.dram_t("dws1bf", [2048, 352], BF16)
    ws3bf = em.dram_t("dws3bf", [2048, 352], BF16)
    ws2bf = em.dram_t("dws2bf", [352, 2048], BF16)

    LD, SWX, SWW, CC = "d_ld", "d_swx", "d_sww", "d_cc"
    PE, DVE, ACTS = "d_pe", "E_dve", "E_act"
    TRD, XRS, HST, YLD = "d_trd", "d_xrs", "d_hst", "d_yld"

    em.dma("gpsimd", xbf_d[:, :], x_in[:, :], SWX)
    em.wait("gpsimd", SWX)
    em.collective("AllGather", OP.bypass, [list(range(8))],
                  [xbf_d.ap().opt()], [x4all.ap().opt()], CC)
    cc_x = em.cnt[CC]
    for (dst, srcname) in ((w1bf, "D_w1T"), (w3bf, "D_w3T"), (w2bf, "D_w2T"),
                           (ws1bf, "D_ws1T"), (ws3bf, "D_ws3T"), (ws2bf, "D_ws2T")):
        em.dma("gpsimd", dst.ap(), em.dram[srcname].ap(), SWW)
    sww_done = em.cnt[SWW]

    xT_tb = em.sbuf("dxT_tb", [128, 16 * 1024], BF16, is_transpose_dest=True)
    wgT_bf = em.sbuf("dwgT", [128, 16 * 16], BF16)
    selm = em.sbuf("dselm", [128, 2 * 16], F32)
    ones16 = em.sbuf("dones1", [128, 16], F32)
    onesr = em.sbuf("donesr", [128, 128], BF16)
    P_sb = em.sbuf("dP", [128, 16], F32)
    msel = em.sbuf("dmsel", [128, 16], F32)
    gsx = em.sbuf("dgs", [128, 4], F32)
    grk = em.sbuf("dgrk", [128, 4], F32)
    mw = em.sbuf("dmw", [128, 16], F32)
    mw0 = em.sbuf("dmw0", [128, 16], F32)
    myc = em.sbuf("dmyc", [128, 2], F32)
    mycb = em.sbuf("dmycb", [128, 2], BF16)
    cbrow = em.sbuf("dcbrow", [128, 256], BF16)
    cb = em.sbuf("dcb", [128, 2 * 1024], BF16)
    sm = em.sbuf("dsm", [128, 8], F32)
    w1buf = em.sbuf("dw1buf", [128, 2 * 2048], BF16)
    w3buf = em.sbuf("dw3buf", [128, 2 * 2048], BF16)
    hT = em.sbuf("dhT", [128, 25 * 1024], BF16)
    silu_sb = em.sbuf("dsilu", [128, 512], BF16)
    w2sb = em.sbuf("dw2sb", [128, 2 * 25 * 256], BF16)
    ystg = em.sbuf("dystg", [128, 8 * 256], F32)
    junk_bf = em.sbuf("djunk", [128, 2048], BF16)
    stat = em.sbuf("dstat", [128, 8], F32)
    xres = em.sbuf("dxres", [128, 2048], F32)
    yfin = em.sbuf("dyfin", [128, 2048], F32)
    ps_g = em.psum("dps_g", [128, 512], F32)
    ps_u = em.psum("dps_u", [128, 512], F32)
    ps_l = em.psum("dps_l", [128, 512], F32)
    ps_y = [em.psum("dps_y0", [128, 512], F32),
            em.psum("dps_y1", [128, 512], F32)]
    ps_t = em.psum("dps_t", [128, 1024], BF16)

    ident = em.sb["ident_bf"]
    ld_wg = em.dma("gpsimd", wgT_bf[:, :].rearrange("p (t w) -> p t w", w=16),
                   _tiled(em.dram["D_wgT"].ap()), "d_ldwg")
    ld_sel = em.dma("sync", selm[:, :], em.dram["D_selm"].ap(), LD)
    for _eng in ("tensor", "vector", "scalar"):
        em.wait(_eng, LD, ld_sel)
    em.op("vector", lambda e: e.memset(ones16[:, :], 1.0))
    em.op("vector", lambda e: e.memset(onesr[0:1, :], 1.0))
    em.wait("tensor", "d_ldwg", ld_wg)
    em.wait("sync", CC, cc_x)
    em.wait("tensor", SWW, sww_done)
    em.wait("sync", SWW, sww_done)

    NIT = {0: 11, 1: 11, 2: 3}
    IBASE = {0: 0, 1: 11, 2: 22}
    W13SRC = {0: (w1bf, w3bf, 0), 1: (w1bf, w3bf, 1), 2: (ws1bf, ws3bf, None)}

    last_pst = 0
    w1_pe = {0: 0, 1: 0}
    w2_pe = {0: 0, 1: 0}
    w1par = 0
    last_yst = 0
    for tb in range(8):
        em.wait("sync", PE)
        for i in range(16):
            em.dma("sync", xT_tb[:, i * 1024:(i + 1) * 1024],
                   x4all.ap()[tb * 1024:(tb + 1) * 1024, i * 128:(i + 1) * 128], TRD,
                   transpose=True)
        trd = em.cnt[TRD]
        em.wait("tensor", TRD, trd)
        for tt in range(8):
            em.wait("tensor", DVE)
            pe_c = 0
            for d in range(16):
                pe_c = em.op("tensor", lambda e, d=d, tt=tt: e.matmul(
                    ps_l[:, 0:16],
                    xT_tb[:, d * 1024 + tt * 128: d * 1024 + (tt + 1) * 128],
                    wgT_bf[:, d * 16:(d + 1) * 16],
                    start=(d == 0), stop=(d == 15)), inc=PE if d == 15 else None)
            em.wait("vector", PE, pe_c)
            em.op("vector", lambda e: e.tensor_reduce(
                sm[:, 0:1], ps_l[:, 0:16], mybir.AxisListType.X, OP.max))
            dv = em.op("vector", lambda e: e.tensor_scalar(
                sm[:, 1:2], sm[:, 0:1], -1.0, None, OP.mult), inc=DVE)
            em.wait("scalar", DVE, dv)
            em.wait("scalar", PE, pe_c)
            ac = em.op("scalar", lambda e: e.activation(
                P_sb[:, :], ps_l[:, 0:16], AF.Exp, bias=sm[:, 1:2],
                accum_out=sm[:, 2:3]), inc=ACTS)
            em.wait("vector", ACTS, ac)
            em.op("vector", lambda e: e.reciprocal(sm[:, 3:4], sm[:, 2:3]))
            em.op("vector", lambda e: e.tensor_scalar(
                P_sb[:, :], P_sb[:, :], sm[:, 3:4], None, OP.mult))
            em.op("vector", lambda e: e.tensor_tensor(
                gsx[:, :], P_sb[:, 0:16:4], P_sb[:, 1:16:4], OP.max))
            em.op("vector", lambda e: e.tensor_tensor(
                mw[:, 0:4], P_sb[:, 2:16:4], P_sb[:, 3:16:4], OP.max))
            em.op("vector", lambda e: e.tensor_tensor(
                gsx[:, :], gsx[:, :], mw[:, 0:4], OP.max))
            for g in range(4):
                em.op("vector", lambda e, g=g: e.tensor_scalar(
                    mw[:, 0:4], gsx[:, :], gsx[:, g:g + 1], None, OP.is_gt))
                em.op("vector", lambda e, g=g: e.tensor_reduce(
                    grk[:, g:g + 1], mw[:, 0:4], mybir.AxisListType.X, OP.add))
            em.op("vector", lambda e: e.tensor_scalar(
                grk[:, :], grk[:, :], 2.0, None, OP.is_lt))
            for g in range(4):
                em.op("vector", lambda e, g=g: e.tensor_scalar(
                    msel[:, 4 * g:4 * (g + 1)], ones16[:, 0:4], grk[:, g:g + 1],
                    None, OP.mult))
            em.op("vector", lambda e: e.tensor_scalar(
                mw[:, :], msel[:, :], 1e30, -1e30, OP.mult, OP.add))
            em.op("vector", lambda e: e.tensor_tensor(
                mw[:, :], P_sb[:, :], mw[:, :], OP.add))
            em.op("vector", lambda e: e.tensor_copy(mw0[:, :], mw[:, :]))
            for it4 in range(3):
                em.op("vector", lambda e: e.tensor_reduce(
                    sm[:, 4:5], mw[:, :], mybir.AxisListType.X, OP.max))
                em.op("vector", lambda e: e.tensor_scalar(
                    msel[:, :], mw[:, :], sm[:, 4:5], None, OP.is_ge))
                em.op("vector", lambda e: e.scalar_tensor_tensor(
                    mw[:, :], msel[:, :], -2e30, mw[:, :], OP.mult, OP.add))
            em.op("vector", lambda e: e.tensor_reduce(
                sm[:, 4:5], mw[:, :], mybir.AxisListType.X, OP.max))
            em.op("vector", lambda e: e.tensor_scalar(
                msel[:, :], mw0[:, :], sm[:, 4:5], None, OP.is_ge))
            em.op("vector", lambda e: e.tensor_tensor(
                msel[:, :], msel[:, :], P_sb[:, :], OP.mult))
            for el in range(2):
                em.op("vector", lambda e, el=el: e.tensor_tensor(
                    mw[:, :], msel[:, :], selm[:, el * 16:(el + 1) * 16], OP.mult))
                em.op("vector", lambda e, el=el: e.tensor_reduce(
                    myc[:, el:el + 1], mw[:, :], mybir.AxisListType.X, OP.add))
            dv = em.op("vector", lambda e: e.tensor_copy(mycb[:, :], myc[:, :]),
                       inc=DVE)
            em.wait("tensor", DVE, dv)
            em.wait("tensor", ACTS, last_pst)
            for el in range(2):
                em.op("tensor", lambda e, el=el: e.transpose(
                    ps_t[0:1, el * 128:(el + 1) * 128], mycb[:, el:el + 1],
                    ident[:, :]), inc=PE)
            pe_t = em.cnt[PE]
            em.wait("scalar", PE, pe_t)
            for el in range(2):
                last_pst = em.op("scalar", lambda e, el=el: e.copy(
                    cbrow[0:1, el * 128:(el + 1) * 128],
                    ps_t[0:1, el * 128:(el + 1) * 128]), inc=ACTS)
            em.wait("tensor", ACTS, last_pst)
            for el in range(2):
                em.op("tensor", lambda e, el=el: e.matmul(
                    ps_l[:, 128 + el * 128: 128 + (el + 1) * 128],
                    onesr[0:1, :], cbrow[0:1, el * 128:(el + 1) * 128],
                    start=True, stop=True), inc=PE)
            pe_b = em.cnt[PE]
            em.wait("vector", PE, pe_b)
            for el in range(2):
                em.op("vector", lambda e, el=el, tt=tt: e.tensor_copy(
                    cb[:, el * 1024 + tt * 128: el * 1024 + (tt + 1) * 128],
                    ps_l[:, 128 + el * 128: 128 + (el + 1) * 128]), inc=DVE)
        cb_done = em.cnt[DVE]
        em.wait("tensor", DVE, cb_done)
        em.wait("vector", PE)
        for sl in range(3):
            w1s, w3s, ei = W13SRC[sl]
            idim = MOE_I if sl < 2 else 352
            for ib in range(NIT[sl]):
                iw = min(128, idim - ib * 128)
                buf = w1par % 2
                w1par += 1
                em.wait("sync", PE, w1_pe[buf])
                if ei is not None:
                    s1 = w1s.ap()[ei, :, ib * 128:ib * 128 + iw]
                    s3 = w3s.ap()[ei, :, ib * 128:ib * 128 + iw]
                else:
                    s1 = w1s.ap()[:, ib * 128:ib * 128 + iw]
                    s3 = w3s.ap()[:, ib * 128:ib * 128 + iw]
                l1 = em.dma("sync",
                            w1buf[:, buf * 2048:buf * 2048 + 16 * iw].rearrange(
                                "p (t o) -> p t o", o=iw), _tiled(s1), f"d_w1l{buf}")
                l3 = em.dma("sync",
                            w3buf[:, buf * 2048:buf * 2048 + 16 * iw].rearrange(
                                "p (t o) -> p t o", o=iw), _tiled(s3), f"d_w3l{buf}")
                em.wait("tensor", f"d_w1l{buf}", l1)
                em.wait("tensor", f"d_w3l{buf}", l3)
                for th in range(2):
                    em.wait("tensor", ACTS)
                    pe_g = 0
                    for d in range(16):
                        pe_g = em.op("tensor", lambda e, d=d, th=th, buf=buf, iw=iw:
                                     e.matmul(
                            ps_g[0:iw, :],
                            w1buf[:, buf * 2048 + d * iw: buf * 2048 + (d + 1) * iw],
                            xT_tb[:, d * 1024 + th * 512: d * 1024 + (th + 1) * 512],
                            start=(d == 0), stop=(d == 15)),
                            inc=PE if d == 15 else None)
                    em.wait("tensor", DVE)
                    pe_u = 0
                    for d in range(16):
                        pe_u = em.op("tensor", lambda e, d=d, th=th, buf=buf, iw=iw:
                                     e.matmul(
                            ps_u[0:iw, :],
                            w3buf[:, buf * 2048 + d * iw: buf * 2048 + (d + 1) * iw],
                            xT_tb[:, d * 1024 + th * 512: d * 1024 + (th + 1) * 512],
                            start=(d == 0), stop=(d == 15)),
                            inc=PE if d == 15 else None)
                    em.wait("scalar", pe_g if False else PE, pe_g)
                    em.wait("scalar", DVE)
                    ac_s = em.op("scalar", lambda e, iw=iw: e.activation(
                        silu_sb[0:iw, :], ps_g[0:iw, :], AF.Silu), inc=ACTS)
                    em.wait("vector", ACTS, ac_s)
                    em.wait("vector", PE, pe_u)
                    em.op("vector", lambda e, iw=iw: e.tensor_tensor(
                        silu_sb[0:iw, :], silu_sb[0:iw, :], ps_u[0:iw, :], OP.mult))
                    hslot = (IBASE[sl] + ib) * 1024 + th * 512
                    if sl < 2:
                        em.op("vector", lambda e, iw=iw, hslot=hslot, sl=sl, th=th:
                              e.tensor_tensor(
                            hT[0:iw, hslot:hslot + 512], silu_sb[0:iw, :],
                            cb[0:iw, sl * 1024 + th * 512: sl * 1024 + (th + 1) * 512],
                            OP.mult), inc=DVE)
                    else:
                        em.op("vector", lambda e, iw=iw, hslot=hslot: e.tensor_copy(
                            hT[0:iw, hslot:hslot + 512], silu_sb[0:iw, :]), inc=DVE)
                w1_pe[buf] = em.cnt[PE]
        hT_done = em.cnt[DVE]
        for ob in range(8):
            buf = ob % 2
            em.wait("sync", PE, w2_pe[buf])
            l2a = em.dma("sync",
                         w2sb[:, buf * 6400: buf * 6400 + 11 * 256].rearrange(
                             "p (t o) -> p t o", o=256),
                         _tiled(w2bf.ap()[0, :, ob * 256:(ob + 1) * 256]),
                         f"d_w2a{buf}")
            l2b = em.dma("sync",
                         w2sb[:, buf * 6400 + 11 * 256: buf * 6400 + 22 * 256].rearrange(
                             "p (t o) -> p t o", o=256),
                         _tiled(w2bf.ap()[1, :, ob * 256:(ob + 1) * 256]),
                         f"d_w2b{buf}")
            l2c = em.dma("sync",
                         w2sb[:, buf * 6400 + 22 * 256: buf * 6400 + 24 * 256].rearrange(
                             "p (t o) -> p t o", o=256),
                         _tiled(ws2bf.ap()[0:256, ob * 256:(ob + 1) * 256]),
                         f"d_w2c{buf}")
            l2d = em.dma("sync",
                         w2sb[0:96, buf * 6400 + 24 * 256: buf * 6400 + 25 * 256],
                         ws2bf.ap()[256:352, ob * 256:(ob + 1) * 256], f"d_w2d{buf}")
            em.wait("tensor", f"d_w2a{buf}", l2a)
            em.wait("tensor", f"d_w2b{buf}", l2b)
            em.wait("tensor", f"d_w2c{buf}", l2c)
            em.wait("tensor", f"d_w2d{buf}", l2d)
            em.wait("tensor", DVE, hT_done)
            for tt in range(8):
                par = (ob * 8 + tt) % 2
                em.wait("tensor", DVE)
                pe_c = 0
                nsl = 0
                for sl in range(3):
                    idim = MOE_I if sl < 2 else 352
                    for ib in range(NIT[sl]):
                        iw = min(128, idim - ib * 128)
                        it = IBASE[sl] + ib
                        nsl += 1
                        pe_c = em.op("tensor", lambda e, it=it, iw=iw, tt=tt, par=par,
                                     buf=buf, first=(nsl == 1), last=(nsl == 25):
                                     e.matmul(
                            ps_y[par][:, 0:256],
                            hT[0:iw, it * 1024 + tt * 128: it * 1024 + (tt + 1) * 128],
                            w2sb[0:iw, buf * 6400 + it * 256: buf * 6400 + (it + 1) * 256],
                            start=first, stop=last), inc=PE if nsl == 25 else None)
                em.wait("vector", PE, pe_c)
                em.op("vector", lambda e, tt=tt, par=par: e.tensor_copy(
                    ystg[:, tt * 256:(tt + 1) * 256], ps_y[par][:, 0:256]), inc=DVE)
            w2_pe[buf] = em.cnt[PE]
            dv = em.cnt[DVE]
            em.wait("gpsimd", DVE, dv)
            last_yst = em.dma(
                "gpsimd",
                _tiled(dy_part.ap()[tb * 1024:(tb + 1) * 1024,
                                    ob * 256:(ob + 1) * 256]),
                ystg[:, :].rearrange("p (t o) -> p t o", o=256), "d_yst")
            em.wait("vector", "d_yst", last_yst)
    em.wait("gpsimd", "d_yst", last_yst)
    em.collective("ReduceScatter", OP.add, [list(range(8))],
                  [dy_part.ap().opt()], [dy_mine.ap().opt()], CC)
    cc_y = em.cnt[CC]
    em.wait("sync", CC, cc_y)
    last_hst = 0
    for tt in range(8):
        em.wait("sync", DVE)
        em.wait("sync", HST, last_hst)
        yl = em.dma("sync", yfin[:, :], dy_mine.ap()[tt * 128:(tt + 1) * 128, :], YLD)
        xl = em.dma("sync", xres[:, :], x_in[tt * 128:(tt + 1) * 128, :], XRS)
        em.wait("vector", YLD, yl)
        em.wait("vector", XRS, xl)
        em.wait("vector", HST, last_hst)
        em.op("vector", lambda e: e.tensor_tensor(
            yfin[:, :], yfin[:, :], xres[:, :], OP.add), inc=DVE)
        em.wait("scalar", DVE)
        vln = emit_ln(em, yfin, junk_bf, stat)
        em.wait("gpsimd", DVE, vln)
        last_hst = em.dma("gpsimd", x_out[tt * 128:(tt + 1) * 128, :], yfin[:, :], HST)
    em.wait("gpsimd", HST)


def build_stageD():
    nc = bass.Bass("TRN2", target_bir_lowering=False)
    em = Em(nc)
    x_in = em.dram_t("dx_in", [TLOC, 2048], F32, kind="ExternalInput").ap()
    em.dram_t("D_w1T", [2, 2048, MOE_I], F32, kind="ExternalInput")
    em.dram_t("D_w3T", [2, 2048, MOE_I], F32, kind="ExternalInput")
    em.dram_t("D_w2T", [2, MOE_I, 2048], F32, kind="ExternalInput")
    em.dram_t("D_ws1T", [2048, 352], F32, kind="ExternalInput")
    em.dram_t("D_ws3T", [2048, 352], F32, kind="ExternalInput")
    em.dram_t("D_ws2T", [352, 2048], F32, kind="ExternalInput")
    em.dram_t("D_wgT", [2048, 16], F32, kind="ExternalInput")
    em.dram_t("D_selm", [128, 32], F32, kind="ExternalInput")
    ident_in = em.dram_t("ident_in", [128, 128], BF16, kind="ExternalInput").ap()
    constv_in = em.dram_t("constv_in", [128, 32], F32, kind="ExternalInput").ap()
    x_out = em.dram_t("dx_out", [TLOC, 2048], F32, kind="ExternalOutput").ap()
    with em.ctx:
        em.sbuf("ident_bf", [128, 128], BF16)
        em.sbuf("constv", [128, 32], F32)
        em.dma("sync", em.sb["ident_bf"][:, :], ident_in, "d_ld")
        em.dma("sync", em.sb["constv"][:, :], constv_in, "d_ld")
        emit_stageD(em, x_in, x_out)
        em.emit()
    return em


def stageD_inputs(em, x3_full, inputs, core_id):
    c = core_id
    f32 = np.float32
    W1 = np.asarray(inputs["L1_ffn_W1"], f32)
    W3 = np.asarray(inputs["L1_ffn_W3"], f32)
    W2 = np.asarray(inputs["L1_ffn_W2"], f32)
    Ws1 = np.asarray(inputs["L1_ffn_Ws1"], f32)
    Ws3 = np.asarray(inputs["L1_ffn_Ws3"], f32)
    Ws2 = np.asarray(inputs["L1_ffn_Ws2"], f32)
    Wg = np.asarray(inputs["L1_ffn_Wg"], f32)
    selm = np.zeros((128, 32), dtype=f32)
    selm[:, 2 * c] = 1.0
    selm[:, 16 + 2 * c + 1] = 1.0
    return {
        "dx_in": np.ascontiguousarray(x3_full[c * TLOC:(c + 1) * TLOC]),
        "D_w1T": np.ascontiguousarray(W1[2 * c:2 * c + 2].transpose(0, 2, 1)),
        "D_w3T": np.ascontiguousarray(W3[2 * c:2 * c + 2].transpose(0, 2, 1)),
        "D_w2T": np.ascontiguousarray(W2[2 * c:2 * c + 2].transpose(0, 2, 1)),
        "D_ws1T": np.ascontiguousarray(Ws1[352 * c:352 * (c + 1)].T),
        "D_ws3T": np.ascontiguousarray(Ws3[352 * c:352 * (c + 1)].T),
        "D_ws2T": np.ascontiguousarray(Ws2[:, 352 * c:352 * (c + 1)].T),
        "D_wgT": np.ascontiguousarray(Wg.T),
        "D_selm": selm,
        "ident_in": np.eye(128, dtype=ml_dtypes.bfloat16),
        "constv_in": consts_array(em),
    }


def run_stageD(x3_full, inputs, sim=False):
    em = build_stageD()
    in_maps = [stageD_inputs(em, x3_full, inputs, c) for c in range(NCORES)]
    if sim:
        import concourse.bass_interp as bass_interp
        msim = bass_interp.MultiCoreSim(em.nc, NCORES)
        for c in range(NCORES):
            for k, v in in_maps[c].items():
                msim.cores[c].tensor(k)[:] = v
        msim.simulate()
        return np.concatenate(
            [np.asarray(msim.cores[c].mem_tensor("dx_out")) for c in range(NCORES)],
            axis=0)
    res = run_bass_kernel_spmd(em.nc, in_maps, core_ids=list(range(NCORES)))
    return np.concatenate([np.asarray(res.results[c]["dx_out"]) for c in range(NCORES)],
                          axis=0)




def emit_stageA1(em):
    xT_in = em.dram["A_xT"].ap()
    wq_in = em.dram["A_wqkvz"].ap()
    wba_in = em.dram["A_wbaP"].ap()
    conv_in = em.dram["A_convw"].ap()
    alog_in = em.dram["A_alog"].ap()
    dt_in = em.dram["A_dt"].ap()
    qT_out = em.dram["a1_qT"].ap()
    kT_out = em.dram["a1_kT"].ap()
    vT_out = em.dram["a1_vT"].ap()
    z_out = em.dram["a1_z"].ap()
    gb_out = em.dram["a1_gb"].ap()

    wqkvz_loc = em.dram_t("a1_wloc", [256, 12288], BF16)
    wqkvz_bf = em.dram_t("a1_wbf", [2048, 12288], BF16)

    LD, SWX, SWW, CC = "a1_ld", "a1_swx", "a1_sww", "a1_cc"
    PE, DVE, ACTS = "a1_pe", "E_dve", "E_act"
    QST = "a1_qst"

    em.dma("gpsimd", wqkvz_loc.ap(), wq_in, SWW)
    em.wait("gpsimd", SWW)
    em.collective("AllGather", OP.bypass, [list(range(8))],
                  [wqkvz_loc.ap().opt()], [wqkvz_bf.ap().opt()], CC)
    cc_w = em.cnt[CC]

    xT_bf = em.sbuf("a1_xT", [128, 16 * 1027], BF16)
    wba_bf = em.sbuf("a1_wba", [128, 16 * 64], BF16)
    conv_sb = em.sbuf("a1_conv", [128, 64 * 4], F32)
    alog = em.sbuf("a1_alog", [128, 2], F32)
    negA = em.sbuf("a1_negA", [128, 2], F32)
    wbuf = em.sbuf("a1_wbuf", [128, 2 * 4096], BF16)
    raw = em.sbuf("a1_raw", [128, 4 * 1027], F32)
    sil = em.sbuf("a1_sil", [128, 4 * 1027], F32)
    sq = em.sbuf("a1_sq", [128, 1024], F32)
    onesc = em.sbuf("a1_onesc", [128, 1], F32)
    onesrow = em.sbuf("a1_onesrow", [128, 128], F32)
    rowb = em.sbuf("a1_rowb", [128, 2 * 1024], F32)
    obf = em.sbuf("a1_obf", [128, 4 * 1024], BF16)
    zbuf = em.sbuf("a1_zbuf", [128, 2048], BF16)
    ba_sb = em.sbuf("a1_ba", [128, 1024], F32)
    spl = em.sbuf("a1_spl", [128, 1024], F32)
    gbo = em.sbuf("a1_gbo", [128, 1024], F32)
    ps_a = em.psum("a1_ps_a", [128, 512], F32)
    ps_b = em.psum("a1_ps_b", [128, 512], F32)
    ps_r = em.psum("a1_ps_r", [128, 1024], F32)

    em.dma("gpsimd", xT_bf[:, :].rearrange("p (t w) -> p t w", w=1027),
           _tiled(xT_in), SWX)
    em.dma("gpsimd", wba_bf[:, :].rearrange("p (t w) -> p t w", w=64),
           _tiled(wba_in), SWX)
    ld_c = em.dma("sync", conv_sb[:, :].rearrange("p (t w) -> p t w", w=4),
                  _tiled(conv_in), LD)
    em.dma("sync", alog[32:64, 0:1], alog_in, LD)
    ld_al = em.dma("sync", alog[32:64, 1:2], dt_in, LD)
    swx_done = em.cnt[SWX]
    em.wait("vector", LD, ld_al)
    em.wait("vector", SWX, swx_done)
    em.op("vector", lambda e: e.memset(onesc[:, :], 1.0))
    em.op("vector", lambda e: e.memset(onesrow[0:1, :], 1.0))
    em.wait("scalar", LD, ld_al)
    ac = em.op("scalar", lambda e: e.activation(negA[32:64, 0:1], alog[32:64, 0:1],
                                                AF.Exp), inc=ACTS)
    em.wait("vector", ACTS, ac)
    em.op("vector", lambda e: e.tensor_scalar(
        negA[32:64, 0:1], negA[32:64, 0:1], -1.0, None, OP.mult))
    em.wait("tensor", SWX, swx_done)
    em.wait("tensor", CC, cc_w)
    em.wait("tensor", "a1_ld", ld_al)

    INV = float(DK) ** -0.5
    em.wait("sync", CC, cc_w)
    wb_pe = {0: 0, 1: 0}
    last_qst = 0
    for h in range(NK):
        for ot in range(4):
            buf = (h * 4 + ot) % 2
            ocol = 768 * h + ot * 128
            em.wait("sync", PE, wb_pe[buf])
            wl = em.dma("sync",
                        wbuf[:, buf * 4096:buf * 4096 + 2048].rearrange(
                            "p (t o) -> p t o", o=128),
                        _tiled(wqkvz_bf.ap()[:, ocol:ocol + 128]), f"a1_wl{buf}")
            em.wait("tensor", f"a1_wl{buf}", wl)
            em.wait("tensor", ACTS)
            for tb, (t0, tw) in enumerate(((0, 512), (512, 512), (1024, 3))):
                ps = ps_a if tb % 2 == 0 else ps_b
                em.wait("tensor", ACTS)
                pe_c = 0
                for d in range(16):
                    pe_c = em.op("tensor", lambda e, d=d, t0=t0, tw=tw, buf=buf,
                                 ps=ps: e.matmul(
                        ps[:, 0:tw],
                        wbuf[:, buf * 4096 + d * 128:buf * 4096 + (d + 1) * 128],
                        xT_bf[:, d * 1027 + t0: d * 1027 + t0 + tw],
                        start=(d == 0), stop=(d == 15)), inc=PE if d == 15 else None)
                em.wait("scalar", PE, pe_c)
                em.wait("scalar", DVE)
                em.op("scalar", lambda e, ot=ot, t0=t0, tw=tw, ps=ps: e.copy(
                    raw[:, ot * 1027 + t0: ot * 1027 + t0 + tw], ps[:, 0:tw]),
                    inc=ACTS)
            wb_pe[buf] = em.cnt[PE]
        ac_raw = em.cnt[ACTS]
        em.wait("vector", ACTS, ac_raw)
        for ot in range(4):
            cidx = {0: h, 1: 16 + h, 2: 32 + 2 * h, 3: 32 + 2 * h + 1}[ot]
            em.op("vector", lambda e, ot=ot, cidx=cidx: e.tensor_scalar(
                sil[:, ot * 1027:ot * 1027 + 1024], raw[:, ot * 1027:ot * 1027 + 1024],
                conv_sb[:, cidx * 4:cidx * 4 + 1], None, OP.mult))
            for j in range(1, 4):
                em.op("vector", lambda e, ot=ot, cidx=cidx, j=j: e.scalar_tensor_tensor(
                    sil[:, ot * 1027:ot * 1027 + 1024],
                    raw[:, ot * 1027 + j: ot * 1027 + j + 1024],
                    conv_sb[:, cidx * 4 + j:cidx * 4 + j + 1],
                    sil[:, ot * 1027:ot * 1027 + 1024], OP.mult, OP.add))
        dv_conv = em.cnt[DVE]
        em.wait("scalar", DVE, dv_conv)
        for ot in range(4):
            em.op("scalar", lambda e, ot=ot: e.activation(
                sil[:, ot * 1027:ot * 1027 + 1024], sil[:, ot * 1027:ot * 1027 + 1024],
                AF.Silu), inc=ACTS)
        ac_sil = em.cnt[ACTS]
        for ot in range(2):
            em.wait("scalar", ACTS, ac_sil)
            em.wait("scalar", PE)
            sq_a = em.op("scalar", lambda e, ot=ot: e.activation(
                sq[:, 0:1024], sil[:, ot * 1027:ot * 1027 + 1024], AF.Square),
                inc=ACTS)
            em.wait("tensor", ACTS, sq_a)
            em.wait("tensor", DVE)
            for tb in range(2):
                em.op("tensor", lambda e, tb=tb: e.matmul(
                    ps_r[0:1, tb * 512:(tb + 1) * 512], sq[:, tb * 512:(tb + 1) * 512],
                    onesc[:, 0:1], start=True, stop=True) if False else e.matmul(
                    ps_r[0:1, tb * 512:(tb + 1) * 512], onesc[:, 0:1],
                    sq[:, tb * 512:(tb + 1) * 512], start=True, stop=True), inc=PE)
            pe_s = em.cnt[PE]
            em.wait("scalar", PE, pe_s)
            cfold = math.log(INV * INV) if ot == 0 else math.log(INV)
            em.op("scalar", lambda e: e.activation(
                rowb[0:1, 0:1024], ps_r[0:1, 0:1024], AF.Ln, scale=1.0 / DK,
                bias=em.cv(1e-6, 1)))
            ac_r = em.op("scalar", lambda e, cfold=cfold: e.activation(
                rowb[0:1, 0:1024], rowb[0:1, 0:1024], AF.Exp, scale=-0.5,
                bias=em.cv(cfold, 1)), inc=ACTS)
            em.wait("tensor", ACTS, ac_r)
            for tb in range(2):
                em.op("tensor", lambda e, tb=tb: e.matmul(
                    ps_r[:, tb * 512:(tb + 1) * 512], onesrow[0:1, :],
                    rowb[0:1, tb * 512:(tb + 1) * 512], start=True, stop=True),
                    inc=PE)
            pe_b2 = em.cnt[PE]
            em.wait("vector", PE, pe_b2)
            em.op("vector", lambda e, ot=ot: e.tensor_tensor(
                obf[:, ot * 1024:(ot + 1) * 1024], sil[:, ot * 1027:ot * 1027 + 1024],
                ps_r[:, 0:1024], OP.mult), inc=DVE)
        em.wait("vector", ACTS, ac_sil)
        for ot in (2, 3):
            em.op("vector", lambda e, ot=ot: e.tensor_copy(
                obf[:, ot * 1024:(ot + 1) * 1024],
                sil[:, ot * 1027:ot * 1027 + 1024]), inc=DVE)
        dv_o = em.cnt[DVE]
        em.wait("gpsimd", DVE, dv_o)
        last_qst = em.dma("gpsimd", qT_out[h * 128:(h + 1) * 128, :],
                          obf[:, 0:1024], QST)
        last_qst = em.dma("gpsimd", kT_out[h * 128:(h + 1) * 128, :],
                          obf[:, 1024:2048], QST)
        last_qst = em.dma("gpsimd", vT_out[2 * h * 128:(2 * h + 1) * 128, :],
                          obf[:, 2048:3072], QST)
        last_qst = em.dma("gpsimd", vT_out[(2 * h + 1) * 128:(2 * h + 2) * 128, :],
                          obf[:, 3072:4096], QST)
        em.wait("vector", QST, last_qst)
        em.wait("scalar", QST, last_qst)

    em.wait("sync", PE)
    z_pe = {0: 0, 1: 0}
    for zb in range(8):
        pass
    for h in range(NK):
        buf = h % 2
        ocol = 768 * h + 512
        em.wait("sync", PE, z_pe[buf])
        wl = em.dma("sync",
                    wbuf[:, buf * 4096:buf * 4096 + 16 * 256].rearrange(
                        "p (t o) -> p t o", o=256),
                    _tiled(wqkvz_bf.ap()[:, ocol:ocol + 256]), f"a1_zl{buf}")
        em.wait("tensor", f"a1_zl{buf}", wl)
        for tt in range(8):
            par = tt % 2
            em.wait("tensor", DVE)
            pe_c = 0
            for d in range(16):
                pe_c = em.op("tensor", lambda e, d=d, tt=tt, buf=buf, par=par:
                             e.matmul(
                    (ps_a if par == 0 else ps_b)[:, 0:256],
                    xT_bf[:, d * 1027 + 3 + tt * 128: d * 1027 + 3 + (tt + 1) * 128],
                    wbuf[:, buf * 4096 + d * 256: buf * 4096 + (d + 1) * 256],
                    start=(d == 0), stop=(d == 15)), inc=PE if d == 15 else None)
            em.wait("vector", PE, pe_c)
            em.wait("vector", "a1_zst")
            dv_z = em.op("vector", lambda e, par=par: e.tensor_copy(
                zbuf[:, 0:256], (ps_a if par == 0 else ps_b)[:, 0:256]), inc=DVE)
            em.wait("gpsimd", DVE, dv_z)
            em.dma("gpsimd", z_out[tt * 128:(tt + 1) * 128, h * 256:(h + 1) * 256],
                   zbuf[:, 0:256], "a1_zst")
        z_pe[buf] = em.cnt[PE]

    em.wait("tensor", SWX, swx_done)
    for tb in range(2):
        ps = ps_a if tb % 2 == 0 else ps_b
        em.wait("tensor", DVE)
        pe_c = 0
        for d in range(16):
            pe_c = em.op("tensor", lambda e, d=d, tb=tb, ps=ps: e.matmul(
                ps[0:64, :],
                wba_bf[:, d * 64:(d + 1) * 64],
                xT_bf[:, d * 1027 + 3 + tb * 512: d * 1027 + 3 + (tb + 1) * 512],
                start=(d == 0), stop=(d == 15)), inc=PE if d == 15 else None)
        em.wait("scalar", PE, pe_c)
        em.op("scalar", lambda e, tb=tb, ps=ps: e.copy(
            ba_sb[0:64, tb * 512:(tb + 1) * 512], ps[0:64, :]), inc=ACTS)
    ac_ba = em.cnt[ACTS]
    em.wait("scalar", ACTS, ac_ba)
    em.op("scalar", lambda e: e.activation(
        gbo[0:32, :], ba_sb[0:32, :], AF.Sigmoid), inc=ACTS)
    em.wait("vector", ACTS)
    em.op("vector", lambda e: e.tensor_scalar(
        ba_sb[32:64, :], ba_sb[32:64, :], alog[32:64, 1:2], None, OP.add), inc=DVE)
    em.wait("scalar", DVE)
    em.op("scalar", lambda e: e.activation(spl[32:64, :], ba_sb[32:64, :], AF.Abs))
    em.op("scalar", lambda e: e.activation(spl[32:64, :], spl[32:64, :], AF.Exp,
                                           scale=-1.0))
    ac_sp = em.op("scalar", lambda e: e.activation(
        spl[32:64, :], spl[32:64, :], AF.Ln, bias=em.cv(1.0, 32, 32)), inc=ACTS)
    em.wait("vector", ACTS, ac_sp)
    em.op("vector", lambda e: e.tensor_scalar(
        gbo[32:64, :], ba_sb[32:64, :], 0.0, None, OP.max))
    em.op("vector", lambda e: e.tensor_tensor(
        gbo[32:64, :], gbo[32:64, :], spl[32:64, :], OP.add))
    dv_g = em.op("vector", lambda e: e.tensor_scalar(
        gbo[32:64, :], gbo[32:64, :], negA[32:64, 0:1], None, OP.mult), inc=DVE)
    em.wait("gpsimd", DVE, dv_g)
    em.wait("gpsimd", ACTS)
    em.dma("gpsimd", gb_out[:, :], gbo[0:64, :], QST)
    em.wait("gpsimd", QST)
    em.wait("gpsimd", "a1_zst")


def build_stageA1():
    nc = bass.Bass("TRN2", target_bir_lowering=False)
    em = Em(nc)
    em.dram_t("A_xT", [2048, 1027], F32, kind="ExternalInput")
    em.dram_t("A_wqkvz", [256, 12288], F32, kind="ExternalInput")
    em.dram_t("A_wbaP", [2048, 64], F32, kind="ExternalInput")
    em.dram_t("A_convw", [8192, 4], F32, kind="ExternalInput")
    em.dram_t("A_alog", [32, 1], F32, kind="ExternalInput")
    em.dram_t("A_dt", [32, 1], F32, kind="ExternalInput")
    ident_in = em.dram_t("ident_in", [128, 128], BF16, kind="ExternalInput").ap()
    constv_in = em.dram_t("constv_in", [128, 32], F32, kind="ExternalInput").ap()
    em.dram_t("a1_qT", [2048, TLOC], BF16, kind="ExternalOutput")
    em.dram_t("a1_kT", [2048, TLOC], BF16, kind="ExternalOutput")
    em.dram_t("a1_vT", [4096, TLOC], BF16, kind="ExternalOutput")
    em.dram_t("a1_z", [TLOC, 4096], BF16, kind="ExternalOutput")
    em.dram_t("a1_gb", [64, TLOC], F32, kind="ExternalOutput")
    with em.ctx:
        em.sbuf("ident_bf", [128, 128], BF16)
        em.sbuf("constv", [128, 32], F32)
        em.dma("sync", em.sb["ident_bf"][:, :], ident_in, "a1_ld")
        em.dma("sync", em.sb["constv"][:, :], constv_in, "a1_ld")
        emit_stageA1(em)
        em.emit()
    return em


def stageA1_inputs(em, inputs, core_id):
    c = core_id
    x = np.asarray(inputs["L0_attn_x"], np.float32).reshape(TOK, 2048)
    xs = np.zeros((TLOC + 3, 2048), dtype=np.float32)
    t0 = c * TLOC
    xs[3:] = x[t0:t0 + TLOC]
    if t0 % S != 0:
        xs[0:3] = x[t0 - 3:t0]
    wba = np.asarray(inputs["L0_attn_W_ba"], np.float32).reshape(2048, 16, 4)
    wbaP = np.concatenate([wba[:, :, 0:2].reshape(2048, 32),
                           wba[:, :, 2:4].reshape(2048, 32)], axis=1)
    return {
        "A_xT": np.ascontiguousarray(xs.T),
        "A_wqkvz": np.ascontiguousarray(
            np.asarray(inputs["L0_attn_W_qkvz"], np.float32)[256 * c:256 * (c + 1)]),
        "A_wbaP": np.ascontiguousarray(wbaP),
        "A_convw": np.asarray(inputs["L0_attn_conv_w"], np.float32),
        "A_alog": np.asarray(inputs["L0_attn_A_log"], np.float32).reshape(32, 1),
        "A_dt": np.asarray(inputs["L0_attn_dt_bias"], np.float32).reshape(32, 1),
        "ident_in": np.eye(128, dtype=ml_dtypes.bfloat16),
        "constv_in": consts_array(em),
    }


def run_stageA1(inputs, sim=False):
    em = build_stageA1()
    in_maps = [stageA1_inputs(em, inputs, c) for c in range(NCORES)]
    outs = ["a1_qT", "a1_kT", "a1_vT", "a1_z", "a1_gb"]
    if sim:
        import concourse.bass_interp as bass_interp
        msim = bass_interp.MultiCoreSim(em.nc, NCORES)
        for c in range(NCORES):
            for k, v in in_maps[c].items():
                msim.cores[c].tensor(k)[:] = v
        msim.simulate()
        return [{k: np.asarray(msim.cores[c].mem_tensor(k)) for k in outs}
                for c in range(NCORES)]
    res = run_bass_kernel_spmd(em.nc, in_maps, core_ids=list(range(NCORES)))
    return [{k: np.asarray(res.results[c][k]) for k in outs} for c in range(NCORES)]




A2_LIM = {"b": B, "c": 16, "kp": 2, "vi": 2, "db": 7, "stage": 9}


def emit_stageA2(em):
    qT_in = [em.dram[f"A2_qT{k}"].ap() for k in range(2)]
    kT_in = [em.dram[f"A2_kT{k}"].ap() for k in range(2)]
    vT_in = [em.dram[f"A2_vT{v}"].ap() for v in range(4)]
    z_in = em.dram["A2_z"].ap()
    g_in = em.dram["A2_g"].ap()
    b_in = em.dram["A2_b"].ap()
    nw_in = em.dram["A2_normw"].ap()
    masks_in = em.dram["A2_masks"].ap()
    gT_out = em.dram["a2_gT"].ap()

    LD, GLD, PE, DVE, ACTS = "a2_ld", "a2_gld", "a2_pe", "E_dve", "E_act"
    QLD, GST = "a2_qld", "a2_gst"
    ident = em.sb["ident_bf"]

    qTb = [em.sbuf(f"a2_qTb{k}", [128, 2048], BF16) for k in range(2)]
    kTb = [em.sbuf(f"a2_kTb{k}", [128, 2048], BF16) for k in range(2)]
    vTb = [em.sbuf(f"a2_vTb{v}", [128, 2048], BF16) for v in range(4)]
    gch = em.sbuf("a2_gch", [128, 256], F32)
    masks = em.sbuf("a2_masks", [128, 384], F32)
    onesrow = em.sbuf("a2_onesrow", [128, 128], F32)
    normw = em.sbuf("a2_normw", [128, 128], F32)
    nwrow = em.sbuf("a2_nwrow", [128, 128], F32)
    G_colT = em.sbuf("a2_Gcol", [128, 4], F32)
    expGc = em.sbuf("a2_expG", [128, 4], F32)
    bcolT = em.sbuf("a2_bcol", [128, 4], F32)
    bgcol = em.sbuf("a2_bgcol", [128, 4], F32)
    scal = em.sbuf("a2_scal", [128, 8], F32)
    grow = em.sbuf("a2_grow", [128, 384], F32)
    KKtM = em.sbuf("a2_KKtM", [128, 128], BF16)
    KQtM = em.sbuf("a2_KQtM", [128, 128], BF16)
    K_nat = em.sbuf("a2_Knat", [128, 128], BF16)
    gm = em.sbuf("a2_gm", [128, 128], F32)
    dec = em.sbuf("a2_dec", [128, 128], F32)
    decu = em.sbuf("a2_decu", [128, 128], F32)
    decui = em.sbuf("a2_decui", [128, 128], F32)
    bn = [em.sbuf(f"a2_bn{i}", [128, 128], BF16) for i in range(2)]
    bt = [em.sbuf(f"a2_bt{i}", [128, 128], BF16) for i in range(2)]
    pt = [em.sbuf(f"a2_pt{i}", [128, 128], BF16) for i in range(2)]
    WK = em.sbuf("a2_WK", [128, 128], BF16)
    WTn = em.sbuf("a2_WTn", [128, 128], BF16)
    Urhs = em.sbuf("a2_Urhs", [128, 128], BF16)
    Rbf = em.sbuf("a2_Rbf", [128, 128], BF16)
    qtl = em.sbuf("a2_qtl", [128, 128], BF16)
    DTt = em.sbuf("a2_DT", [128, 128], BF16)
    Kbr = em.sbuf("a2_Kbr", [128, 128], BF16)
    Sst = em.sbuf("a2_S", [128, 4 * 2 * 128], BF16)
    zl = em.sbuf("a2_zl", [128, 2 * 128], BF16)
    silz = em.sbuf("a2_silz", [128, 128], BF16)
    gt1 = em.sbuf("a2_gt1", [128, 128], F32)
    sqj = em.sbuf("a2_sqj", [128, 128], BF16)
    gTo = [em.sbuf(f"a2_gTo{v}", [128, 2048], BF16) for v in range(4)]
    ps_c = em.psum("a2_ps_c", [128, 512], F32)
    ps_f = em.psum("a2_ps_f", [128, 512], F32)
    ps_m = em.psum("a2_ps_m", [128, 512], F32)
    ps_t = em.psum("a2_ps_t", [128, 1024], BF16)
    ps_ft = em.psum("a2_ps_ft", [128, 512], F32)

    ld_m = em.dma("sync", masks[:, :], masks_in, LD)
    ld_nw = em.dma("sync", nwrow[0:1, :], nw_in, LD)
    for _eng in ("tensor", "vector", "scalar"):
        em.wait(_eng, LD, ld_nw)
        em.wait(_eng, "a2_ld0")
    em.op("vector", lambda e: e.memset(onesrow[0:1, :], 1.0))
    dv0 = em.op("vector", lambda e: e.memset(Sst[:, :], 0.0), inc=DVE)
    lo01 = masks[:, 0:128]
    up01 = masks[:, 128:256]
    idf = masks[:, 256:384]
    em.wait("tensor", DVE, dv0)
    pnw = em.op("tensor", lambda e: e.matmul(
        ps_f[:, 0:128], onesrow[0:1, :], nwrow[0:1, :], start=True, stop=True),
        inc=PE)
    em.wait("vector", PE, pnw)
    em.op("vector", lambda e: e.tensor_copy(normw[:, :], ps_f[:, 0:128]), inc=DVE)

    last_gst = 0
    zpar = 0
    zld_c = {0: 0, 1: 0}

    def wboth(eng):
        em.wait(eng, DVE)
        em.wait(eng, ACTS)

    for b in range(A2_LIM["b"]):
        em.wait("sync", PE)
        qlds = 0
        for k in range(2):
            em.dma("sync", qTb[k][:, :], qT_in[k][:, b * 2048:(b + 1) * 2048], QLD)
            qlds = em.dma("sync", kTb[k][:, :], kT_in[k][:, b * 2048:(b + 1) * 2048],
                          QLD)
        for v in range(4):
            qlds = em.dma("sync", vTb[v][:, :], vT_in[v][:, b * 2048:(b + 1) * 2048],
                          QLD)
        em.wait("tensor", QLD, qlds)
        em.wait("vector", PE)
        em.wait("vector", GST, last_gst)
        em.wait("scalar", GST, last_gst)
        dv0 = em.op("vector", lambda e: e.memset(Sst[:, :], 0.0), inc=DVE)
        em.wait("tensor", DVE, dv0)
        for c in range(A2_LIM["c"]):
            par = c % 2
            csl = slice(c * 128, (c + 1) * 128)
            em.wait("sync", PE)
            gld = em.dma("sync", gch[0:4, 0:128],
                         g_in[0:4, b * 2048 + c * 128: b * 2048 + (c + 1) * 128], GLD)
            gld = em.dma("sync", gch[0:4, 128:256],
                         b_in[0:4, b * 2048 + c * 128: b * 2048 + (c + 1) * 128], GLD)
            em.wait("tensor", GLD, gld)
            wboth("tensor")
            em.op("tensor", lambda e: e.transpose(ps_ft[:, 0:4], gch[0:4, 0:128],
                                                  idf[0:4, 0:4]))
            p1 = em.op("tensor", lambda e: e.transpose(ps_ft[:, 4:8],
                                                       gch[0:4, 128:256],
                                                       idf[0:4, 0:4]), inc=PE)
            em.wait("vector", PE, p1)
            em.op("vector", lambda e: e.tensor_copy(grow[:, 0:4], ps_ft[:, 0:4]))
            dvg = em.op("vector", lambda e: e.tensor_copy(bcolT[:, :], ps_ft[:, 4:8]),
                        inc=DVE)
            em.wait("tensor", DVE, dvg)
            p2 = em.op("tensor", lambda e: e.matmul(
                ps_ft[:, 256:260], up01, grow[:, 0:4], start=True, stop=True), inc=PE)
            em.wait("vector", PE, p2)
            dvG = em.op("vector", lambda e: e.tensor_copy(G_colT[:, :],
                                                          ps_ft[:, 256:260]), inc=DVE)
            em.wait("scalar", DVE, dvG)
            acG = em.op("scalar", lambda e: e.activation(expGc[:, :], G_colT[:, :],
                                                         AF.Exp), inc=ACTS)
            em.wait("vector", ACTS, acG)
            em.op("vector", lambda e: e.tensor_tensor(bgcol[:, :], bcolT[:, :],
                                                      expGc[:, :], OP.mult), inc=DVE)
            for kp in range(A2_LIM["kp"]):
                kT_c = kTb[kp][:, csl]
                qT_c = qTb[kp][:, csl]
                wboth("tensor")
                em.op("tensor", lambda e, kT_c=kT_c: e.matmul(
                    ps_c[:, 0:128], kT_c, kT_c, start=True, stop=True))
                em.op("tensor", lambda e, kT_c=kT_c, qT_c=qT_c: e.matmul(
                    ps_c[:, 128:256], kT_c, qT_c, start=True, stop=True))
                pkt = em.op("tensor", lambda e, kT_c=kT_c: e.transpose(
                    ps_t[:, 0:128], kT_c, ident[:, :]), inc=PE)
                em.wait("vector", PE, pkt)
                em.op("vector", lambda e: e.tensor_tensor(
                    KKtM[:, :], ps_c[:, 0:128], lo01, OP.mult))
                em.op("vector", lambda e: e.tensor_tensor(
                    KQtM[:, :], ps_c[:, 128:256], up01, OP.mult))
                dvk = em.op("vector", lambda e: e.tensor_copy(K_nat[:, :],
                                                              ps_t[:, 0:128]), inc=DVE)
                for vi in range(A2_LIM["vi"]):
                    vl = 2 * kp + vi
                    Scur = Sst[:, (vl * 2 + par) * 128:(vl * 2 + par + 1) * 128]
                    Snew = Sst[:, (vl * 2 + 1 - par) * 128:(vl * 2 + 2 - par) * 128]
                    zp = zpar % 2
                    zpar += 1
                    em.wait("sync", ACTS)
                    zld_c[zp] = em.dma(
                        "sync", zl[:, zp * 128:(zp + 1) * 128],
                        z_in[b * 2048 + c * 128: b * 2048 + (c + 1) * 128,
                             vl * 128:(vl + 1) * 128], f"a2_zl{zp}")
                    em.op("vector", lambda e, vl=vl: e.tensor_scalar(
                        scal[:, 0:1], G_colT[:, vl:vl + 1], -1.0, None, OP.mult))
                    dvs = em.op("vector", lambda e, vl=vl: e.tensor_scalar(
                        scal[:, 1:2], bcolT[:, vl:vl + 1], -1.0, None, OP.mult),
                        inc=DVE)
                    if A2_LIM["stage"] < 1:
                        continue
                    wboth("tensor")
                    pgr = em.op("tensor", lambda e, vl=vl: e.transpose(
                        ps_ft[0:1, 384:512], G_colT[:, vl:vl + 1], idf), inc=PE)
                    em.wait("scalar", PE, pgr)
                    em.wait("scalar", DVE, dvs)
                    acr = em.op("scalar", lambda e: e.copy(
                        grow[0:1, 128:256], ps_ft[0:1, 384:512]), inc=ACTS)
                    em.wait("vector", ACTS, acr)
                    dvr = em.op("vector", lambda e: e.tensor_scalar(
                        grow[0:1, 256:384], grow[0:1, 128:256], -1.0, None, OP.mult),
                        inc=DVE)
                    ace = em.op("scalar", lambda e: e.activation(
                        grow[0:1, 0:128], grow[0:1, 128:256], AF.Exp), inc=ACTS)
                    if A2_LIM["stage"] < 2:
                        continue
                    em.wait("tensor", DVE, dvr)
                    em.wait("tensor", ACTS, ace)
                    em.op("tensor", lambda e: e.matmul(
                        ps_f[:, 0:128], onesrow[0:1, :], grow[0:1, 256:384],
                        start=True, stop=True))
                    em.op("tensor", lambda e: e.matmul(
                        ps_f[:, 128:256], onesrow[0:1, :], grow[0:1, 128:256],
                        start=True, stop=True))
                    pou = em.op("tensor", lambda e: e.matmul(
                        ps_f[:, 256:384], onesrow[0:1, :], grow[0:1, 0:128],
                        start=True, stop=True), inc=PE)
                    em.wait("vector", PE, pou)
                    em.op("vector", lambda e, vl=vl: e.tensor_scalar(
                        gm[:, :], ps_f[:, 0:128], G_colT[:, vl:vl + 1], 0.0,
                        OP.add, OP.min))
                    em.op("vector", lambda e: e.tensor_scalar(
                        decui[:, :], ps_f[:, 128:256], scal[:, 0:1], 0.0,
                        OP.add, OP.min))
                    dvq = em.op("vector", lambda e, qT_c=qT_c: e.tensor_tensor(
                        qtl[:, :], qT_c, ps_f[:, 256:384], OP.mult), inc=DVE)
                    if A2_LIM["stage"] < 3:
                        continue
                    em.wait("scalar", DVE, dvq)
                    em.op("scalar", lambda e: e.activation(dec[:, :], gm[:, :], AF.Exp))
                    acd = em.op("scalar", lambda e: e.activation(
                        decu[:, :], decui[:, :], AF.Exp), inc=ACTS)
                    em.wait("vector", ACTS, acd)
                    if A2_LIM.get("s4", 5) >= 1:
                        em.op("vector", lambda e: e.scalar_tensor_tensor(
                            bn[0][:, :], dec[:, :], scal[:, 1:2], KKtM[:, :],
                            OP.mult, OP.mult))
                    if A2_LIM.get("s4", 5) >= 2:
                        em.op("vector", lambda e: e.tensor_tensor(
                            DTt[:, :], KQtM[:, :], decu[:, :], OP.mult))
                    if A2_LIM.get("s4", 5) >= 3:
                        em.op("vector", lambda e: e.tensor_scalar(
                            Kbr[:, :], K_nat[:, :], decu[:, 127:128], None, OP.mult))
                    if A2_LIM.get("s4", 5) >= 4:
                        em.op("vector", lambda e, vl=vl: e.tensor_tensor(
                            scal[:, 2:3], decu[:, 127:128], expGc[:, vl:vl + 1],
                            OP.mult))
                    dvb = em.op("vector", lambda e, vl=vl: e.tensor_scalar(
                        WK[:, :], K_nat[:, :], bgcol[:, vl:vl + 1], None, OP.mult),
                        inc=DVE)
                    if A2_LIM["stage"] < 4:
                        continue
                    em.wait("tensor", DVE, dvb)
                    em.wait("tensor", ACTS)
                    pbt = em.op("tensor", lambda e: e.transpose(
                        ps_t[:, 128:256], bn[0][:, :], ident[:, :]), inc=PE)
                    em.wait("vector", PE, pbt)
                    em.wait("scalar", PE, pbt)
                    em.op("vector", lambda e: e.tensor_copy(bt[0][:, :],
                                                            ps_t[:, 128:256]))
                    em.op("vector", lambda e: e.tensor_copy(pt[0][:, :],
                                                            ps_t[:, 128:256]))
                    em.op("vector", lambda e: e.tensor_tensor(
                        pt[0][:, :], pt[0][:, :], ident[:, :], OP.add), inc=DVE)
                    if A2_LIM["stage"] < 5:
                        continue
                    cur = 0
                    for kdb in range(1, A2_LIM["db"]):
                        wboth("tensor")
                        if kdb < 6:
                            em.op("tensor", lambda e, cur=cur: e.matmul(
                                ps_m[:, 0:128], bt[cur][:, :], bn[cur][:, :],
                                start=True, stop=True))
                            psq = em.op("tensor", lambda e, cur=cur: e.matmul(
                                ps_m[:, 128:256], bn[cur][:, :], bt[cur][:, :],
                                start=True, stop=True), inc=PE)
                        else:
                            psq = em.op("tensor", lambda e, cur=cur: e.matmul(
                                ps_m[:, 0:128], bt[cur][:, :], bn[cur][:, :],
                                start=True, stop=True), inc=PE)
                        em.wait("scalar", PE, psq)
                        acq = em.op("scalar", lambda e, cur=cur: e.copy(
                            bn[1 - cur][:, :], ps_m[:, 0:128]), inc=ACTS)
                        if kdb < 6:
                            acq = em.op("scalar", lambda e, cur=cur: e.copy(
                                bt[1 - cur][:, :], ps_m[:, 128:256]), inc=ACTS)
                        em.wait("tensor", ACTS, acq)
                        em.wait("tensor", DVE)
                        pup = em.op("tensor", lambda e, cur=cur: (
                            e.matmul(ps_m[:, 256:384], bn[1 - cur][:, :],
                                     pt[cur][:, :], start=True, stop=False),
                            e.matmul(ps_m[:, 256:384], ident[:, :], pt[cur][:, :],
                                     start=False, stop=True))[-1], inc=PE)
                        em.wait("vector", PE, pup)
                        em.op("vector", lambda e, cur=cur: e.tensor_copy(
                            pt[1 - cur][:, :], ps_m[:, 256:384]), inc=DVE)
                        cur = 1 - cur
                    if A2_LIM["stage"] < 6:
                        continue
                    wboth("tensor")
                    pw = em.op("tensor", lambda e, cur=cur: e.matmul(
                        ps_m[:, 0:128], WK[:, :], pt[cur][:, :],
                        start=True, stop=True), inc=PE)
                    em.wait("vector", PE, pw)
                    em.op("vector", lambda e: e.tensor_scalar(
                        WTn[:, :], ps_m[:, 0:128], -1.0, None, OP.mult), inc=DVE)
                    if A2_LIM["stage"] < 6:
                        continue
                    wboth("tensor")
                    pv = em.op("tensor", lambda e, vl=vl, csl=csl: e.transpose(
                        ps_t[:, 256:384], vTb[vl][:, csl], ident[:, :]), inc=PE)
                    em.wait("vector", PE, pv)
                    em.op("vector", lambda e, vl=vl: e.tensor_scalar(
                        Urhs[:, :], ps_t[:, 256:384], bcolT[:, vl:vl + 1],
                        None, OP.mult), inc=DVE)
                    if A2_LIM["stage"] < 7:
                        continue
                    wboth("tensor")
                    pr = em.op("tensor", lambda e, cur=cur, Scur=Scur: (
                        e.matmul(ps_m[:, 128:256], pt[cur][:, :], Urhs[:, :],
                                 start=True, stop=False),
                        e.matmul(ps_m[:, 128:256], WTn[:, :], Scur,
                                 start=False, stop=True))[-1], inc=PE)
                    em.wait("vector", PE, pr)
                    em.op("vector", lambda e: e.tensor_copy(Rbf[:, :],
                                                            ps_m[:, 128:256]), inc=DVE)
                    if A2_LIM["stage"] < 8:
                        continue
                    wboth("tensor")
                    po = em.op("tensor", lambda e, Scur=Scur: (
                        e.matmul(ps_m[:, 256:384], qtl[:, :], Scur,
                                 start=True, stop=False),
                        e.matmul(ps_m[:, 256:384], DTt[:, :], Rbf[:, :],
                                 start=False, stop=True))[-1], inc=PE)
                    if A2_LIM["stage"] < 8:
                        continue
                    em.wait("scalar", PE, po)
                    em.wait("scalar", f"a2_zl{zp}", zld_c[zp])
                    em.op("scalar", lambda e: e.activation(
                        sqj[:, :], ps_m[:, 256:384], AF.Square,
                        accum_out=scal[:, 3:4]))
                    em.op("scalar", lambda e: e.activation(
                        scal[:, 4:5], scal[:, 3:4], AF.Ln, scale=1.0 / DV,
                        bias=em.cv(1e-6)))
                    em.op("scalar", lambda e: e.activation(
                        scal[:, 4:5], scal[:, 4:5], AF.Exp, scale=-0.5))
                    acz = em.op("scalar", lambda e, zp=zp: e.activation(
                        silz[:, :], zl[:, zp * 128:(zp + 1) * 128], AF.Silu),
                        inc=ACTS)
                    em.wait("vector", ACTS, acz)
                    em.wait("vector", PE, po)
                    em.op("vector", lambda e: e.tensor_scalar(
                        gt1[:, :], ps_m[:, 256:384], scal[:, 4:5], None, OP.mult))
                    em.op("vector", lambda e: e.tensor_tensor(
                        gt1[:, :], gt1[:, :], normw[:, :], OP.mult))
                    dvga = em.op("vector", lambda e: e.tensor_tensor(
                        qtl[:, :], gt1[:, :], silz[:, :], OP.mult), inc=DVE)
                    em.wait("tensor", DVE, dvga)
                    pga = em.op("tensor", lambda e: e.transpose(
                        ps_t[:, 384:512], qtl[:, :], ident[:, :]), inc=PE)
                    em.wait("scalar", PE, pga)
                    em.op("scalar", lambda e, vl=vl, csl=csl: e.copy(
                        gTo[vl][:, csl], ps_t[:, 384:512]), inc=ACTS)
                    if A2_LIM["stage"] < 9:
                        continue
                    wboth("tensor")
                    pkr = em.op("tensor", lambda e: e.matmul(
                        ps_c[:, 256:384], Kbr[:, :], Rbf[:, :],
                        start=True, stop=True), inc=PE)
                    em.wait("vector", PE, pkr)
                    em.op("vector", lambda e, Scur=Scur, Snew=Snew: e.scalar_tensor_tensor(
                        Snew, Scur, scal[:, 2:3], ps_c[:, 256:384],
                        OP.mult, OP.add), inc=DVE)
        dvall = em.cnt[ACTS]
        em.wait("gpsimd", ACTS, dvall)
        em.wait("gpsimd", DVE)
        for v in range(4):
            last_gst = em.dma("gpsimd", gT_out[v * 128:(v + 1) * 128,
                                               b * 2048:(b + 1) * 2048],
                              gTo[v][:, :], GST)
    em.wait("gpsimd", GST, last_gst)


def build_stageA2():
    nc = bass.Bass("TRN2", target_bir_lowering=False)
    em = Em(nc)
    for k in range(2):
        em.dram_t(f"A2_qT{k}", [128, TOK], BF16, kind="ExternalInput")
        em.dram_t(f"A2_kT{k}", [128, TOK], BF16, kind="ExternalInput")
    for v in range(4):
        em.dram_t(f"A2_vT{v}", [128, TOK], BF16, kind="ExternalInput")
    em.dram_t("A2_z", [TOK, 512], BF16, kind="ExternalInput")
    em.dram_t("A2_g", [4, TOK], F32, kind="ExternalInput")
    em.dram_t("A2_b", [4, TOK], F32, kind="ExternalInput")
    em.dram_t("A2_normw", [1, 128], F32, kind="ExternalInput")
    em.dram_t("A2_masks", [128, 384], F32, kind="ExternalInput")
    ident_in = em.dram_t("ident_in", [128, 128], BF16, kind="ExternalInput").ap()
    constv_in = em.dram_t("constv_in", [128, 32], F32, kind="ExternalInput").ap()
    em.dram_t("a2_gT", [512, TOK], BF16, kind="ExternalOutput")
    with em.ctx:
        em.sbuf("ident_bf", [128, 128], BF16)
        em.sbuf("constv", [128, 32], F32)
        em.dma("sync", em.sb["ident_bf"][:, :], ident_in, "a2_ld0")
        em.dma("sync", em.sb["constv"][:, :], constv_in, "a2_ld0")
        emit_stageA2(em)
        em.emit()
    return em


def stageA2_inputs(em, a1_outs, inputs, core_id):
    c = core_id
    qT = np.concatenate([o["a1_qT"] for o in a1_outs], axis=1)
    kT = np.concatenate([o["a1_kT"] for o in a1_outs], axis=1)
    vT = np.concatenate([o["a1_vT"] for o in a1_outs], axis=1)
    zf = np.concatenate([o["a1_z"] for o in a1_outs], axis=0)
    gb = np.concatenate([o["a1_gb"] for o in a1_outs], axis=1)
    lo01 = np.tril(np.ones((128, 128), np.float32), -1)
    up01 = np.triu(np.ones((128, 128), np.float32), 0)
    idf = np.eye(128, dtype=np.float32)
    d = {
        "A2_z": np.ascontiguousarray(zf[:, 512 * c:512 * (c + 1)]),
        "A2_g": np.ascontiguousarray(gb[32 + 4 * c:32 + 4 * (c + 1), :]),
        "A2_b": np.ascontiguousarray(gb[4 * c:4 * (c + 1), :]),
        "A2_normw": np.asarray(inputs["L0_attn_norm_w"], np.float32).reshape(1, 128),
        "A2_masks": np.concatenate([lo01, up01, idf], axis=1),
        "ident_in": np.eye(128, dtype=ml_dtypes.bfloat16),
        "constv_in": consts_array(em),
    }
    for k in range(2):
        kh = 2 * c + k
        d[f"A2_qT{k}"] = np.ascontiguousarray(qT[kh * 128:(kh + 1) * 128])
        d[f"A2_kT{k}"] = np.ascontiguousarray(kT[kh * 128:(kh + 1) * 128])
    for v in range(4):
        vh = 4 * c + v
        d[f"A2_vT{v}"] = np.ascontiguousarray(vT[vh * 128:(vh + 1) * 128])
    return d


def run_stageA2(a1_outs, inputs, sim=False):
    em = build_stageA2()
    in_maps = [stageA2_inputs(em, a1_outs, inputs, c) for c in range(NCORES)]
    if sim:
        import concourse.bass_interp as bass_interp
        msim = bass_interp.MultiCoreSim(em.nc, NCORES)
        for c in range(NCORES):
            for k, v in in_maps[c].items():
                msim.cores[c].tensor(k)[:] = v
        msim.simulate()
        return [np.asarray(msim.cores[c].mem_tensor("a2_gT")) for c in range(NCORES)]
    res = run_bass_kernel_spmd(em.nc, in_maps, core_ids=list(range(NCORES)))
    return [np.asarray(res.results[c]["a2_gT"]) for c in range(NCORES)]




def emit_stageA3(em, x_out=None):
    gT_in = em.dram["A3_gT"].ap()
    wout_in = em.dram["A3_wout"].ap()
    x_in = em.dram["A3_x"].ap()
    x_out = em.dram["a3_x1"].ap() if x_out is None else x_out
    wout_loc = em.dram_t("a3_wloc", [512, 2048], BF16)
    wout_bf = em.dram_t("a3_wbf", [4096, 2048], BF16)

    SWW, CC, PE, DVE, ACTS = "a3_sww", "a3_cc", "a3_pe", "E_dve", "E_act"
    GLD, XRS, HST = "a3_gld", "a3_xrs", "a3_hst"

    for _eng in ("tensor", "vector", "scalar"):
        em.wait(_eng, "a3_ld")
    em.dma("gpsimd", wout_loc.ap(), wout_in, SWW)
    em.wait("gpsimd", SWW)
    em.collective("AllGather", OP.bypass, [list(range(8))],
                  [wout_loc.ap().opt()], [wout_bf.ap().opt()], CC)
    cc_w = em.cnt[CC]

    a3stack = ExitStack()
    gt_sb = em.sbuf("a3_gt", [128, 32 * 128], BF16, a3stack)
    wo_sb = em.sbuf("a3_wo", [128, 2 * 32 * 512], BF16, a3stack)
    yacc = em.sbuf("a3_yacc", [128, 2048], F32, a3stack)
    junk_bf = em.sbuf("a3_junk", [128, 2048], BF16, a3stack)
    stat = em.sbuf("a3_stat", [128, 8], F32, a3stack)
    xres = em.sbuf("a3_xres", [128, 2048], F32, a3stack)
    ps_y = [em.psum("a3_ps_y0", [128, 512], F32, a3stack),
            em.psum("a3_ps_y1", [128, 512], F32, a3stack)]

    em.wait("sync", CC, cc_w)
    wo_pe = {0: 0, 1: 0}
    last_hst = 0
    last_stt = 0
    gld = 0
    for tt in range(8):
        em.wait("sync", PE)
        gld = em.dma("sync", gt_sb[:, :].rearrange("p (t o) -> p t o", o=128),
                     _tiled(gT_in[:, tt * 128:(tt + 1) * 128]), GLD)
        em.wait("sync", DVE, last_stt)
        xl = em.dma("sync", xres[:, :], x_in[tt * 128:(tt + 1) * 128, :], XRS)
        em.wait("tensor", GLD, gld)
        em.wait("vector", HST, last_hst)
        for ob in range(4):
            buf = ob % 2
            em.wait("sync", PE, wo_pe[buf])
            wl = em.dma(
                "sync", wo_sb[:, buf * 16384:(buf + 1) * 16384].rearrange(
                    "p (t o) -> p t o", o=512),
                _tiled(wout_bf.ap()[:, ob * 512:(ob + 1) * 512]), f"a3_wl{buf}")
            em.wait("tensor", f"a3_wl{buf}", wl)
            par = ob % 2
            em.wait("tensor", DVE)
            pe_c = 0
            for d in range(32):
                pe_c = em.op("tensor", lambda e, d=d, buf=buf, par=par: e.matmul(
                    ps_y[par][:, :], gt_sb[:, d * 128:(d + 1) * 128],
                    wo_sb[:, buf * 16384 + d * 512:buf * 16384 + (d + 1) * 512],
                    start=(d == 0), stop=(d == 31)), inc=PE if d == 31 else None)
            wo_pe[buf] = pe_c
            em.wait("vector", PE, pe_c)
            em.op("vector", lambda e, ob=ob, par=par: e.tensor_copy(
                yacc[:, ob * 512:(ob + 1) * 512], ps_y[par][:, :]), inc=DVE)
        em.wait("vector", XRS, xl)
        last_stt = em.op("vector", lambda e: e.tensor_tensor(
            yacc[:, :], yacc[:, :], xres[:, :], OP.add), inc=DVE)
        em.wait("scalar", DVE, last_stt)
        vln = emit_ln(em, yacc, junk_bf, stat)
        em.wait("gpsimd", DVE, vln)
        last_hst = em.dma("gpsimd", x_out[tt * 128:(tt + 1) * 128, :], yacc[:, :], HST)
    em.wait("gpsimd", HST)
    a3stack.close()


def build_stageA3():
    nc = bass.Bass("TRN2", target_bir_lowering=False)
    em = Em(nc)
    em.dram_t("A3_gT", [4096, TLOC], BF16, kind="ExternalInput")
    em.dram_t("A3_wout", [512, 2048], F32, kind="ExternalInput")
    em.dram_t("A3_x", [TLOC, 2048], F32, kind="ExternalInput")
    ident_in = em.dram_t("ident_in", [128, 128], BF16, kind="ExternalInput").ap()
    constv_in = em.dram_t("constv_in", [128, 32], F32, kind="ExternalInput").ap()
    em.dram_t("a3_x1", [TLOC, 2048], F32, kind="ExternalOutput")
    with em.ctx:
        em.sbuf("ident_bf", [128, 128], BF16)
        em.sbuf("constv", [128, 32], F32)
        em.dma("sync", em.sb["ident_bf"][:, :], ident_in, "a3_ld")
        em.dma("sync", em.sb["constv"][:, :], constv_in, "a3_ld")
        emit_stageA3(em)
        em.emit()
    return em


def stageA3_inputs(em, a2_gT, inputs, core_id):
    c = core_id
    gT_full = np.concatenate(a2_gT, axis=0)
    x = np.asarray(inputs["L0_attn_x"], np.float32).reshape(TOK, 2048)
    return {
        "A3_gT": np.ascontiguousarray(gT_full[:, c * TLOC:(c + 1) * TLOC]),
        "A3_wout": np.ascontiguousarray(
            np.asarray(inputs["L0_attn_W_out"], np.float32)[512 * c:512 * (c + 1)]),
        "A3_x": np.ascontiguousarray(x[c * TLOC:(c + 1) * TLOC]),
        "ident_in": np.eye(128, dtype=ml_dtypes.bfloat16),
        "constv_in": consts_array(em),
    }


def run_stageA3(a2_gT, inputs, sim=False):
    em = build_stageA3()
    in_maps = [stageA3_inputs(em, a2_gT, inputs, c) for c in range(NCORES)]
    if sim:
        import concourse.bass_interp as bass_interp
        msim = bass_interp.MultiCoreSim(em.nc, NCORES)
        for c in range(NCORES):
            for k, v in in_maps[c].items():
                msim.cores[c].tensor(k)[:] = v
        msim.simulate()
        return np.concatenate(
            [np.asarray(msim.cores[c].mem_tensor("a3_x1")) for c in range(NCORES)],
            axis=0)
    res = run_bass_kernel_spmd(em.nc, in_maps, core_ids=list(range(NCORES)))
    return np.concatenate([np.asarray(res.results[c]["a3_x1"]) for c in range(NCORES)],
                          axis=0)





def build_tail():
    nc = bass.Bass("TRN2", target_bir_lowering=False)
    em = Em(nc)
    em.dram_t("A3_gT", [4096, TLOC], BF16, kind="ExternalInput")
    em.dram_t("A3_wout", [512, 2048], F32, kind="ExternalInput")
    em.dram_t("A3_x", [TLOC, 2048], F32, kind="ExternalInput")
    em.dram_t("B_gup", [2 * FF_I, 32], I32, kind="ExternalInput")
    em.dram_t("B_gus", [2 * FF_I, 2], F16, kind="ExternalInput")
    em.dram_t("B_dnp", [2048, 88], I32, kind="ExternalInput")
    em.dram_t("B_dns", [2048, 11], F32, kind="ExternalInput")
    em.dram_t("C_wq", [256, 2048], F32, kind="ExternalInput")
    em.dram_t("C_wk", [256, 2048], F32, kind="ExternalInput")
    em.dram_t("C_wv", [256, 2048], F32, kind="ExternalInput")
    em.dram_t("C_wo", [2048, 256], F32, kind="ExternalInput")
    em.dram_t("C_cosq", [128, 2048], F32, kind="ExternalInput")
    em.dram_t("C_sinq", [128, 2048], F32, kind="ExternalInput")
    em.dram_t("C_cosk", [128, 2048], F32, kind="ExternalInput")
    em.dram_t("C_sink", [128, 2048], F32, kind="ExternalInput")
    em.dram_t("C_permT", [128, 128], BF16, kind="ExternalInput")
    em.dram_t("C_dmask", [128, 128], F32, kind="ExternalInput")
    em.dram_t("D_w1T", [2, 2048, MOE_I], F32, kind="ExternalInput")
    em.dram_t("D_w3T", [2, 2048, MOE_I], F32, kind="ExternalInput")
    em.dram_t("D_w2T", [2, MOE_I, 2048], F32, kind="ExternalInput")
    em.dram_t("D_ws1T", [2048, 352], F32, kind="ExternalInput")
    em.dram_t("D_ws3T", [2048, 352], F32, kind="ExternalInput")
    em.dram_t("D_ws2T", [352, 2048], F32, kind="ExternalInput")
    em.dram_t("D_wgT", [2048, 16], F32, kind="ExternalInput")
    em.dram_t("D_selm", [128, 32], F32, kind="ExternalInput")
    ident_in = em.dram_t("ident_in", [128, 128], BF16, kind="ExternalInput").ap()
    constv_in = em.dram_t("constv_in", [128, 32], F32, kind="ExternalInput").ap()
    x1 = em.dram_t("t_x1", [TLOC, 2048], F32).ap()
    x2 = em.dram_t("t_x2", [TLOC, 2048], F32).ap()
    x3 = em.dram_t("t_x3", [TLOC, 2048], F32).ap()
    x_out = em.dram_t("dx_out", [TLOC, 2048], F32, kind="ExternalOutput").ap()
    with em.ctx:
        em.sbuf("ident_bf", [128, 128], BF16)
        em.sbuf("constv", [128, 32], F32)
        em.dma("sync", em.sb["ident_bf"][:, :], ident_in, "a3_ld")
        em.dma("sync", em.sb["constv"][:, :], constv_in, "a3_ld")
        emit_stageA3(em, x_out=x1)
        em.stage_barrier()
        emit_stageB(em, 0, x1, x2)
        em.stage_barrier()
        emit_stageC(em, x2, x3)
        em.stage_barrier()
        emit_stageD(em, x3, x_out)
        em.emit()
    return em


def run_tail(a2_gT, inputs, sim=False):
    em = build_tail()
    in_maps = []
    for c in range(NCORES):
        m = {}
        m.update(stageA3_inputs(em, a2_gT, inputs, c))
        m.pop("a3_x1", None)
        bi = stageB_inputs(em, np.zeros((TOK, 2048), np.float32), inputs, c)
        bi.pop("bx_in")
        m.update(bi)
        ci = stageC_inputs(em, np.zeros((TOK, 2048), np.float32), inputs, c)
        ci.pop("cx_in")
        m.update(ci)
        di = stageD_inputs(em, np.zeros((TOK, 2048), np.float32), inputs, c)
        di.pop("dx_in")
        m.update(di)
        m["constv_in"] = consts_array(em)
        in_maps.append(m)
    if sim:
        import concourse.bass_interp as bass_interp
        msim = bass_interp.MultiCoreSim(em.nc, NCORES)
        for c in range(NCORES):
            for k, v in in_maps[c].items():
                msim.cores[c].tensor(k)[:] = v
        msim.simulate()
        return np.concatenate(
            [np.asarray(msim.cores[c].mem_tensor("dx_out")) for c in range(NCORES)],
            axis=0)
    res = run_bass_kernel_spmd(em.nc, in_maps, core_ids=list(range(NCORES)))
    return np.concatenate([np.asarray(res.results[c]["dx_out"]) for c in range(NCORES)],
                          axis=0)


def kernel(**inputs):
    a1 = run_stageA1(inputs)
    gT = run_stageA2(a1, inputs)
    x4 = run_tail(gT, inputs)
    return x4.reshape(B, S, 2048).astype(np.float32)
